# version 01
# baseline (speedup 1.0000x reference)
import numpy as np
import ml_dtypes
from contextlib import ExitStack
import concourse.bass as bass
import concourse.mybir as mybir
from concourse.bass_utils import run_bass_kernel_spmd

F32 = mybir.dt.float32
BF16 = mybir.dt.bfloat16
F32R = mybir.dt.float32r
AF = mybir.ActivationFunctionType
ALU = mybir.AluOpType
AX = mybir.AxisListType

NCORES = 8
T = 2048
HALO = 128
TTOK = T + HALO
D = 1024
DIN = 2048
CONVD = 2176
DPROJ = 4256
H = 32
PDIM = 64
NS = 64
CH = 64
NCH = T // CH
NPAIR = NCH // 2
NT = TTOK // 128
DFF = 2048
NE = 8
EPS = 1e-5

_TAPS = set()
_STOP = None


class Sem:
    def __init__(self, ctx, nc, name):
        self.h = ctx.enter_context(nc.semaphore(name))
        self.n = 0
        self.name = name
        self.maxwait = 0

    def inc(self, inst, k=1):
        inst.then_inc(self.h, k)
        self.n += k
        return inst

    def dinc(self, inst):
        return self.inc(inst, 16)

    def wait(self, eng, n=None):
        v = self.n if n is None else n
        self.maxwait = max(self.maxwait, v)
        if v > 0:
            eng.wait_ge(self.h, v)

    def check(self):
        assert self.maxwait <= self.n, \
            f"sem {self.name}: wait {self.maxwait} exceeds final count {self.n}"


class KB:
    def __init__(self, taps):
        self.taps = set(taps)
        self.nc = bass.Bass("TRN2", target_bir_lowering=False, debug=False, detect_race_conditions=False)
        self.ctx = ExitStack()

    def sb(self, name, shape, dt):
        return self.ctx.enter_context(self.nc.sbuf_tensor(name, shape, dt))

    def psum(self, pctx, name, shape, dt=F32):
        return pctx.enter_context(self.nc.psum_tensor(name, shape, dt))

    def lsb(self, pctx, name, shape, dt):
        return pctx.enter_context(self.nc.sbuf_tensor(name, shape, dt))

    def snap(self):
        return {s: s.n for s in self.sems}

    def bar(self, eng, snap):
        for s in self.sems:
            s.wait(eng, snap[s])


def build_program(taps=()):
    kb = KB(taps)
    nc = kb.nc

    def dram_in(name, shape, dt=F32):
        return nc.dram_tensor(name, shape, dt, kind="ExternalInput").ap()

    def dram_tmp(name, shape, dt):
        kind = "ExternalOutput" if name in kb.taps else "Internal"
        return nc.dram_tensor(name, shape, dt, kind=kind).ap()

    u_pad = dram_in("u_pad", [TTOK, D])
    w_in = dram_in("w_in", [DPROJ, D])
    conv_w = dram_in("conv_w", [CONVD, 4])
    conv_b = dram_in("conv_b", [1, CONVD])
    init_nhp = dram_in("init_nhp", [NS, H * PDIM])
    dt_bias = dram_in("dt_bias", [H, 1])
    a_log = dram_in("a_log", [H, 1])
    d_head = dram_in("d_head", [1, H])
    norm_w = dram_in("norm_w", [1, DIN])
    w_out = dram_in("w_out", [D, DIN])
    router_w = dram_in("router_w", [NE, D])
    gate_w = dram_in("gate_w", [NE, D, DFF])
    gate_b = dram_in("gate_b", [1, NE * DFF])
    up_w = dram_in("up_w", [NE, D, DFF])
    up_b = dram_in("up_b", [1, NE * DFF])
    down_w = dram_in("down_w", [NE, DFF, D])
    down_b = dram_in("down_b", [1, NE * D])
    eye_f = dram_in("eye_f", [128, 128])
    eye_bfc = dram_in("eye_bf", [128, 128], BF16)
    ltm2_c = dram_in("ltm2", [128, CH])
    dd_c = dram_in("deltadelta", [64, H * CH])
    hones_c = dram_in("halfones", [2, 128])
    m9_c = dram_in("m9", [1, 9])
    sel8_c = dram_in("sel8", [NE, NE * 128])
    ones_c = dram_in("ones_row", [1, 128])

    out = nc.dram_tensor("out", [T, D], F32, kind="ExternalOutput").ap()

    d_xnfm = dram_tmp("d_xnfm", [D, TTOK], BF16)
    d_z = dram_tmp("d_z", [DFF, TTOK], BF16)
    d_xbc = dram_tmp("d_xbc", [CONVD, TTOK], BF16)
    d_xconv = dram_tmp("d_xconv", [T, CONVD], BF16)
    d_sz = dram_tmp("d_sz", [T, DIN], BF16)
    d_x = dram_tmp("d_x", [T, D], F32)
    d_woutT = dram_tmp("d_woutT", [DIN, D], BF16)
    d_cumahi = dram_tmp("d_cumahi", [H, T], BF16)
    d_cumalo = dram_tmp("d_cumalo", [H, T], F32R)
    d_dec = dram_tmp("d_dec", [NCH, H], F32)
    d_ccin = nc.dram_tensor("d_ccin", [65, 2048], F32, kind="Internal").ap()
    d_ccout = nc.dram_tensor("d_ccout", [520, 2048], F32, kind="Internal").ap()
    d_dt = dram_tmp("d_dt", [H, T], F32)
    d_cuma = dram_tmp("d_cuma", [H, T], F32)
    d_sk = dram_tmp("d_sk", [NS + 1, H * PDIM], F32)
    d_sin = dram_tmp("d_sin", [NS, H * PDIM], F32)
    d_yg = dram_tmp("d_yg", [T, DIN], BF16)
    d_yfm = dram_tmp("d_yfm", [T, DIN], BF16)
    d_ms = dram_tmp("d_ms", [T, 16], F32)

    with kb.ctx:
        sb = kb.sb
        eye32 = sb("eye32", [128, 128], F32)
        eyebf = sb("eyebf", [128, 128], BF16)
        ones_bf = sb("ones_bf", [1, 128], BF16)
        gb_cols = sb("gb_cols", [128, NE * 16], F32)
        ub_cols = sb("ub_cols", [128, NE * 16], F32)
        wrT = sb("wrT", [128, 8 * NE], BF16)
        epsc = sb("epsc", [128, 1], F32)
        ones64f = sb("ones64f", [1, 64], F32)
        nc.const_aps.register(F32, EPS, epsc[:])
        ctx_mix = ExitStack()
        msb = lambda n, s, d=F32: kb.lsb(ctx_mix, n, s, d)
        ltm2 = msb("ltm2s", [128, CH])
        ddel_bf = msb("ddel_bf", [64, H * CH], BF16)
        ddel_r = msb("ddel_r", [64, H * CH], F32R)
        hones_bf = msb("hones_bf", [2, 128], BF16)
        hones_r = msb("hones_r", [2, 128], F32R)
        rstmask = msb("rstmask", [H, T])
        m9 = msb("m9s", [1, 9])
        sel8_r = msb("sel8_r", [NE, NE * 128], F32R)
        convb_row = msb("convb_row", [1, CONVD], BF16)
        normw_cols = msb("normw_cols", [128, 16])
        dhead_rep = msb("dhead_rep", [64, H])
        Ddiag = msb("Ddiag", [128, H * CH], BF16)
        ddt_tm = msb("ddt_tm", [128, NPAIR * H], BF16)
        lhsc_hi = msb("lhsc_hi", [66, NPAIR * 128], BF16)
        lhsc_lo = msb("lhsc_lo", [66, NPAIR * 128], F32R)
        decrows = msb("decrows", [2, NPAIR * H], F32R)
        lhscv_hi = msb("lhscv_hi", [66, 128], BF16)
        lhscv_lo = msb("lhscv_lo", [66, 128], F32R)
        totrow = msb("totrow", [1, H])
        B_fm = msb("B_fm", [NS, TTOK], BF16)
        C_fm = msb("C_fm", [128, TTOK], BF16)
        S_pp = msb("S_pp", [128, H * PDIM], BF16)
        dtraw_fm = msb("dtraw_fm", [H, TTOK])

        s_ld = Sem(kb.ctx, nc, "s_ld")
        s_wl = Sem(kb.ctx, nc, "s_wl")
        s_mm = Sem(kb.ctx, nc, "s_mm")
        s_ev = Sem(kb.ctx, nc, "s_ev")
        s_ac = Sem(kb.ctx, nc, "s_ac")
        s_po = Sem(kb.ctx, nc, "s_po")
        s_st = Sem(kb.ctx, nc, "s_st")
        s_cc = Sem(kb.ctx, nc, "s_cc")
        s_x1 = Sem(kb.ctx, nc, "s_x1")
        s_x2 = Sem(kb.ctx, nc, "s_x2")
        s_vc = Sem(kb.ctx, nc, "s_vc")
        s_sc = Sem(kb.ctx, nc, "s_sc")
        s_pc = Sem(kb.ctx, nc, "s_pc")
        s_c1 = Sem(kb.ctx, nc, "s_c1")
        ST = [[Sem(kb.ctx, nc, "st%d%s" % (i, ab)) for ab in "ab"] for i in range(8)]
        kb.sems = [s_ld, s_wl, s_mm, s_ev, s_ac, s_po, s_st, s_cc, s_x1, s_x2,
                   s_vc, s_sc, s_pc, s_c1] + [s for pr in ST for s in pr]

        def stld(e, slot, idx, out, in_):
            ST[slot][idx % 2].dinc(e.dma_start(out=out, in_=in_))

        def stw(e, slot, idx, base):
            s = ST[slot][idx % 2]
            s.wait(e, base[s] + 16 * (idx // 2 + 1))

        def vE(e, inst):
            s_vc.inc(inst)
            s_vc.wait(e)
            return inst

        def sE(e, inst):
            s_sc.inc(inst)
            s_sc.wait(e)
            return inst

        def pE(e, inst):
            s_pc.inc(inst)
            s_pc.wait(e)
            return inst

        blk = kb.ctx.enter_context(nc.Block())

        def finalize_dummy():
            with nc.sbuf_tensor("fin_zero", [128, D], F32) as fz:
                sn = kb.snap()

                @blk.vector
                def _(e):
                    kb.bar(e, sn)
                    s_ev.inc(e.memset(fz[:], 0.0))

                @blk.scalar
                def _(e):
                    kb.bar(e, sn)
                    s_ev.wait(e)
                    for t in range(T // 128):
                        s_st.dinc(e.dma_start(out=out[t * 128:(t + 1) * 128, :], in_=fz[:]))
                    s_st.wait(e)

        with ExitStack() as pctx:
            lsb = lambda n, s, d: kb.lsb(pctx, n, s, d)
            ps = lambda n, s, d=F32: kb.psum(pctx, n, s, d)
            rw_bf = lsb("rw_bf", [NE, D], BF16)
            c32a = lsb("c32a", [64, H * CH], F32)
            c32b = lsb("c32b", [NE, NE * 128], F32)
            rowbuf = lsb("rowbuf", [1, DIN], F32)
            dh_row = lsb("dh_row", [1, H], F32)
            ps_cols = ps("ps_cols", [128, 17 * 16])
            ps_dh = ps("ps_dh", [64, H])
            ps_rt = ps("ps_rt", [128, 8 * NE], BF16)
            wo_nat = [lsb("wo_nat%d" % i, [128, DIN], BF16) for i in range(2)]
            wo_col = [lsb("wo_col%d" % i, [128, 16 * 128], BF16) for i in range(2)]
            ps_wo = [ps("ps_wo%d" % i, [128, 2048], BF16) for i in range(2)]

            rows = [(norm_w[:], DIN, normw_cols, 0)]
            for ei in range(NE):
                rows.append((gate_b[0:1, ei * DFF:(ei + 1) * DFF], DFF, gb_cols, ei * 16))
                rows.append((up_b[0:1, ei * DFF:(ei + 1) * DFF], DFF, ub_cols, ei * 16))

            b = kb.snap()

            @blk.sync
            def _(e):
                s_c1.dinc(e.dma_start(out=eye32[:], in_=eye_f[:]))
                s_c1.dinc(e.dma_start(out=eyebf[:], in_=eye_bfc[:]))
                s_c1.dinc(e.dma_start(out=ltm2[:], in_=ltm2_c[:]))
                s_c1.dinc(e.dma_start(out=m9[:], in_=m9_c[:]))
                s_c1.dinc(e.dma_start(out=sel8_r[:].bitcast(F32), in_=sel8_c[:]))
                s_c1.dinc(e.dma_start(out=dh_row[:], in_=d_head[:]))
                for i, (ap, w, _, _o) in enumerate(rows):
                    if i >= 1:
                        s_mm.wait(e, b[s_mm] + 16 * i)
                    s_ld.dinc(e.dma_start(out=rowbuf[0:1, 0:w], in_=ap))

            NC1 = 11 * 16

            @blk.gpsimd
            def _(e):
                s_c1.dinc(e.dma_start(out=ddel_bf[:], in_=dd_c[:]))
                s_c1.dinc(e.dma_start(out=hones_bf[:], in_=hones_c[:]))
                s_c1.dinc(e.dma_start(out=ones_bf[:], in_=ones_c[:]))
                s_c1.dinc(e.dma_start(out=convb_row[:], in_=conv_b[:]))
                s_c1.dinc(e.dma_start(out=rw_bf[:], in_=router_w[:]))

            @blk.tensor
            def _(e):
                s_c1.wait(e, b[s_c1] + NC1)
                for i, (ap, w, _, _o) in enumerate(rows):
                    s_ld.wait(e, b[s_ld] + 16 * (i + 1))
                    for j in range(w // 128):
                        s_mm.inc(e.transpose(ps_cols[:, i * 16 + j:i * 16 + j + 1],
                                             rowbuf[0:1, j * 128:(j + 1) * 128], eye32[0:1, 0:1]))
                s_ev.wait(e, b[s_ev] + 1)
                s_mm.inc(e.matmul(ps_dh[:], ones64f[:], dh_row[:], start=True, stop=True))
                s_c1.wait(e, b[s_c1] + NC1)
                for dc in range(8):
                    s_mm.inc(e.transpose(ps_rt[:, dc * NE:(dc + 1) * NE],
                                         rw_bf[:, dc * 128:(dc + 1) * 128], eyebf[0:NE, 0:NE]))

            @blk.vector
            def _(e):
                e.memset(epsc[:], EPS)
                s_ev.inc(e.memset(ones64f[:], 1.0))
                vE(e, e.memset(rstmask[:], 0.0))
                for cc in range(NCH):
                    e.memset(rstmask[:, cc * CH + 1:(cc + 1) * CH], 1.0)
                s_c1.wait(e, b[s_c1] + NC1)
                e.tensor_copy(c32a[:], ddel_bf[:])
                e.tensor_copy(ddel_r[:], c32a[:])
                vE(e, e.tensor_copy(c32b[0:2, 0:128], hones_bf[:]))
                e.tensor_copy(hones_r[:], c32b[0:2, 0:128])
                e.tensor_copy(c32b[:], sel8_r[:].bitcast(F32))
                s_ev.inc(e.tensor_copy(sel8_r[:], c32b[:]))
                s_mm.wait(e, b[s_mm] + 17 * 16 + 1 + 8)
                for i, (ap, w, ctile, off) in enumerate(rows):
                    e.tensor_copy(ctile[:, off:off + w // 128], ps_cols[:, i * 16:i * 16 + w // 128])
                vE(e, e.tensor_copy(dhead_rep[:], ps_dh[:]))
                e.tensor_copy(wrT[:], ps_rt[:])
                ey_v = eyebf[0:64, 0:CH].unsqueeze(1).broadcast_to([64, H, CH])
                dh_v = dhead_rep[:].unsqueeze(2).broadcast_to([64, H, CH])
                dg_v = Ddiag[0:64, :].rearrange("s (h l) -> s h l", h=H)
                e.tensor_tensor(dg_v, ey_v, dh_v, ALU.mult)
                s_ev.inc(e.tensor_copy(Ddiag[64:128, :], Ddiag[0:64, :]))

            if _STOP == "p0a":
                finalize_dummy()
                for s in kb.sems:
                    s.check()
                return nc
            b2 = kb.snap()

            @blk.gpsimd
            def _(e):
                for dc in range(8):
                    if dc >= 2:
                        s_mm.wait(e, b2[s_mm] + 16 * (dc - 1))
                    stld(e, 0, dc, wo_nat[dc % 2][:], w_out[dc * 128:(dc + 1) * 128, :])

            @blk.tensor
            def _(e):
                for dc in range(8):
                    stw(e, 0, dc, b2)
                    if dc >= 2:
                        s_x1.wait(e, b2[s_x1] + (dc - 1))
                    for fc in range(16):
                        s_mm.inc(e.transpose(ps_wo[dc % 2][:, fc * 128:(fc + 1) * 128],
                                             wo_nat[dc % 2][:, fc * 128:(fc + 1) * 128], eyebf[:]))

            @blk.vector
            def _(e):
                for dc in range(8):
                    s_mm.wait(e, b2[s_mm] + 16 * (dc + 1))
                    if dc >= 2:
                        stw(e, 1, dc - 2, b2)
                    for fc in range(16):
                        inst = e.tensor_scalar_mul(wo_col[dc % 2][:, fc * 128:(fc + 1) * 128],
                                                   ps_wo[dc % 2][:, fc * 128:(fc + 1) * 128],
                                                   normw_cols[:, fc:fc + 1])
                        if fc == 15:
                            s_x1.inc(inst)

            @blk.scalar
            def _(e):
                for dc in range(8):
                    s_x1.wait(e, b2[s_x1] + (dc + 1))
                    dst = d_woutT.rearrange("(fc p) d -> p fc d", p=128)[:, :, dc * 128:(dc + 1) * 128]
                    src = wo_col[dc % 2][:].rearrange("p (fc d) -> p fc d", fc=16)
                    stld(e, 1, dc, dst, src)

        if _STOP == "p0":
            ctx_mix.close()
            finalize_dummy()
            for s in kb.sems:
                s.check()
            return nc

        ctx12 = ExitStack()
        xn_sb = kb.lsb(ctx12, "xn_sb", [128, 8 * TTOK], BF16)
        with ExitStack() as pctx:
            lsb = lambda n, s, d: kb.lsb(pctx, n, s, d)
            utile = [lsb("u_tile%d" % i, [128, D], F32) for i in range(2)]
            ln_sc = lsb("ln_sc", [128, 8], F32)
            ln_scr = lsb("ln_scr", [128, D], F32)
            xn_tm = lsb("xn_tm", [128, D], BF16)
            ps_xt = kb.psum(pctx, "ps_xt", [128, D], BF16)
            b = kb.snap()
            EVT = 3

            @blk.sync
            def _(e):
                kb.bar(e, b)
                for t in range(NT):
                    if t >= 2:
                        s_ev.wait(e, b[s_ev] + (t - 1) * EVT)
                    stld(e, 0, t, utile[t % 2][:], u_pad[t * 128:(t + 1) * 128, :])

            @blk.vector
            def _(e):
                kb.bar(e, b)
                for t in range(NT):
                    stw(e, 0, t, b)
                    ub = utile[t % 2]
                    vE(e, e.reduce_sum(ln_sc[:, 0:1], ub[:], AX.X))
                    s_ac.wait(e, b[s_ac] + 2 * t + 1)
                    vE(e, e.tensor_scalar_mul(ln_sc[:, 1:2], ln_sc[:, 0:1], 1.0 / D))
                    vE(e, e.tensor_scalar_mul(ln_sc[:, 5:6], ln_sc[:, 4:5], 1.0 / D))
                    vE(e, e.tensor_tensor(ln_sc[:, 2:3], ln_sc[:, 1:2], ln_sc[:, 1:2], ALU.mult))
                    s_ev.inc(e.tensor_tensor(ln_sc[:, 3:4], ln_sc[:, 5:6], ln_sc[:, 2:3], ALU.subtract))
                    s_ev.wait(e)
                    s_ac.wait(e, b[s_ac] + 2 * t + 2)
                    s_ev.inc(e.tensor_scalar(xn_tm[:], ub[:], ln_sc[:, 1:2], ln_sc[:, 7:8],
                                             ALU.subtract, ALU.mult))
                    s_mm.wait(e, b[s_mm] + 8 * (t + 1))
                    v = xn_sb[:].rearrange("p (dc tt) -> p dc tt", dc=8)
                    pv = ps_xt[:].rearrange("p (dc tt) -> p dc tt", dc=8)
                    s_ev.inc(e.tensor_copy(v[:, :, t * 128:(t + 1) * 128], pv))

            @blk.scalar
            def _(e):
                kb.bar(e, b)
                for t in range(NT):
                    stw(e, 0, t, b)
                    ub = utile[t % 2]
                    s_ac.inc(e.activation(ln_scr[:], ub[:], AF.Square, accum_out=ln_sc[:, 4:5]))
                    s_ev.wait(e, b[s_ev] + t * EVT + 1)
                    sE(e, e.activation(ln_sc[:, 6:7], ln_sc[:, 3:4], AF.Ln, bias=EPS))
                    s_ac.inc(e.activation(ln_sc[:, 7:8], ln_sc[:, 6:7], AF.Exp, scale=-0.5))
                    if "d_xnfm" in kb.taps:
                        s_ev.wait(e, b[s_ev] + t * EVT + 3)
                        dst = d_xnfm.rearrange("(dc p) tt -> p dc tt", p=128)[:, :, t * 128:(t + 1) * 128]
                        src = xn_sb[:].rearrange("p (dc tt) -> p dc tt", dc=8)[:, :, t * 128:(t + 1) * 128]
                        s_st.dinc(e.dma_start(out=dst, in_=src))

            @blk.tensor
            def _(e):
                kb.bar(e, b)
                for t in range(NT):
                    s_ev.wait(e, b[s_ev] + t * EVT + 2)
                    if t >= 1:
                        s_ev.wait(e, b[s_ev] + t * EVT)
                    for dc in range(8):
                        s_mm.inc(e.transpose(ps_xt[:, dc * 128:(dc + 1) * 128],
                                             xn_tm[:, dc * 128:(dc + 1) * 128], eyebf[:]))

        if _STOP == "p1":
            ctx12.close()
            ctx_mix.close()
            finalize_dummy()
            for s in kb.sems:
                s.check()
            return nc

        NSPL = [(0, 512), (512, 512), (1024, 512), (1536, 512), (2048, 128)]
        NSP = len(NSPL)
        with ExitStack() as pctx:
            lsb = lambda n, s, d: kb.lsb(pctx, n, s, d)
            wnat = [lsb("wnat%d" % i, [128, D], BF16) for i in range(2)]
            wTt = [lsb("wTt%d" % i, [128, 8 * 128], BF16) for i in range(2)]
            evb = [lsb("evb%d" % i, [128, TTOK], BF16) for i in range(2)]
            ps_wt = [kb.psum(pctx, "ps_wt%d" % i, [128, 8 * 128], BF16) for i in range(2)]
            ps_ip = [kb.psum(pctx, "ps_ip%d" % i, [128, 512]) for i in range(2)]
            b = kb.snap()
            MMF = 8 + 8 * NSP

            @blk.gpsimd
            def _(e):
                kb.bar(e, b)
                for fc in range(34):
                    mrows = 32 if fc == 33 else 128
                    if fc >= 2:
                        s_mm.wait(e, b[s_mm] + (fc - 2) * MMF + 8)
                    stld(e, 0, fc, wnat[fc % 2][0:mrows, :],
                         w_in[fc * 128:fc * 128 + mrows, :])

            @blk.tensor
            def _(e):
                kb.bar(e, b)
                for fc in range(34):
                    mrows = 32 if fc == 33 else 128
                    stw(e, 0, fc, b)
                    if fc >= 2:
                        s_x1.wait(e, b[s_x1] + (fc - 1))
                    for dc in range(8):
                        s_mm.inc(e.transpose(ps_wt[fc % 2][:, dc * 128:dc * 128 + mrows],
                                             wnat[fc % 2][0:mrows, dc * 128:(dc + 1) * 128],
                                             eyebf[0:mrows, 0:mrows]))
                    s_x1.wait(e, b[s_x1] + (fc + 1))
                    for j, (t0, tw) in enumerate(NSPL):
                        g = fc * NSP + j
                        if g >= 2:
                            s_x2.wait(e, b[s_x2] + (g - 1))
                        for dc in range(8):
                            s_mm.inc(e.matmul(ps_ip[g % 2][0:mrows, 0:tw],
                                              wTt[fc % 2][:, dc * 128:dc * 128 + mrows],
                                              xn_sb[:, dc * TTOK + t0: dc * TTOK + t0 + tw],
                                              start=(dc == 0), stop=(dc == 7)))

            @blk.vector
            def _(e):
                kb.bar(e, b)
                for fc in range(34):
                    mrows = 32 if fc == 33 else 128
                    s_mm.wait(e, b[s_mm] + fc * MMF + 8)
                    dv = wTt[fc % 2][:].rearrange("p (dc m) -> p dc m", dc=8)[:, :, 0:mrows]
                    pv = ps_wt[fc % 2][:].rearrange("p (dc m) -> p dc m", dc=8)[:, :, 0:mrows]
                    s_x1.inc(e.tensor_copy(dv, pv))
                    if fc >= 2 and fc < 33:
                        stw(e, 1, fc - 2, b)
                    for j, (t0, tw) in enumerate(NSPL):
                        g = fc * NSP + j
                        s_mm.wait(e, b[s_mm] + fc * MMF + 8 + 8 * (j + 1))
                        if fc == 33:
                            s_x2.inc(e.tensor_copy(dtraw_fm[:, t0:t0 + tw], ps_ip[g % 2][0:32, 0:tw]))
                        else:
                            s_x2.inc(e.tensor_copy(evb[fc % 2][:, t0:t0 + tw], ps_ip[g % 2][:, 0:tw]))

            @blk.scalar
            def _(e):
                kb.bar(e, b)
                for fc in range(33):
                    s_x2.wait(e, b[s_x2] + (fc + 1) * NSP)
                    if fc < 16:
                        dst = d_z[fc * 128:(fc + 1) * 128, :]
                    else:
                        dst = d_xbc[(fc - 16) * 128:(fc - 15) * 128, :]
                    stld(e, 1, fc, dst, evb[fc % 2][:])

        if _STOP == "p2":
            ctx12.close()
            ctx_mix.close()
            finalize_dummy()
            for s in kb.sems:
                s.check()
            return nc

        ctx12.close()

        with ExitStack() as pctx:
            lsb = lambda n, s, d: kb.lsb(pctx, n, s, d)
            dtb_col = lsb("dtb_col", [H, 1], F32)
            alog_col = lsb("alog_col", [H, 1], F32)
            a_col = lsb("a_col", [H, 1], F32)
            sp0 = lsb("sp0", [H, T], F32)
            sp1 = lsb("sp1", [H, T], F32)
            sp2 = lsb("sp2", [H, T], F32)
            dt_fm = lsb("dt_fm", [H, T], F32)
            cuma = lsb("cuma", [H, T], F32)
            lnadj = lsb("lnadj", [H, T], F32)
            cahi_bf = lsb("cahi_bf", [H, T], BF16)
            calo = lsb("calo", [H, T], F32)
            calo_r = lsb("calo_r", [H, T], F32R)
            lahi_bf = lsb("lahi_bf", [H, T], BF16)
            lalo_r = lsb("lalo_r", [H, T], F32R)
            a2_col = lsb("a2_col", [H, 1], F32)
            adt = sp2
            lndt = sp1
            lalo = sp2
            ddarg = sp1
            ddexp = sp0
            ddt_fm = sp2
            decfm = lsb("decfm", [H, NCH], F32)
            dectm = lsb("dectm", [NCH, H], F32)
            tot_col = lsb("tot_col", [H, 1], F32)
            dr32 = lsb("dr32", [2, NPAIR * H], F32)
            ps_sm = kb.psum(pctx, "ps_sm", [128, NPAIR * H])
            ps_sm2 = kb.psum(pctx, "ps_sm2", [NCH, 2 * H])
            b = kb.snap()

            @blk.sync
            def _(e):
                kb.bar(e, b)
                s_ld.dinc(e.dma_start(out=dtb_col[:], in_=dt_bias[:]))
                s_ld.dinc(e.dma_start(out=alog_col[:], in_=a_log[:]))
                s_st.wait(e, b[s_st] + 16 * 3)
                src2 = d_dec.rearrange("(p eo) h -> eo p h", eo=2)
                s_ld.dinc(e.dma_start(out=dr32[:].rearrange("eo (p h) -> eo p h", p=NPAIR), in_=src2))

            @blk.scalar
            def _(e):
                kb.bar(e, b)
                s_ld.wait(e, b[s_ld] + 32)
                e.activation(sp0[:], dtraw_fm[:, HALO:], AF.Identity, bias=dtb_col[:])
                e.activation(sp1[:], sp0[:], AF.Abs)
                e.activation(sp2[:], sp1[:], AF.Exp, scale=-1.0)
                sE(e, e.activation(sp1[:], sp2[:], AF.Ln, bias=1.0))
                e.activation(sp2[:], sp0[:], AF.Relu)
                s_ac.inc(e.activation(a_col[:], alog_col[:], AF.Exp))
                s_ev.wait(e, b[s_ev] + 1)
                s_ac.inc(e.activation(lndt[:], dt_fm[:], AF.Ln))
                s_ev.wait(e, b[s_ev] + 2)
                e.activation(ddexp[:], ddarg[:], AF.Exp)
                lastv2 = cuma[:].rearrange("h (c s) -> h c s", c=NCH)[:, :, CH - 1:CH].rearrange("h c s -> h (c s)")
                s_ac.inc(e.activation(decfm[:], lastv2, AF.Exp))

            @blk.vector
            def _(e):
                kb.bar(e, b)
                s_ac.wait(e, b[s_ac] + 1)
                e.tensor_tensor(dt_fm[:], sp1[:], sp2[:], ALU.add)
                vE(e, e.tensor_scalar_mul(a2_col[:], a_col[:], -1.0))
                e.tensor_scalar_mul(adt[:], dt_fm[:], a2_col[:])
                s_ev.inc(e.tensor_tensor_scan(cuma[:], rstmask[:], adt[:], 0.0, ALU.mult, ALU.add))
                s_ac.wait(e, b[s_ac] + 2)
                e.tensor_tensor(lnadj[:], lndt[:], cuma[:], ALU.subtract)
                e.tensor_copy(cahi_bf[:], cuma[:])
                e.tensor_copy(sp0[:], cahi_bf[:])
                e.tensor_tensor(calo[:], cuma[:], sp0[:], ALU.subtract)
                e.tensor_copy(calo_r[:], calo[:])
                e.tensor_copy(lahi_bf[:], lnadj[:])
                e.tensor_copy(sp0[:], lahi_bf[:])
                e.tensor_tensor(lalo[:], lnadj[:], sp0[:], ALU.subtract)
                e.tensor_copy(lalo_r[:], lalo[:])
                e.memset(lhsc_hi[:], 0.0)
                e.memset(lhsc_lo[:].bitcast(F32), 0.0)
                lh = lhsc_hi[:].rearrange("k (p m) -> k p m", p=NPAIR)
                ll = lhsc_lo[:].rearrange("k (p m) -> k p m", p=NPAIR)
                la_hi = lahi_bf[:].rearrange("h (p eo s) -> h p eo s", p=NPAIR, eo=2)
                e.tensor_copy(lh[0:H, :, 0:CH], la_hi[:, :, 0, :])
                e.tensor_copy(lh[32:64, :, CH:128], la_hi[:, :, 1, :])
                la_lo = lalo_r[:].rearrange("h (p eo s) -> h p eo s", p=NPAIR, eo=2)
                e.tensor_copy(ll[0:H, :, 0:CH], la_lo[:, :, 0, :])
                e.tensor_copy(ll[32:64, :, CH:128], la_lo[:, :, 1, :])
                ho_v = hones_bf[:].unsqueeze(1).broadcast_to([2, NPAIR, 128])
                e.tensor_copy(lh[64:66, :, :], ho_v)
                ho_vr = hones_r[:].unsqueeze(1).broadcast_to([2, NPAIR, 128])
                e.tensor_copy(ll[64:66, :, :], ho_vr)
                e.memset(lhscv_hi[:], 0.0)
                e.memset(lhscv_lo[:].bitcast(F32), 0.0)
                e.tensor_copy(lhscv_hi[64:66, :], hones_bf[:])
                e.tensor_copy(lhscv_lo[64:66, :], hones_r[:])
                last_v = cuma[:].rearrange("h (c s) -> h c s", c=NCH)[:, :, CH - 1:CH]
                last_b = last_v.broadcast_to([H, NCH, CH])
                e.tensor_tensor(ddarg[:].rearrange("h (c s) -> h c s", c=NCH), last_b,
                                cuma[:].rearrange("h (c s) -> h c s", c=NCH), ALU.subtract)
                s_ev.inc(e.reduce_sum(tot_col[:], last_v.squeeze(2), AX.X))
                s_ac.wait(e, b[s_ac] + 3)
                s_ev.inc(e.tensor_tensor(ddt_fm[:], ddexp[:], dt_fm[:], ALU.mult))
                s_mm.wait(e, b[s_mm] + NPAIR + 2)
                e.tensor_copy(ddt_tm[:], ps_sm[:])
                e.tensor_copy(dectm[:], ps_sm2[0:NCH, 0:H])
                s_ev.inc(e.tensor_copy(totrow[:], ps_sm2[0:1, H:2 * H]))
                s_ld.wait(e, b[s_ld] + 48)
                s_ev.inc(e.tensor_copy(decrows[:], dr32[:]))

            @blk.tensor
            def _(e):
                kb.bar(e, b)
                s_ev.wait(e, b[s_ev] + 3)
                for p in range(NPAIR):
                    s_mm.inc(e.transpose(ps_sm[:, p * H:(p + 1) * H],
                                         ddt_fm[:, p * 128:(p + 1) * 128], eye32[0:H, 0:H]))
                s_mm.inc(e.transpose(ps_sm2[0:NCH, 0:H], decfm[:], eye32[0:H, 0:H]))
                s_mm.inc(e.transpose(ps_sm2[0:1, H:2 * H], tot_col[:], eye32[0:H, 0:H]))

            @blk.scalar
            def _(e):
                s_ev.wait(e, b[s_ev] + 4)
                s_st.dinc(e.dma_start(out=d_cumahi[:], in_=cahi_bf[:]))
                s_st.dinc(e.dma_start(out=d_cumalo[:], in_=calo_r[:]))
                s_st.dinc(e.dma_start(out=d_dec[:], in_=dectm[:]))
                if "d_dt" in kb.taps:
                    s_st.dinc(e.dma_start(out=d_dt[:], in_=dt_fm[:]))
                if "d_cuma" in kb.taps:
                    s_st.dinc(e.dma_start(out=d_cuma[:], in_=cuma[:]))

        if _STOP == "p2b":
            ctx_mix.close()
            finalize_dummy()
            for s in kb.sems:
                s.check()
            return nc

        with ExitStack() as pctx:
            lsb = lambda n, s, d: kb.lsb(pctx, n, s, d)
            cw_cols = lsb("cw_cols", [128, 17 * 4], F32)
            cb32 = lsb("cb32", [1, 128], F32)
            diag = lsb("diag", [128, 17 * 4 * 128], BF16)
            bcb_col = lsb("bcb_col", [128, 1], F32)
            strip = lsb("strip", [128, TTOK], BF16)
            acc0 = lsb("acc0", [128, TTOK], BF16)
            acc1 = lsb("acc1", [128, TTOK], BF16)
            bcs = lsb("bcs", [128, TTOK], BF16)
            xbct = [lsb("xbct%d" % i, [128, 17 * 131], BF16) for i in range(2)]
            zt = [lsb("zt%d" % i, [128, 16 * 128], BF16) for i in range(2)]
            xcv = [lsb("xcv%d" % i, [128, CONVD], BF16) for i in range(2)]
            szt = [lsb("szt%d" % i, [128, DIN], BF16) for i in range(2)]
            ps_cv = [kb.psum(pctx, "ps_cv%d" % i, [128, 512]) for i in range(2)]
            ps_z = kb.psum(pctx, "ps_z", [128, DIN], BF16)
            ps_bc = kb.psum(pctx, "ps_bc", [128, 512])
            b = kb.snap()
            NG = 5
            CUM = [20, 40, 60, 80, 85]
            MMT = 85 + 16
            ACT_T = 5 + 2
            PRE_MM = 1

            def evac_done(gg):
                t2, g2 = divmod(gg, NG)
                return b[s_ac] + 1 + t2 * ACT_T + g2 + 1

            @blk.sync
            def _(e):
                kb.bar(e, b)
                src = conv_w.rearrange("(fc p) k -> p fc k", p=128)
                s_ld.dinc(e.dma_start(out=cw_cols[:].rearrange("p (fc k) -> p fc k", fc=17), in_=src))
                s_ld.dinc(e.dma_start(out=strip[:], in_=d_xbc[DIN:CONVD, :]))
                for t in range(NPAIR):
                    if t >= 2:
                        s_mm.wait(e, b[s_mm] + PRE_MM + (t - 1) * MMT)
                    src = d_xbc.rearrange("(fc p) tt -> p fc tt", p=128)[:, :, t * 128 + 125:t * 128 + 256]
                    stld(e, 0, t, xbct[t % 2][:].rearrange("p (fc c) -> p fc c", fc=17), src)
                    srcz = d_z.rearrange("(zc p) tt -> p zc tt", p=128)[:, :, t * 128 + HALO:t * 128 + HALO + 128]
                    stld(e, 2, t, zt[t % 2][:].rearrange("p (zc c) -> p zc c", zc=16), srcz)

            @blk.tensor
            def _(e):
                kb.bar(e, b)
                s_ev.wait(e, b[s_ev] + 1)
                s_mm.inc(e.transpose(ps_bc[:, 0:1], cb32[0:1, :], eye32[0:1, 0:1]))
                for t in range(NPAIR):
                    stw(e, 0, t, b)
                    stw(e, 2, t, b)
                    s_ev.wait(e, b[s_ev] + 2)
                    xb = xbct[t % 2][:].rearrange("p (fc c) -> p fc c", fc=17)
                    for fcg in range(NG):
                        gg = t * NG + fcg
                        if gg >= 2:
                            s_ac.wait(e, evac_done(gg - 2))
                        fcs = range(fcg * 4, min(fcg * 4 + 4, 17))
                        for fc in fcs:
                            col = (fc % 4) * 128
                            for k in range(4):
                                s_mm.inc(e.matmul(ps_cv[fcg % 2][:, col:col + 128],
                                                  xb[:, fc, k:k + 128],
                                                  diag[:, (fc * 4 + k) * 128:(fc * 4 + k + 1) * 128],
                                                  start=(k == 0), stop=False))
                            s_mm.inc(e.matmul(ps_cv[fcg % 2][:, col:col + 128],
                                              ones_bf[:],
                                              convb_row[0:1, fc * 128:(fc + 1) * 128],
                                              start=False, stop=True))
                    ztv = zt[t % 2][:].rearrange("p (zc c) -> p zc c", zc=16)
                    if t >= 1:
                        s_ac.wait(e, b[s_ac] + 1 + (t - 1) * ACT_T + 7)
                    for zc in range(16):
                        s_mm.inc(e.transpose(ps_z[:, zc * 128:(zc + 1) * 128], ztv[:, zc, :], eyebf[:]))

            @blk.scalar
            def _(e):
                kb.bar(e, b)
                s_ev.wait(e, b[s_ev] + 3)
                s_ac.inc(e.activation(bcs[:], acc0[:], AF.Silu, bias=bcb_col[:]))
                for t in range(NPAIR):
                    if t >= 2:
                        stw(e, 1, t - 2, b)
                        stw(e, 3, t - 2, b)
                    for fcg in range(NG):
                        s_mm.wait(e, b[s_mm] + PRE_MM + t * MMT + CUM[fcg])
                        w = 512 if fcg < 4 else 128
                        s_ac.inc(e.activation(xcv[t % 2][:, fcg * 512:fcg * 512 + w],
                                              ps_cv[fcg % 2][:, 0:w], AF.Silu))
                    s_mm.wait(e, b[s_mm] + PRE_MM + t * MMT + 85 + 16)
                    s_ac.inc(e.activation(szt[t % 2][:, 0:1024], ps_z[:, 0:1024], AF.Silu))
                    s_ac.inc(e.activation(szt[t % 2][:, 1024:2048], ps_z[:, 1024:2048], AF.Silu))

            @blk.vector
            def _(e):
                kb.bar(e, b)
                s_ld.wait(e, b[s_ld] + 32)
                cwv = cw_cols[:].rearrange("p (fc k) -> p fc k", fc=17)
                s_ev.inc(e.tensor_copy(cb32[0:1, :], convb_row[0:1, DIN:CONVD]))
                for fc in range(17):
                    for k in range(4):
                        inst = e.tensor_scalar_mul(diag[:, (fc * 4 + k) * 128:(fc * 4 + k + 1) * 128],
                                                   eyebf[:], cwv[:, fc, k:k + 1])
                s_ev.inc(inst)
                s_mm.wait(e, b[s_mm] + 1)
                e.tensor_copy(bcb_col[:], ps_bc[:, 0:1])
                s_ld.wait(e, b[s_ld] + 32)
                W = TTOK - 3
                e.memset(acc0[:, 0:3], 0.0)
                e.memset(acc1[:, 0:3], 0.0)
                e.tensor_scalar_mul(acc1[:, 3:], strip[:, 0:W], cwv[:, 16, 0:1])
                e.scalar_tensor_tensor(acc0[:, 3:], strip[:, 1:W + 1], cwv[:, 16, 1:2],
                                       acc1[:, 3:], ALU.mult, ALU.add)
                e.scalar_tensor_tensor(acc1[:, 3:], strip[:, 2:W + 2], cwv[:, 16, 2:3],
                                       acc0[:, 3:], ALU.mult, ALU.add)
                s_ev.inc(e.scalar_tensor_tensor(acc0[:, 3:], strip[:, 3:W + 3], cwv[:, 16, 3:4],
                                                acc1[:, 3:], ALU.mult, ALU.add))
                s_ac.wait(e, b[s_ac] + 1)
                e.tensor_copy(B_fm[:], bcs[0:64, :])
                e.tensor_copy(C_fm[0:64, :], bcs[64:128, :])
                s_ev.inc(e.tensor_copy(C_fm[64:128, :], bcs[64:128, :]))

            @blk.gpsimd
            def _(e):
                kb.bar(e, b)
                for t in range(NPAIR):
                    s_ac.wait(e, b[s_ac] + 1 + t * ACT_T + 5)
                    stld(e, 1, t, d_xconv[t * 128:(t + 1) * 128, :], xcv[t % 2][:])
                    s_ac.wait(e, b[s_ac] + 1 + (t + 1) * ACT_T)
                    stld(e, 3, t, d_sz[t * 128:(t + 1) * 128, :], szt[t % 2][:])

        if _STOP == "p3":
            ctx_mix.close()
            finalize_dummy()
            for s in kb.sems:
                s.check()
            return nc

        def emit_states_pair(e_pe, p, xct, Bp, ps_cs, ps_dr, b, ev_of_pair, evs_per_pair):
            s_ev.wait(e_pe, ev_of_pair(p) + 1)
            s_mm.inc(e_pe.matmul(ps_dr[:, 0:H], hones_r[:], decrows[:, p * H:(p + 1) * H],
                                 start=True, stop=True))
            for half, base in ((0, 0), (1, 64)):
                for h in range(H):
                    s_mm.inc(e_pe.matmul(ps_cs[base:base + 64, h * 64:(h + 1) * 64],
                                         Bp[base:base + 64, h * 64:(h + 1) * 64],
                                         xct[base:base + 64, h * 64:(h + 1) * 64],
                                         start=True, stop=True))

        def emit_scan_pair(e_v, p, ps_cs, ps_dr, tmp_e, tmp_o):
            drv0 = ps_dr[0:64, 0:H].unsqueeze(2).broadcast_to([64, H, 64])
            drv1 = ps_dr[64:128, 0:H].unsqueeze(2).broadcast_to([64, H, 64])
            s3 = lambda ap: ap.rearrange("n (h pp) -> n h pp", h=H)
            s_ev.inc(e_v.tensor_tensor(s3(tmp_e[:]), s3(S_pp[0:64, :]), drv0, ALU.mult))
            s_ev.inc(e_v.tensor_tensor(S_pp[64:128, :], tmp_e[:], ps_cs[0:64, :], ALU.add))
            s_ev.inc(e_v.tensor_tensor(s3(tmp_o[:]), s3(S_pp[64:128, :]), drv1, ALU.mult))
            s_ev.inc(e_v.tensor_tensor(S_pp[0:64, :], tmp_o[:], ps_cs[64:128, :], ALU.add))

        with ExitStack() as pctx:
            lsb = lambda n, s, d: kb.lsb(pctx, n, s, d)
            xct = [lsb("xctA%d" % i, [128, CONVD], BF16) for i in range(2)]
            Bp = [lsb("BpA%d" % i, [128, H * NS], BF16) for i in range(2)]
            tmp_e = lsb("tmpAe", [64, H * PDIM], BF16)
            tmp_o = lsb("tmpAo", [64, H * PDIM], BF16)
            sk32 = lsb("sk32", [NS, H * PDIM], F32)
            ps_cs = kb.psum(pctx, "ps_csA", [128, H * PDIM])
            ps_dr = kb.psum(pctx, "ps_drA", [128, 512])
            b = kb.snap()
            EVP = 5
            MMP = 65

            def ev_of_pair(p):
                return b[s_ev] + p * EVP

            @blk.sync
            def _(e):
                kb.bar(e, b)
                for p in range(NPAIR):
                    if p >= 2:
                        s_ev.wait(e, ev_of_pair(p - 1))
                    stld(e, 0, p, xct[p % 2][:], d_xconv[p * 128:(p + 1) * 128, :])

            @blk.vector
            def _(e):
                kb.bar(e, b)
                e.memset(S_pp[:], 0.0)
                for p in range(NPAIR):
                    stw(e, 0, p, b)
                    bview = xct[p % 2][:, DIN:DIN + NS].unsqueeze(1).broadcast_to([128, H, NS])
                    dview = ddt_tm[:, p * H:(p + 1) * H].unsqueeze(2).broadcast_to([128, H, NS])
                    s_ev.inc(e.tensor_tensor(Bp[p % 2][:].rearrange("t (h n) -> t h n", h=H),
                                             bview, dview, ALU.mult))
                    s_mm.wait(e, b[s_mm] + (p + 1) * MMP)
                    emit_scan_pair(e, p, ps_cs, ps_dr, tmp_e, tmp_o)
                s_ev.inc(e.tensor_copy(sk32[:], S_pp[0:64, :]))

            @blk.tensor
            def _(e):
                kb.bar(e, b)
                for p in range(NPAIR):
                    if p >= 1:
                        s_ev.wait(e, ev_of_pair(p))
                    stw(e, 0, p, b)
                    emit_states_pair(e, p, xct[p % 2][:], Bp[p % 2][:], ps_cs, ps_dr,
                                     b, ev_of_pair, EVP)

            @blk.gpsimd
            def _(e):
                kb.bar(e, b)
                s_ev.wait(e, b[s_ev] + NPAIR * EVP + 1)
                s_st.dinc(e.dma_start(out=d_ccin[0:NS, :], in_=sk32[:]))
                s_st.dinc(e.dma_start(out=d_ccin[NS:NS + 1, 0:H], in_=totrow[:]))
                if "d_sk" in kb.taps:
                    s_st.dinc(e.dma_start(out=d_sk[0:NS, :], in_=sk32[:]))
                    s_st.dinc(e.dma_start(out=d_sk[NS:NS + 1, 0:H], in_=totrow[:]))

        if _STOP == "p4":
            ctx_mix.close()
            finalize_dummy()
            for s in kb.sems:
                s.check()
            return nc

        with ExitStack() as pctx:
            lsb = lambda n, s, d: kb.lsb(pctx, n, s, d)
            ttl = lsb("ttl", [1, NCORES * H], F32)
            mt = lsb("mt", [1, NCORES * H], F32)
            srow = lsb("srow", [1, 9 * H], F32)
            wrow = lsb("wrow", [1, 9 * H], F32)
            wm = lsb("wm", [1, 9 * H], F32)
            sgb = [lsb("sgb%d" % i, [NS, H * PDIM], F32) for i in range(2)]
            tb = lsb("tb", [NS, H * PDIM], F32)
            accb = [lsb("accb%d" % i, [NS, H * PDIM], F32) for i in range(2)]
            ps_w = kb.psum(pctx, "ps_w", [NS, 512])
            b = kb.snap()

            @blk.gpsimd
            def _(e):
                kb.bar(e, b)
                e.collective_compute(
                    "AllGather", ALU.bypass,
                    replica_groups=[list(range(NCORES))],
                    ins=[d_ccin.opt()],
                    outs=[d_ccout.opt()],
                ).then_inc(s_cc.h)
                s_cc.n += 1
                s_cc.wait(e)
                srct = d_ccout.rearrange("(j r) w -> j r w", r=65)[:, 64, 0:H].unsqueeze(0)
                s_wl.dinc(e.dma_start(out=ttl[:].rearrange("one (j h) -> one j h", j=NCORES),
                                      in_=srct))

            @blk.sync
            def _(e):
                kb.bar(e, b)
                s_cc.wait(e)
                for j in range(9):
                    if j >= 2:
                        s_ev.wait(e, b[s_ev] + 3 + (j - 1) * 2)
                    if j == 0:
                        stld(e, 0, j, sgb[0][:], init_nhp[:])
                    else:
                        stld(e, 0, j, sgb[j % 2][:],
                             d_ccout[65 * (j - 1):65 * (j - 1) + NS, :])

            @blk.vector
            def _(e):
                kb.bar(e, b)
                s_wl.wait(e, b[s_wl] + 16)
                m9v = m9[0:1, 1:9].unsqueeze(2).broadcast_to([1, 8, H])
                vE(e, e.tensor_tensor(mt[:].rearrange("one (j h) -> one j h", j=8),
                                      ttl[:].rearrange("one (j h) -> one j h", j=8),
                                      m9v, ALU.mult))
                vE(e, e.memset(srow[0:1, 8 * H:9 * H], 0.0))
                for j in range(6, -2, -1):
                    sl = (j + 1) * H
                    vE(e, e.tensor_tensor(srow[0:1, sl:sl + H], srow[0:1, sl + H:sl + 2 * H],
                                          mt[0:1, (j + 1) * H:(j + 2) * H] if j < 7 else mt[0:1, 7 * H:8 * H],
                                          ALU.add))
                s_ev.inc(e.nop())

            @blk.scalar
            def _(e):
                kb.bar(e, b)
                s_ev.wait(e, b[s_ev] + 1)
                s_ac.inc(e.activation(wrow[:], srow[:], AF.Exp))

            @blk.vector
            def _(e):
                s_ac.wait(e, b[s_ac] + 1)
                m9w = m9[0:1, 0:9].unsqueeze(2).broadcast_to([1, 9, H])
                s_ev.inc(e.tensor_tensor(wm[:].rearrange("one (j h) -> one j h", j=9),
                                         wrow[:].rearrange("one (j h) -> one j h", j=9),
                                         m9w, ALU.mult))
                for j in range(9):
                    stw(e, 0, j, b)
                    s_mm.wait(e, b[s_mm] + (j + 1))
                    wv = ps_w[:, 0:H].unsqueeze(2).broadcast_to([NS, H, 64])
                    s_ev.inc(e.tensor_tensor(tb[:].rearrange("n (h pp) -> n h pp", h=H),
                                             sgb[j % 2][:].rearrange("n (h pp) -> n h pp", h=H),
                                             wv, ALU.mult))
                    if j == 0:
                        s_ev.inc(e.tensor_copy(accb[1][:], tb[:]))
                    else:
                        s_ev.inc(e.tensor_tensor(accb[(j + 1) % 2][:], accb[j % 2][:], tb[:], ALU.add))
                s_ev.inc(e.tensor_copy(S_pp[0:64, :], accb[1][:]))

            @blk.tensor
            def _(e):
                kb.bar(e, b)
                s_ev.wait(e, b[s_ev] + 2)
                for j in range(9):
                    if j >= 1:
                        s_ev.wait(e, b[s_ev] + 3 + (j - 1) * 2 + 1)
                    s_mm.inc(e.matmul(ps_w[:, 0:H], ones64f[:], wm[0:1, j * H:(j + 1) * H],
                                      start=True, stop=True))

            @blk.gpsimd
            def _(e):
                if "d_sin" in kb.taps:
                    s_ev.wait(e, b[s_ev] + 3 + 18)
                    s_st.dinc(e.dma_start(out=d_sin[:], in_=accb[1][:]))

        if _STOP == "p5":
            ctx_mix.close()
            finalize_dummy()
            for s in kb.sems:
                s.check()
            return nc

        with ExitStack() as pctx:
            lsb = lambda n, s, d: kb.lsb(pctx, n, s, d)
            xct = [lsb("xctB%d" % i, [128, CONVD], BF16) for i in range(2)]
            sztl = [lsb("sztB%d" % i, [128, DIN], BF16) for i in range(2)]
            utl = [lsb("utB%d" % i, [128, D], F32) for i in range(2)]
            comp_hi = [lsb("comph%d" % i, [66, H * CH], BF16) for i in range(2)]
            comp_lo = [lsb("compl%d" % i, [66, H * CH], F32R) for i in range(2)]
            wog = [lsb("wog%d" % i, [128, 4 * D], BF16) for i in range(2)]
            Bp = [lsb("BpB%d" % i, [128, H * NS], BF16) for i in range(2)]
            gd = lsb("gd", [128, 96], F32)
            expsR = lsb("expsR", [128, H * CH], BF16)
            expsV = lsb("expsV", [128, H * CH], BF16)
            rt0 = lsb("rt0", [128, H * CH], BF16)
            rtF = lsb("rtF", [128, H * CH], BF16)
            vtF = lsb("vtF", [128, H * CH], BF16)
            tmp_e = lsb("tmpBe", [64, H * PDIM], BF16)
            tmp_o = lsb("tmpBo", [64, H * PDIM], BF16)
            yg = lsb("yg", [128, DIN], BF16)
            sqscr = lsb("sqscr", [128, 1024], BF16)
            msc = lsb("msc", [128, 16], F32)
            yfm = lsb("yfm", [128, 16 * 128], BF16)
            xout = [lsb("xoutB%d" % i, [128, D], F32) for i in range(2)]
            ps_big = kb.psum(pctx, "ps_bigB", [128, 2048])
            ps_y0 = kb.psum(pctx, "ps_y0B", [128, 1024])
            ps_y1 = kb.psum(pctx, "ps_y1B", [128, 1024])
            ps_gt = ps_big[:, 0:64]
            ps_dr = ps_big[:, 64:96]
            ps_o0 = ps_big[:, 0:512]
            ps_o1 = ps_big[:, 512:1024]
            ps_yt0 = ps_big[:, 1024:1536].bitcast(BF16)
            ps_yt1 = ps_big[:, 1536:2048].bitcast(BF16)
            clampR = lsb("clampR", [128, H * CH], F32)
            b = kb.snap()
            MMP = 259
            EVP = 18
            ACP = 5
            WLP = 18 * 16

            def pb(p):
                return b[s_mm] + p * MMP

            def evb(p):
                return b[s_ev] + 1 + p * EVP

            def acb(p):
                return b[s_ac] + p * ACP

            @blk.sync
            def _(e):
                kb.bar(e, b)
                for p in range(NPAIR):
                    if p >= 2:
                        s_ev.wait(e, evb(p - 1))
                    stld(e, 0, p, xct[p % 2][:], d_xconv[p * 128:(p + 1) * 128, :])
                    stld(e, 2, p, sztl[p % 2][:], d_sz[p * 128:(p + 1) * 128, :])
                    stld(e, 4, p, utl[p % 2][:], u_pad[p * 128 + HALO:p * 128 + HALO + 128, :])

            @blk.gpsimd
            def _(e):
                kb.bar(e, b)
                for p in range(NPAIR):
                    if p >= 2:
                        s_mm.wait(e, pb(p - 2) + 19)
                    srch = d_cumahi.rearrange("h (c s) -> c h s", c=NCH)[2 * p:2 * p + 2, :, :]
                    stld(e, 5, p, comp_hi[p % 2][64:66, :].rearrange("k (h s) -> k h s", h=H), srch)
                    srcl = d_cumalo.rearrange("h (c s) -> c h s", c=NCH)[2 * p:2 * p + 2, :, :]
                    stld(e, 6, p, comp_lo[p % 2][64:66, :].rearrange("k (h s) -> k h s", h=H), srcl)
                    for g in range(4):
                        gg = p * 4 + g
                        if gg >= 2:
                            p2, g2 = divmod(gg - 2, 4)
                            s_mm.wait(e, pb(p2) + 237 + 8 * g2)
                        srcw = d_woutT.rearrange("(fc p2) d -> p2 fc d", p2=128)[:, 4 * g:4 * g + 4, :]
                        stld(e, 7, gg, wog[gg % 2][:].rearrange("p2 (fc d) -> p2 fc d", fc=4), srcw)

            @blk.vector
            def _(e):
                kb.bar(e, b)
                e.tensor_copy(comp_hi[0][0:64, :], ddel_bf[:])
                e.tensor_copy(comp_lo[0][0:64, :], ddel_r[:])
                e.tensor_copy(comp_hi[1][0:64, :], ddel_bf[:])
                s_ev.inc(e.tensor_copy(comp_lo[1][0:64, :], ddel_r[:]))
                for p in range(NPAIR):
                    s_mm.wait(e, pb(p) + 3)
                    e.scalar_tensor_tensor(gd[:, 0:64], ps_big[:, 0:64], 1.0, ltm2[:],
                                           ALU.mult, ALU.mult)
                    s_ev.inc(e.tensor_copy(gd[:, 64:96], ps_big[:, 64:96]))
                    stw(e, 0, p, b)
                    bview = xct[p % 2][:, DIN:DIN + NS].unsqueeze(1).broadcast_to([128, H, NS])
                    dview = ddt_tm[:, p * H:(p + 1) * H].unsqueeze(2).broadcast_to([128, H, NS])
                    s_ev.inc(e.tensor_tensor(Bp[p % 2][:].rearrange("t (h n) -> t h n", h=H),
                                             bview, dview, ALU.mult))
                    s_mm.wait(e, pb(p) + 11)
                    s_ev.inc(e.tensor_scalar(clampR[:], ps_big[:], 30.0, None,
                                             ALU.min))
                    s_ac.wait(e, acb(p) + 1)
                    gtv = gd[:, 0:64].unsqueeze(1).broadcast_to([128, H, CH])
                    s_ev.inc(e.tensor_tensor(rt0[:].rearrange("s (h l) -> s h l", h=H),
                                             expsR[:].rearrange("s (h l) -> s h l", h=H),
                                             gtv, ALU.mult))
                    s_ev.inc(e.tensor_tensor(rtF[:].rearrange("s (h l) -> s h l", h=H),
                                             rt0[:].rearrange("s (h l) -> s h l", h=H),
                                             Ddiag[:].rearrange("s (h l) -> s h l", h=H),
                                             ALU.add))
                    s_ac.wait(e, acb(p) + 2)
                    cve = C_fm[0:64, HALO + 128 * p:HALO + 128 * p + CH].unsqueeze(1).broadcast_to([NS, H, CH])
                    cvo = C_fm[64:128, HALO + 128 * p + CH:HALO + 128 * p + 2 * CH].unsqueeze(1).broadcast_to([NS, H, CH])
                    s_ev.inc(e.tensor_tensor(vtF[0:64, :].rearrange("s (h l) -> s h l", h=H),
                                             expsV[0:64, :].rearrange("s (h l) -> s h l", h=H),
                                             cve, ALU.mult))
                    s_ev.inc(e.tensor_tensor(vtF[64:128, :].rearrange("s (h l) -> s h l", h=H),
                                             expsV[64:128, :].rearrange("s (h l) -> s h l", h=H),
                                             cvo, ALU.mult))
                    s_mm.wait(e, pb(p) + 83)
                    drv0 = gd[0:64, 64:96].unsqueeze(2).broadcast_to([64, H, 64])
                    drv1 = gd[64:128, 64:96].unsqueeze(2).broadcast_to([64, H, 64])
                    s3 = lambda ap: ap.rearrange("n (h pp) -> n h pp", h=H)
                    s_ev.inc(e.tensor_tensor(s3(tmp_e[:]), s3(S_pp[0:64, :]), drv0, ALU.mult))
                    s_ev.inc(e.tensor_tensor(S_pp[64:128, :], tmp_e[:], ps_big[0:64, :], ALU.add))
                    s_ev.inc(e.tensor_tensor(s3(tmp_o[:]), s3(S_pp[64:128, :]), drv1, ALU.mult))
                    s_mm.wait(e, pb(p) + 147)
                    s_ev.inc(e.tensor_tensor(S_pp[0:64, :], tmp_o[:], ps_big[64:128, :], ALU.add))
                    s_mm.wait(e, pb(p) + 179)
                    stw(e, 2, p, b)
                    s_ev.inc(e.tensor_tensor(yg[:, 0:1024], ps_y0[:], sztl[p % 2][:, 0:1024], ALU.mult))
                    s_mm.wait(e, pb(p) + 211)
                    s_ev.inc(e.tensor_tensor(yg[:, 1024:2048], ps_y1[:], sztl[p % 2][:, 1024:2048], ALU.mult))
                    s_ac.wait(e, acb(p) + 4)
                    vE(e, e.tensor_tensor(msc[:, 2:3], msc[:, 0:1], msc[:, 1:2], ALU.add))
                    s_ev.inc(e.nop())
                    s_mm.wait(e, pb(p) + 219)
                    s_ev.inc(e.tensor_copy(yfm[:, 0:1024], ps_yt0[:]))
                    s_mm.wait(e, pb(p) + 227)
                    s_ev.inc(e.tensor_copy(yfm[:, 1024:2048], ps_yt1[:]))
                    s_mm.wait(e, pb(p) + 259)
                    s_ac.wait(e, acb(p) + 5)
                    stw(e, 4, p, b)
                    if p >= 2:
                        stw(e, 1, p - 2, b)
                    s_ev.inc(e.scalar_tensor_tensor(xout[p % 2][:, 0:512], ps_o0, msc[:, 4:5],
                                                    utl[p % 2][:, 0:512], ALU.mult, ALU.add))
                    s_ev.inc(e.scalar_tensor_tensor(xout[p % 2][:, 512:1024], ps_o1, msc[:, 4:5],
                                                    utl[p % 2][:, 512:1024], ALU.mult, ALU.add))

            @blk.tensor
            def _(e):
                kb.bar(e, b)
                s_ev.wait(e, b[s_ev] + 1)
                for p in range(NPAIR):
                    s_ev.wait(e, evb(p))
                    stw(e, 0, p, b)
                    stw(e, 5, p, b)
                    stw(e, 6, p, b)
                    ec = HALO + 128 * p
                    s_mm.inc(e.matmul(ps_gt[0:64, :], B_fm[:, ec:ec + CH], C_fm[0:64, ec:ec + CH],
                                      start=True, stop=True))
                    s_mm.inc(e.matmul(ps_gt[64:128, :], B_fm[:, ec + CH:ec + 2 * CH],
                                      C_fm[0:64, ec + CH:ec + 2 * CH], start=True, stop=True))
                    s_mm.inc(e.matmul(ps_dr[:, 0:H], hones_r[:], decrows[:, p * H:(p + 1) * H],
                                      start=True, stop=True))
                    s_ev.wait(e, evb(p) + 1)
                    for j in range(4):
                        s_mm.inc(e.matmul(ps_big[:, j * 512:(j + 1) * 512],
                                          lhsc_hi[:, p * 128:(p + 1) * 128],
                                          comp_hi[p % 2][:, j * 512:(j + 1) * 512],
                                          start=True, stop=False))
                        s_mm.inc(e.matmul(ps_big[:, j * 512:(j + 1) * 512],
                                          lhsc_lo[:, p * 128:(p + 1) * 128],
                                          comp_lo[p % 2][:, j * 512:(j + 1) * 512],
                                          start=False, stop=True))
                    s_ac.wait(e, acb(p) + 1)
                    for j in range(4):
                        s_mm.inc(e.matmul(ps_big[:, j * 512:(j + 1) * 512],
                                          lhscv_hi[:], comp_hi[p % 2][:, j * 512:(j + 1) * 512],
                                          start=True, stop=False))
                        s_mm.inc(e.matmul(ps_big[:, j * 512:(j + 1) * 512],
                                          lhscv_lo[:], comp_lo[p % 2][:, j * 512:(j + 1) * 512],
                                          start=False, stop=True))
                    s_ac.wait(e, acb(p) + 2)
                    s_ev.wait(e, evb(p) + 2)
                    for base in (0, 64):
                        for h in range(H):
                            s_mm.inc(e.matmul(ps_big[base:base + 64, h * 64:(h + 1) * 64],
                                              Bp[p % 2][base:base + 64, h * 64:(h + 1) * 64],
                                              xct[p % 2][base:base + 64, h * 64:(h + 1) * 64],
                                              start=True, stop=True))
                    s_ev.wait(e, evb(p) + 7)
                    for base, psn in ((0, None), (64, None)):
                        if base == 64:
                            s_ev.wait(e, evb(p) + 9)
                        for hg in range(2):
                            psy = ps_y0 if hg == 0 else ps_y1
                            for hh in range(16):
                                h = hg * 16 + hh
                                s_mm.inc(e.matmul(psy[base:base + 64, hh * 64:(hh + 1) * 64],
                                                  rtF[base:base + 64, h * 64:(h + 1) * 64],
                                                  xct[p % 2][base:base + 64, h * 64:(h + 1) * 64],
                                                  start=True, stop=False))
                                s_mm.inc(e.matmul(psy[base:base + 64, hh * 64:(hh + 1) * 64],
                                                  vtF[base:base + 64, h * 64:(h + 1) * 64],
                                                  S_pp[base:base + 64, h * 64:(h + 1) * 64],
                                                  start=False, stop=True))
                    s_ev.wait(e, evb(p) + 13)
                    for fc in range(8):
                        s_mm.inc(e.transpose(ps_yt0[:, fc * 128:(fc + 1) * 128],
                                             yg[:, fc * 128:(fc + 1) * 128], eyebf[:]))
                    for fc in range(8):
                        s_mm.inc(e.transpose(ps_yt1[:, fc * 128:(fc + 1) * 128],
                                             yg[:, (8 + fc) * 128:(9 + fc) * 128], eyebf[:]))
                    s_ev.wait(e, evb(p) + 16)
                    for fc in range(16):
                        gg = p * 4 + fc // 4
                        if fc % 4 == 0:
                            stw(e, 7, gg, b)
                        wslice = wog[gg % 2][:, (fc % 4) * D:(fc % 4) * D + D]
                        s_mm.inc(e.matmul(ps_o0, yfm[:, fc * 128:(fc + 1) * 128],
                                          wslice[:, 0:512],
                                          start=(fc == 0), stop=(fc == 15)))
                        s_mm.inc(e.matmul(ps_o1, yfm[:, fc * 128:(fc + 1) * 128],
                                          wslice[:, 512:1024],
                                          start=(fc == 0), stop=(fc == 15)))

            @blk.scalar
            def _(e):
                kb.bar(e, b)
                for p in range(NPAIR):
                    s_ev.wait(e, evb(p) + 3)
                    s_ac.inc(e.activation(expsR[:], clampR[:], AF.Exp))
                    s_mm.wait(e, pb(p) + 19)
                    s_ac.inc(e.activation(expsV[:], ps_big[:], AF.Exp))
                    s_ev.wait(e, evb(p) + 12)
                    s_ac.inc(e.activation(sqscr[:], yg[:, 0:1024], AF.Square, accum_out=msc[:, 0:1]))
                    s_ev.wait(e, evb(p) + 13)
                    s_ac.inc(e.activation(sqscr[:], yg[:, 1024:2048], AF.Square, accum_out=msc[:, 1:2]))
                    s_ev.wait(e, evb(p) + 14)
                    sE(e, e.activation(msc[:, 3:4], msc[:, 2:3], AF.Ln, bias=EPS, scale=1.0 / DIN))
                    s_ac.inc(e.activation(msc[:, 4:5], msc[:, 3:4], AF.Exp, scale=-0.5))
                    s_ev.wait(e, evb(p) + 18)
                    stld(e, 1, p, d_x[p * 128:(p + 1) * 128, :], xout[p % 2][:])
                    if "d_yg" in kb.taps:
                        s_st.dinc(e.dma_start(out=d_yg[p * 128:(p + 1) * 128, :], in_=yg[:]))
                    if "d_yfm" in kb.taps:
                        s_st.dinc(e.dma_start(out=d_yfm[p * 128:(p + 1) * 128, :], in_=yfm[:]))
                    if "d_ms" in kb.taps:
                        s_st.dinc(e.dma_start(out=d_ms[p * 128:(p + 1) * 128, :], in_=msc[:]))

        if _STOP == "p6":
            ctx_mix.close()
            finalize_dummy()
            for s in kb.sems:
                s.check()
            return nc

        ctx_mix.close()

        with ExitStack() as pctx:
            lsb = lambda n, s, d: kb.lsb(pctx, n, s, d)
            xtl = [lsb("xtl%d" % i, [128, D], F32) for i in range(2)]
            xtf = [lsb("xtf%d" % i, [128, D], F32) for i in range(2)]
            lnb = lsb("lnb", [128, 8], F32)
            lnscr = lsb("lnscr", [128, D], F32)
            xn2_tm = lsb("xn2_tm", [128, D], BF16)
            xn2_fm = lsb("xn2_fm", [128, 8 * 1024], BF16)
            pr0 = lsb("pr0", [128, 8], F32)
            prs = lsb("prs", [128, 64], F32)
            probs_tm = lsb("probs_tm", [128, 2 * 64], F32)
            gsl = [lsb("gsl%d" % i, [128, 8 * 512], BF16) for i in range(2)]
            usl = [lsb("usl%d" % i, [128, 8 * 512], BF16) for i in range(2)]
            dwn = lsb("dwn", [128, 16 * D], BF16)
            sg = [lsb("sg%d" % i, [128, 1024], BF16) for i in range(2)]
            hbuf = lsb("hbuf", [128, 16 * 1024], BF16)
            ybuf = [lsb("ybuf%d" % i, [128, 8 * 1024], BF16) for i in range(2)]
            xo = [lsb("xo%d" % i, [128, D], F32) for i in range(2)]
            donb = lsb("donb", [1, NE * D], BF16)
            psA = kb.psum(pctx, "psA7", [128, 1024])
            psB = kb.psum(pctx, "psB7", [128, 1024])
            psC = kb.psum(pctx, "psC7", [128, 1024])
            ps_sm7 = kb.psum(pctx, "ps_sm7", [128, 512])
            ps_lnf = kb.psum(pctx, "ps_ln7", [128, 512])
            ps_ln = ps_lnf[:].bitcast(BF16)
            slots = [psA, psB, psC]
            b = kb.snap()
            MM_TT = 64 + 64 + 8 * 784
            EV_TT = 24 + 32 + 8 * 24 + 8
            AC_TT = 16 + 16 + 8 * 16
            NC7 = 16

            def m7(tt):
                return b[s_mm] + tt * MM_TT

            def e7(tt):
                return b[s_ev] + tt * EV_TT

            def a7(tt):
                return b[s_ac] + tt * AC_TT

            def ebm(tt, ei):
                return m7(tt) + 128 + 784 * ei

            def ebe(tt, ei):
                return e7(tt) + 56 + 24 * ei

            def eba(tt, ei):
                return a7(tt) + 32 + 16 * ei

            @blk.sync
            def _(e):
                kb.bar(e, b)
                for tt in range(2):
                    for ts in range(8):
                        k = tt * 8 + ts
                        if k >= 2:
                            t2, s2 = divmod(k - 2, 8)
                            s_ev.wait(e, e7(t2) + 3 * s2 + 2)
                            s_ac.wait(e, a7(t2) + 2 * s2 + 1)
                        stld(e, 0, k, xtl[k % 2][:], d_x[k * 128:(k + 1) * 128, :])
                    for ts in range(8):
                        k = tt * 8 + ts
                        if k >= 2:
                            t2, s2 = divmod(k - 2, 8)
                            s_ev.wait(e, e7(t2) + 248 + s2 + 1)
                        stld(e, 6, k, xtf[k % 2][:], d_x[k * 128:(k + 1) * 128, :])

            @blk.gpsimd
            def _(e):
                kb.bar(e, b)
                s_c1.dinc(e.dma_start(out=donb[:], in_=down_b[:]))
                gwv = gate_w.rearrange("ee (dc p) f -> ee p dc f", p=128)
                uwv = up_w.rearrange("ee (dc p) f -> ee p dc f", p=128)
                dwv = down_w.rearrange("ee (fc p) dd -> ee p fc dd", p=128)
                for tt in range(2):
                    for ei in range(NE):
                        di = tt * NE + ei
                        for fs in range(4):
                            gi = di * 4 + fs
                            if gi >= 2:
                                p2, f2 = divmod(gi - 2, 4)
                                t2, e2 = divmod(p2, NE)
                                s_mm.wait(e, ebm(t2, e2) + 128 * f2 + 128)
                            src = gwv[ei:ei + 1, :, :, fs * 512:(fs + 1) * 512].squeeze(0)
                            stld(e, 2, gi, gsl[gi % 2][:].rearrange("p (dc f) -> p dc f", dc=8), src)
                            src = uwv[ei:ei + 1, :, :, fs * 512:(fs + 1) * 512].squeeze(0)
                            stld(e, 4, gi, usl[gi % 2][:].rearrange("p (dc f) -> p dc f", dc=8), src)
                        if di >= 1:
                            t2, e2 = divmod(di - 1, NE)
                            s_mm.wait(e, ebm(t2, e2) + 784)
                        src = dwv[ei:ei + 1].squeeze(0)
                        stld(e, 5, di, dwn[:].rearrange("p (fc dd) -> p fc dd", fc=16), src)

            @blk.vector
            def _(e):
                kb.bar(e, b)
                for tt in range(2):
                    for ts in range(8):
                        k = tt * 8 + ts
                        stw(e, 0, k, b)
                        xb = xtl[k % 2]
                        vE(e, e.reduce_sum(lnb[:, 0:1], xb[:], AX.X))
                        s_ac.wait(e, a7(tt) + 2 * ts + 1)
                        vE(e, e.tensor_scalar_mul(lnb[:, 1:2], lnb[:, 0:1], 1.0 / D))
                        vE(e, e.tensor_scalar_mul(lnb[:, 5:6], lnb[:, 4:5], 1.0 / D))
                        vE(e, e.tensor_tensor(lnb[:, 2:3], lnb[:, 1:2], lnb[:, 1:2], ALU.mult))
                        s_ev.inc(e.tensor_tensor(lnb[:, 3:4], lnb[:, 5:6], lnb[:, 2:3], ALU.subtract))
                        s_ev.wait(e)
                        s_ac.wait(e, a7(tt) + 2 * ts + 2)
                        s_ev.inc(e.tensor_scalar(xn2_tm[:], xb[:], lnb[:, 1:2], lnb[:, 7:8],
                                                 ALU.subtract, ALU.mult))
                        s_mm.wait(e, m7(tt) + 8 * (ts + 1))
                        dst = xn2_fm[:].rearrange("p (dc t2) -> p dc t2", dc=8)
                        pv = ps_ln.rearrange("p (dc t2) -> p dc t2", dc=8)
                        s_ev.inc(e.tensor_copy(dst[:, :, ts * 128:(ts + 1) * 128], pv))
                    for ts in range(8):
                        pr = ps_sm7[:, (ts % 2) * 8:(ts % 2) * 8 + 8]
                        s_mm.wait(e, m7(tt) + 64 + 8 * ts + 8)
                        vE(e, e.reduce_max(prs[:, 0:1], pr, AX.X))
                        s_ev.inc(e.tensor_scalar_mul(prs[:, 1:2], prs[:, 0:1], -1.0))
                        s_ac.wait(e, a7(tt) + 16 + 2 * ts + 1)
                        vE(e, e.reduce_max(prs[:, 4:5], pr0[:], AX.X))
                        vE(e, e.tensor_scalar(prs[:, 16:24], pr0[:], prs[:, 4:5], None, ALU.is_lt))
                        vE(e, e.tensor_tensor(prs[:, 24:32], prs[:, 16:24], pr0[:], ALU.mult))
                        vE(e, e.reduce_max(prs[:, 5:6], prs[:, 24:32], AX.X))
                        vE(e, e.tensor_scalar(prs[:, 32:40], pr0[:], prs[:, 5:6], None, ALU.is_ge))
                        vE(e, e.tensor_tensor(prs[:, 40:48], prs[:, 32:40], pr0[:], ALU.mult))
                        vE(e, e.reduce_sum(prs[:, 6:7], prs[:, 40:48], AX.X))
                        vE(e, e.tensor_scalar_mul(prs[:, 56:57], prs[:, 2:3], 1e-9))
                        s_ev.inc(e.tensor_tensor(prs[:, 7:8], prs[:, 6:7], prs[:, 56:57], ALU.add))
                        s_ac.wait(e, a7(tt) + 16 + 2 * ts + 2)
                        s_ev.inc(e.tensor_scalar_mul(prs[:, 48:56], prs[:, 40:48], prs[:, 3:4]))
                        s_ev.wait(e)
                        s_ev.inc(e.tensor_copy(probs_tm[:, tt * 64 + ts * 8:tt * 64 + ts * 8 + 8],
                                               prs[:, 48:56]))
                    for ei in range(NE):
                        for kk in range(16):
                            s_mm.wait(e, ebm(tt, ei) + 32 * kk + 32)
                            s_ac.wait(e, eba(tt, ei) + kk + 1)
                            ups = slots[(2 * kk + 1) % 3]
                            s_ev.inc(e.scalar_tensor_tensor(
                                hbuf[:, kk * 1024:(kk + 1) * 1024], ups[:],
                                ub_cols[:, ei * 16 + kk:ei * 16 + kk + 1],
                                sg[kk % 2][:], ALU.add, ALU.mult))
                        for ts in range(8):
                            s_mm.wait(e, ebm(tt, ei) + 512 + 34 * (ts + 1))
                            if ei == 0 and ts == 0:
                                s_ev.wait(e, e7(tt) + 56)
                            ysl = slice(ts * 1024, (ts + 1) * 1024)
                            pcol = probs_tm[:, tt * 64 + ts * 8 + ei:tt * 64 + ts * 8 + ei + 1]
                            yps = slots[ts % 2]
                            if ei == 0:
                                s_ev.inc(e.tensor_scalar_mul(ybuf[1][:, ysl], yps[:], pcol))
                            else:
                                s_ev.inc(e.scalar_tensor_tensor(ybuf[(ei + 1) % 2][:, ysl], yps[:],
                                                                pcol, ybuf[ei % 2][:, ysl],
                                                                ALU.mult, ALU.add))
                    for ts in range(8):
                        k = tt * 8 + ts
                        stw(e, 6, k, b)
                        if k >= 2:
                            stw(e, 1, k - 2, b)
                        s_ev.inc(e.tensor_tensor(xo[k % 2][:], xtf[k % 2][:],
                                                 ybuf[0][:, ts * 1024:(ts + 1) * 1024], ALU.add))

            @blk.scalar
            def _(e):
                kb.bar(e, b)
                s_c1.wait(e, b[s_c1] + NC7)
                for tt in range(2):
                    for ts in range(8):
                        k = tt * 8 + ts
                        stw(e, 0, k, b)
                        xb = xtl[k % 2]
                        s_ac.inc(e.activation(lnscr[:], xb[:], AF.Square, accum_out=lnb[:, 4:5]))
                        s_ev.wait(e, e7(tt) + 3 * ts + 1)
                        sE(e, e.activation(lnb[:, 6:7], lnb[:, 3:4], AF.Ln, bias=EPS))
                        s_ac.inc(e.activation(lnb[:, 7:8], lnb[:, 6:7], AF.Exp, scale=-0.5))
                    for ts in range(8):
                        pr = ps_sm7[:, (ts % 2) * 8:(ts % 2) * 8 + 8]
                        s_ev.wait(e, e7(tt) + 24 + 4 * ts + 1)
                        s_ac.inc(e.activation(pr0[:], pr, AF.Exp, bias=prs[:, 1:2],
                                              accum_out=prs[:, 2:3]))
                        s_ev.wait(e, e7(tt) + 24 + 4 * ts + 2)
                        sE(e, e.activation(prs[:, 57:58], prs[:, 7:8], AF.Ln))
                        s_ac.inc(e.activation(prs[:, 3:4], prs[:, 57:58], AF.Exp, scale=-1.0))
                    for ei in range(NE):
                        for kk in range(16):
                            s_mm.wait(e, ebm(tt, ei) + 32 * kk + 16)
                            if kk < 2:
                                if ei > 0:
                                    s_ev.wait(e, ebe(tt, ei - 1) + 15 + kk)
                                elif tt > 0:
                                    s_ev.wait(e, ebe(tt - 1, NE - 1) + 15 + kk)
                            else:
                                s_ev.wait(e, ebe(tt, ei) + kk - 1)
                            gps = slots[(2 * kk) % 3]
                            s_ac.inc(e.activation(sg[kk % 2][:], gps[:], AF.Silu,
                                                  bias=gb_cols[:, ei * 16 + kk:ei * 16 + kk + 1]))
                    for ts in range(8):
                        k = tt * 8 + ts
                        s_ev.wait(e, e7(tt) + 248 + ts + 1)
                        stld(e, 1, k, out[k * 128:(k + 1) * 128, :], xo[k % 2][:])

            @blk.tensor
            def _(e):
                kb.bar(e, b)
                s_c1.wait(e, b[s_c1] + NC7)
                for tt in range(2):
                    for ts in range(8):
                        s_ev.wait(e, e7(tt) + 3 * ts + 2)
                        if ts >= 1:
                            s_ev.wait(e, e7(tt) + 3 * ts)
                        for dc in range(8):
                            s_mm.inc(e.transpose(ps_ln[:, dc * 128:(dc + 1) * 128],
                                                 xn2_tm[:, dc * 128:(dc + 1) * 128], eyebf[:]))
                    for ts in range(8):
                        s_ev.wait(e, e7(tt) + 3 * ts + 3)
                        if ts >= 2:
                            s_ac.wait(e, a7(tt) + 16 + 2 * ts - 3)
                        pr = ps_sm7[:, (ts % 2) * 8:(ts % 2) * 8 + 8]
                        for dc in range(8):
                            s_mm.inc(e.matmul(pr,
                                              xn2_fm[:, dc * 1024 + ts * 128:dc * 1024 + (ts + 1) * 128],
                                              wrT[:, dc * NE:(dc + 1) * NE],
                                              start=(dc == 0), stop=(dc == 7)))
                    for ei in range(NE):
                        if ei > 0:
                            s_ev.wait(e, ebe(tt, ei - 1) + 24)
                        elif tt > 0:
                            s_ev.wait(e, ebe(tt - 1, NE - 1) + 24)
                        for fs in range(4):
                            gi = (tt * NE + ei) * 4 + fs
                            stw(e, 2, gi, b)
                            stw(e, 4, gi, b)
                            for fb in range(4):
                                kk = fs * 4 + fb
                                if kk >= 2:
                                    s_ev.wait(e, ebe(tt, ei) + kk - 1)
                                gps = slots[(2 * kk) % 3]
                                for dc in range(8):
                                    for nn in range(2):
                                        s_mm.inc(e.matmul(
                                            gps[:, nn * 512:(nn + 1) * 512],
                                            gsl[gi % 2][:, dc * 512 + fb * 128:dc * 512 + (fb + 1) * 128],
                                            xn2_fm[:, dc * 1024 + nn * 512:dc * 1024 + (nn + 1) * 512],
                                            start=(dc == 0), stop=(dc == 7)))
                                if kk >= 1:
                                    s_ac.wait(e, eba(tt, ei) + kk)
                                ups = slots[(2 * kk + 1) % 3]
                                for dc in range(8):
                                    for nn in range(2):
                                        s_mm.inc(e.matmul(
                                            ups[:, nn * 512:(nn + 1) * 512],
                                            usl[gi % 2][:, dc * 512 + fb * 128:dc * 512 + (fb + 1) * 128],
                                            xn2_fm[:, dc * 1024 + nn * 512:dc * 1024 + (nn + 1) * 512],
                                            start=(dc == 0), stop=(dc == 7)))
                        di = tt * NE + ei
                        stw(e, 5, di, b)
                        s_ev.wait(e, ebe(tt, ei) + 16)
                        s_ac.wait(e, eba(tt, ei) + 16)
                        for ts in range(8):
                            if ts >= 2:
                                s_ev.wait(e, ebe(tt, ei) + 16 + ts - 1)
                            yps = slots[ts % 2]
                            s_mm.inc(e.matmul(yps[:, 0:512], ones_bf[0:1, :],
                                              donb[0:1, ei * D:ei * D + 512],
                                              start=True, stop=False))
                            s_mm.inc(e.matmul(yps[:, 512:1024], ones_bf[0:1, :],
                                              donb[0:1, ei * D + 512:(ei + 1) * D],
                                              start=True, stop=False))
                            for fc in range(16):
                                s_mm.inc(e.matmul(yps[:, 0:512],
                                                  hbuf[:, fc * 1024 + ts * 128:fc * 1024 + (ts + 1) * 128],
                                                  dwn[:, fc * D:fc * D + 512],
                                                  start=False, stop=(fc == 15)))
                                s_mm.inc(e.matmul(yps[:, 512:1024],
                                                  hbuf[:, fc * 1024 + ts * 128:fc * 1024 + (ts + 1) * 128],
                                                  dwn[:, fc * D + 512:(fc + 1) * D],
                                                  start=False, stop=(fc == 15)))

        sn = kb.snap()

        @blk.sync
        def _(e):
            kb.bar(e, sn)

        for s in kb.sems:
            s.check()
    return nc


def _consts(core):
    c = {}
    c["eye_f"] = np.eye(128, dtype=np.float32)
    c["eye_bf"] = np.eye(128, dtype=ml_dtypes.bfloat16)
    ltm = np.triu(np.ones((CH, CH), np.float32))
    c["ltm2"] = np.concatenate([ltm, ltm], axis=0)
    dd = np.zeros((64, H * CH), np.float32)
    for h2 in range(H):
        dd[h2, h2 * CH:(h2 + 1) * CH] = 1.0
        dd[32 + h2, h2 * CH:(h2 + 1) * CH] = 1.0
    c["deltadelta"] = dd
    ho = np.zeros((2, 128), np.float32)
    ho[0, 0:64] = 1.0
    ho[1, 64:128] = 1.0
    c["halfones"] = ho
    m9 = np.zeros((1, 9), np.float32)
    m9[0, 0] = 1.0
    m9[0, 1:1 + core] = 1.0
    c["m9"] = m9
    s8 = np.zeros((NE, NE * 128), np.float32)
    for ei in range(NE):
        s8[ei, ei * 128:(ei + 1) * 128] = 1.0
    c["sel8"] = s8
    c["ones_row"] = np.ones((1, 128), np.float32)
    return c


def make_inmaps(inputs):
    u = np.asarray(inputs["L0_attn_u"], np.float32).reshape(16384, 1024)
    shared = {
        "w_in": np.ascontiguousarray(np.asarray(inputs["L0_attn_in_proj_w"], np.float32)),
        "conv_w": np.ascontiguousarray(np.asarray(inputs["L0_attn_conv_w"], np.float32)),
        "conv_b": np.asarray(inputs["L0_attn_conv_b"], np.float32).reshape(1, CONVD),
        "init_nhp": np.ascontiguousarray(
            np.transpose(np.asarray(inputs["L0_attn_init_states"], np.float32), (2, 0, 1)).reshape(NS, H * PDIM)),
        "dt_bias": np.asarray(inputs["L0_attn_dt_bias"], np.float32).reshape(H, 1),
        "a_log": np.asarray(inputs["L0_attn_A_log"], np.float32).reshape(H, 1),
        "d_head": np.asarray(inputs["L0_attn_D"], np.float32).reshape(1, H),
        "norm_w": np.asarray(inputs["L0_attn_norm_w"], np.float32).reshape(1, DIN),
        "w_out": np.ascontiguousarray(np.asarray(inputs["L0_attn_out_proj_w"], np.float32)),
        "router_w": np.ascontiguousarray(np.asarray(inputs["L0_ffn_router_w"], np.float32)),
        "gate_w": np.ascontiguousarray(np.asarray(inputs["L0_ffn_gate_w"], np.float32)),
        "gate_b": np.asarray(inputs["L0_ffn_gate_b"], np.float32).reshape(1, NE * DFF),
        "up_w": np.ascontiguousarray(np.asarray(inputs["L0_ffn_up_w"], np.float32)),
        "up_b": np.asarray(inputs["L0_ffn_up_b"], np.float32).reshape(1, NE * DFF),
        "down_w": np.ascontiguousarray(np.asarray(inputs["L0_ffn_down_w"], np.float32)),
        "down_b": np.asarray(inputs["L0_ffn_down_b"], np.float32).reshape(1, NE * D),
    }
    in_maps = []
    for k in range(NCORES):
        m = dict(shared)
        if k == 0:
            up = np.concatenate([np.zeros((HALO, D), np.float32), u[0:T]], axis=0)
        else:
            up = u[k * T - HALO:k * T + T]
        m["u_pad"] = np.ascontiguousarray(up)
        m.update(_consts(k))
        in_maps.append(m)
    return in_maps


def kernel(**inputs):
    in_maps = make_inmaps(inputs)
    nc = build_program(taps=_TAPS)
    res = run_bass_kernel_spmd(nc, in_maps, core_ids=list(range(NCORES)))
    outs = [r["out"] for r in res.results]
    full = np.concatenate(outs, axis=0).reshape(1, 16384, 1024)
    if _TAPS:
        kernel.tap_results = res.results
    return full


# version 02
# speedup vs baseline: 1.0546x; 1.0546x over previous
import numpy as np
import ml_dtypes
from contextlib import ExitStack
import concourse.bass as bass
import concourse.mybir as mybir
from concourse.bass_utils import run_bass_kernel_spmd

F32 = mybir.dt.float32
BF16 = mybir.dt.bfloat16
F32R = mybir.dt.float32r
AF = mybir.ActivationFunctionType
ALU = mybir.AluOpType
AX = mybir.AxisListType

NCORES = 8
T = 2048
HALO = 128
TTOK = T + HALO
D = 1024
DIN = 2048
CONVD = 2176
DPROJ = 4256
H = 32
PDIM = 64
NS = 64
CH = 64
NCH = T // CH
NPAIR = NCH // 2
NT = TTOK // 128
DFF = 2048
NE = 8
EPS = 1e-5

_TAPS = set()
_STOP = None


class Sem:
    def __init__(self, ctx, nc, name):
        self.h = ctx.enter_context(nc.semaphore(name))
        self.n = 0
        self.name = name
        self.maxwait = 0

    def inc(self, inst, k=1):
        inst.then_inc(self.h, k)
        self.n += k
        return inst

    def dinc(self, inst):
        return self.inc(inst, 16)

    def wait(self, eng, n=None):
        v = self.n if n is None else n
        self.maxwait = max(self.maxwait, v)
        if v > 0:
            eng.wait_ge(self.h, v)

    def check(self):
        assert self.maxwait <= self.n, \
            f"sem {self.name}: wait {self.maxwait} exceeds final count {self.n}"


class KB:
    def __init__(self, taps):
        self.taps = set(taps)
        self.nc = bass.Bass("TRN2", target_bir_lowering=False, debug=False, detect_race_conditions=False)
        self.ctx = ExitStack()

    def sb(self, name, shape, dt):
        return self.ctx.enter_context(self.nc.sbuf_tensor(name, shape, dt))

    def psum(self, pctx, name, shape, dt=F32):
        return pctx.enter_context(self.nc.psum_tensor(name, shape, dt))

    def lsb(self, pctx, name, shape, dt):
        return pctx.enter_context(self.nc.sbuf_tensor(name, shape, dt))

    def snap(self):
        return {s: s.n for s in self.sems}

    def bar(self, eng, snap):
        for s in self.sems:
            s.wait(eng, snap[s])


def build_program(taps=()):
    kb = KB(taps)
    nc = kb.nc

    def dram_in(name, shape, dt=F32):
        return nc.dram_tensor(name, shape, dt, kind="ExternalInput").ap()

    def dram_tmp(name, shape, dt):
        kind = "ExternalOutput" if name in kb.taps else "Internal"
        return nc.dram_tensor(name, shape, dt, kind=kind).ap()

    u_pad = dram_in("u_pad", [TTOK, D])
    w_in = dram_in("w_in", [DPROJ, D])
    conv_w = dram_in("conv_w", [CONVD, 4])
    conv_b = dram_in("conv_b", [1, CONVD])
    init_nhp = dram_in("init_nhp", [NS, H * PDIM])
    dt_bias = dram_in("dt_bias", [H, 1])
    a_log = dram_in("a_log", [H, 1])
    d_head = dram_in("d_head", [1, H])
    norm_w = dram_in("norm_w", [1, DIN])
    w_out = dram_in("w_out", [D, DIN])
    router_w = dram_in("router_w", [NE, D])
    gate_w = dram_in("gate_w", [NE, D, DFF])
    gate_b = dram_in("gate_b", [1, NE * DFF])
    up_w = dram_in("up_w", [NE, D, DFF])
    up_b = dram_in("up_b", [1, NE * DFF])
    down_w = dram_in("down_w", [NE, DFF, D])
    down_b = dram_in("down_b", [1, NE * D])
    eye_f = dram_in("eye_f", [128, 128])
    eye_bfc = dram_in("eye_bf", [128, 128], BF16)
    ltm2_c = dram_in("ltm2", [128, CH])
    dd_c = dram_in("deltadelta", [64, H * CH])
    hones_c = dram_in("halfones", [2, 128])
    m9_c = dram_in("m9", [1, 9])
    sel8_c = dram_in("sel8", [NE, NE * 128])
    ones_c = dram_in("ones_row", [1, 128])

    out = nc.dram_tensor("out", [T, D], F32, kind="ExternalOutput").ap()

    d_xnfm = dram_tmp("d_xnfm", [D, TTOK], BF16)
    d_z = dram_tmp("d_z", [DFF, TTOK], BF16)
    d_xbc = dram_tmp("d_xbc", [CONVD, TTOK], BF16)
    d_xconv = dram_tmp("d_xconv", [T, CONVD], BF16)
    d_sz = dram_tmp("d_sz", [T, DIN], BF16)
    d_x = dram_tmp("d_x", [T, D], F32)
    d_woutT = dram_tmp("d_woutT", [DIN, D], BF16)
    d_cumahi = dram_tmp("d_cumahi", [H, T], BF16)
    d_cumalo = dram_tmp("d_cumalo", [H, T], F32R)
    d_dec = dram_tmp("d_dec", [NCH, H], F32)
    d_ccin = nc.dram_tensor("d_ccin", [65, 2048], F32, kind="Internal").ap()
    d_ccout = nc.dram_tensor("d_ccout", [520, 2048], F32, kind="Internal").ap()
    d_dt = dram_tmp("d_dt", [H, T], F32)
    d_cuma = dram_tmp("d_cuma", [H, T], F32)
    d_sk = dram_tmp("d_sk", [NS + 1, H * PDIM], F32)
    d_sin = dram_tmp("d_sin", [NS, H * PDIM], F32)
    d_yg = dram_tmp("d_yg", [T, DIN], BF16)
    d_yfm = dram_tmp("d_yfm", [T, DIN], BF16)
    d_ms = dram_tmp("d_ms", [T, 16], F32)

    with kb.ctx:
        sb = kb.sb
        eye32 = sb("eye32", [128, 128], F32)
        eyebf = sb("eyebf", [128, 128], BF16)
        ones_bf = sb("ones_bf", [1, 128], BF16)
        gb_cols = sb("gb_cols", [128, NE * 16], F32)
        ub_cols = sb("ub_cols", [128, NE * 16], F32)
        wrT = sb("wrT", [128, 8 * NE], BF16)
        epsc = sb("epsc", [128, 1], F32)
        ones64f = sb("ones64f", [1, 64], F32)
        nc.const_aps.register(F32, EPS, epsc[:])
        ctx_mix = ExitStack()
        msb = lambda n, s, d=F32: kb.lsb(ctx_mix, n, s, d)
        ltm2 = msb("ltm2s", [128, CH])
        ddel_bf = msb("ddel_bf", [64, H * CH], BF16)
        ddel_r = msb("ddel_r", [64, H * CH], F32R)
        hones_bf = msb("hones_bf", [2, 128], BF16)
        hones_r = msb("hones_r", [2, 128], F32R)
        rstmask = msb("rstmask", [H, T])
        m9 = msb("m9s", [1, 9])
        sel8_r = msb("sel8_r", [NE, NE * 128], F32R)
        convb_row = msb("convb_row", [1, CONVD], BF16)
        normw_cols = msb("normw_cols", [128, 16])
        dhead_rep = msb("dhead_rep", [64, H])
        Ddiag = msb("Ddiag", [128, H * CH], BF16)
        ddt_tm = msb("ddt_tm", [128, NPAIR * H], BF16)
        lhsc_hi = msb("lhsc_hi", [66, NPAIR * 128], BF16)
        lhsc_lo = msb("lhsc_lo", [66, NPAIR * 128], F32R)
        decrows = msb("decrows", [2, NPAIR * H], F32R)
        lhscv_hi = msb("lhscv_hi", [66, 128], BF16)
        lhscv_lo = msb("lhscv_lo", [66, 128], F32R)
        totrow = msb("totrow", [1, H])
        B_fm = msb("B_fm", [NS, TTOK], BF16)
        C_fm = msb("C_fm", [128, TTOK], BF16)
        S_pp = msb("S_pp", [128, H * PDIM], BF16)
        dtraw_fm = msb("dtraw_fm", [H, TTOK])

        s_ld = Sem(kb.ctx, nc, "s_ld")
        s_wl = Sem(kb.ctx, nc, "s_wl")
        s_mm = Sem(kb.ctx, nc, "s_mm")
        s_ev = Sem(kb.ctx, nc, "s_ev")
        s_ac = Sem(kb.ctx, nc, "s_ac")
        s_po = Sem(kb.ctx, nc, "s_po")
        s_st = Sem(kb.ctx, nc, "s_st")
        s_cc = Sem(kb.ctx, nc, "s_cc")
        s_x1 = Sem(kb.ctx, nc, "s_x1")
        s_x2 = Sem(kb.ctx, nc, "s_x2")
        s_vc = Sem(kb.ctx, nc, "s_vc")
        s_sc = Sem(kb.ctx, nc, "s_sc")
        s_pc = Sem(kb.ctx, nc, "s_pc")
        s_c1 = Sem(kb.ctx, nc, "s_c1")
        s_c2 = Sem(kb.ctx, nc, "s_c2")
        ST = [[Sem(kb.ctx, nc, "st%d%s" % (i, ab)) for ab in "ab"] for i in range(8)]
        STG = [[Sem(kb.ctx, nc, "sg%d%s" % (i, ab)) for ab in "ab"] for i in range(5)]
        kb.sems = [s_ld, s_wl, s_mm, s_ev, s_ac, s_po, s_st, s_cc, s_x1, s_x2,
                   s_vc, s_sc, s_pc, s_c1, s_c2] + \
                  [s for pr in ST for s in pr] + [s for pr in STG for s in pr]

        def stld(e, slot, idx, out, in_):
            ST[slot][idx % 2].dinc(e.dma_start(out=out, in_=in_))

        def stw(e, slot, idx, base):
            s = ST[slot][idx % 2]
            s.wait(e, base[s] + 16 * (idx // 2 + 1))

        def stgld(e, slot, idx, out, in_):
            STG[slot][idx % 2].dinc(e.dma_start(out=out, in_=in_))

        def stgw(e, slot, idx, base):
            s = STG[slot][idx % 2]
            s.wait(e, base[s] + 16 * (idx // 2 + 1))

        def vE(e, inst):
            s_vc.inc(inst)
            s_vc.wait(e)
            return inst

        def sE(e, inst):
            s_sc.inc(inst)
            s_sc.wait(e)
            return inst

        def pE(e, inst):
            s_pc.inc(inst)
            s_pc.wait(e)
            return inst

        blk = kb.ctx.enter_context(nc.Block())

        def finalize_dummy():
            with nc.sbuf_tensor("fin_zero", [128, D], F32) as fz:
                sn = kb.snap()

                @blk.vector
                def _(e):
                    kb.bar(e, sn)
                    s_ev.inc(e.memset(fz[:], 0.0))

                @blk.scalar
                def _(e):
                    kb.bar(e, sn)
                    s_ev.wait(e)
                    for t in range(T // 128):
                        s_st.dinc(e.dma_start(out=out[t * 128:(t + 1) * 128, :], in_=fz[:]))
                    s_st.wait(e)

        with ExitStack() as pctx:
            lsb = lambda n, s, d: kb.lsb(pctx, n, s, d)
            ps = lambda n, s, d=F32: kb.psum(pctx, n, s, d)
            rw_bf = lsb("rw_bf", [NE, D], BF16)
            c32a = lsb("c32a", [64, H * CH], F32)
            c32b = lsb("c32b", [NE, NE * 128], F32)
            rowbuf = lsb("rowbuf", [1, DIN], F32)
            dh_row = lsb("dh_row", [1, H], F32)
            ps_cols = ps("ps_cols", [128, 17 * 16])
            ps_dh = ps("ps_dh", [64, H])
            ps_rt = ps("ps_rt", [128, 8 * NE], BF16)
            wo_nat = [lsb("wo_nat%d" % i, [128, DIN], BF16) for i in range(2)]
            wo_col = [lsb("wo_col%d" % i, [128, 16 * 128], BF16) for i in range(2)]
            ps_wo = [ps("ps_wo%d" % i, [128, 2048], BF16) for i in range(2)]

            rows = [(norm_w[:], DIN, normw_cols, 0)]
            for ei in range(NE):
                rows.append((gate_b[0:1, ei * DFF:(ei + 1) * DFF], DFF, gb_cols, ei * 16))
                rows.append((up_b[0:1, ei * DFF:(ei + 1) * DFF], DFF, ub_cols, ei * 16))

            b = kb.snap()

            @blk.sync
            def _(e):
                s_c1.dinc(e.dma_start(out=eye32[:], in_=eye_f[:]))
                s_c1.dinc(e.dma_start(out=eyebf[:], in_=eye_bfc[:]))
                s_c1.dinc(e.dma_start(out=ltm2[:], in_=ltm2_c[:]))
                s_c1.dinc(e.dma_start(out=m9[:], in_=m9_c[:]))
                s_c1.dinc(e.dma_start(out=sel8_r[:].bitcast(F32), in_=sel8_c[:]))
                s_c1.dinc(e.dma_start(out=dh_row[:], in_=d_head[:]))
                for i, (ap, w, _, _o) in enumerate(rows):
                    if i >= 1:
                        s_mm.wait(e, b[s_mm] + 16 * i)
                    s_ld.dinc(e.dma_start(out=rowbuf[0:1, 0:w], in_=ap))

            NC1 = 6 * 16
            NC2 = 5 * 16

            @blk.gpsimd
            def _(e):
                s_c2.dinc(e.dma_start(out=ddel_bf[:], in_=dd_c[:]))
                s_c2.dinc(e.dma_start(out=hones_bf[:], in_=hones_c[:]))
                s_c2.dinc(e.dma_start(out=ones_bf[:], in_=ones_c[:]))
                s_c2.dinc(e.dma_start(out=convb_row[:], in_=conv_b[:]))
                s_c2.dinc(e.dma_start(out=rw_bf[:], in_=router_w[:]))

            @blk.tensor
            def _(e):
                s_c1.wait(e, b[s_c1] + NC1)
                s_c2.wait(e, b[s_c2] + NC2)
                for i, (ap, w, _, _o) in enumerate(rows):
                    s_ld.wait(e, b[s_ld] + 16 * (i + 1))
                    for j in range(w // 128):
                        s_mm.inc(e.transpose(ps_cols[:, i * 16 + j:i * 16 + j + 1],
                                             rowbuf[0:1, j * 128:(j + 1) * 128], eye32[0:1, 0:1]))
                s_ev.wait(e, b[s_ev] + 1)
                s_mm.inc(e.matmul(ps_dh[:], ones64f[:], dh_row[:], start=True, stop=True))
                s_c1.wait(e, b[s_c1] + NC1)
                s_c2.wait(e, b[s_c2] + NC2)
                for dc in range(8):
                    s_mm.inc(e.transpose(ps_rt[:, dc * NE:(dc + 1) * NE],
                                         rw_bf[:, dc * 128:(dc + 1) * 128], eyebf[0:NE, 0:NE]))

            @blk.vector
            def _(e):
                e.memset(epsc[:], EPS)
                s_ev.inc(e.memset(ones64f[:], 1.0))
                vE(e, e.memset(rstmask[:], 0.0))
                for cc in range(NCH):
                    e.memset(rstmask[:, cc * CH + 1:(cc + 1) * CH], 1.0)
                s_c1.wait(e, b[s_c1] + NC1)
                s_c2.wait(e, b[s_c2] + NC2)
                e.tensor_copy(c32a[:], ddel_bf[:])
                e.tensor_copy(ddel_r[:], c32a[:])
                vE(e, e.tensor_copy(c32b[0:2, 0:128], hones_bf[:]))
                e.tensor_copy(hones_r[:], c32b[0:2, 0:128])
                e.tensor_copy(c32b[:], sel8_r[:].bitcast(F32))
                s_ev.inc(e.tensor_copy(sel8_r[:], c32b[:]))
                s_mm.wait(e, b[s_mm] + 17 * 16 + 1 + 8)
                for i, (ap, w, ctile, off) in enumerate(rows):
                    e.tensor_copy(ctile[:, off:off + w // 128], ps_cols[:, i * 16:i * 16 + w // 128])
                vE(e, e.tensor_copy(dhead_rep[:], ps_dh[:]))
                e.tensor_copy(wrT[:], ps_rt[:])
                ey_v = eyebf[0:64, 0:CH].unsqueeze(1).broadcast_to([64, H, CH])
                dh_v = dhead_rep[:].unsqueeze(2).broadcast_to([64, H, CH])
                dg_v = Ddiag[0:64, :].rearrange("s (h l) -> s h l", h=H)
                e.tensor_tensor(dg_v, ey_v, dh_v, ALU.mult)
                s_ev.inc(e.tensor_copy(Ddiag[64:128, :], Ddiag[0:64, :]))

            if _STOP == "p0a":
                finalize_dummy()
                for s in kb.sems:
                    s.check()
                return nc
            b2 = kb.snap()

            @blk.gpsimd
            def _(e):
                for dc in range(8):
                    if dc >= 2:
                        s_mm.wait(e, b2[s_mm] + 16 * (dc - 1))
                    stgld(e, 0, dc, wo_nat[dc % 2][:], w_out[dc * 128:(dc + 1) * 128, :])

            @blk.tensor
            def _(e):
                for dc in range(8):
                    stgw(e, 0, dc, b2)
                    if dc >= 2:
                        s_x1.wait(e, b2[s_x1] + (dc - 1))
                    for fc in range(16):
                        s_mm.inc(e.transpose(ps_wo[dc % 2][:, fc * 128:(fc + 1) * 128],
                                             wo_nat[dc % 2][:, fc * 128:(fc + 1) * 128], eyebf[:]))

            @blk.vector
            def _(e):
                for dc in range(8):
                    s_mm.wait(e, b2[s_mm] + 16 * (dc + 1))
                    if dc >= 2:
                        stw(e, 1, dc - 2, b2)
                    for fc in range(16):
                        inst = e.tensor_scalar_mul(wo_col[dc % 2][:, fc * 128:(fc + 1) * 128],
                                                   ps_wo[dc % 2][:, fc * 128:(fc + 1) * 128],
                                                   normw_cols[:, fc:fc + 1])
                        if fc == 15:
                            s_x1.inc(inst)

            @blk.scalar
            def _(e):
                for dc in range(8):
                    s_x1.wait(e, b2[s_x1] + (dc + 1))
                    dst = d_woutT.rearrange("(fc p) d -> p fc d", p=128)[:, :, dc * 128:(dc + 1) * 128]
                    src = wo_col[dc % 2][:].rearrange("p (fc d) -> p fc d", fc=16)
                    stld(e, 1, dc, dst, src)

        if _STOP == "p0":
            ctx_mix.close()
            finalize_dummy()
            for s in kb.sems:
                s.check()
            return nc

        ctx12 = ExitStack()
        xn_sb = kb.lsb(ctx12, "xn_sb", [128, 8 * TTOK], BF16)
        with ExitStack() as pctx:
            lsb = lambda n, s, d: kb.lsb(pctx, n, s, d)
            utile = [lsb("u_tile%d" % i, [128, D], F32) for i in range(2)]
            ln_sc = lsb("ln_sc", [128, 8], F32)
            ln_scr = lsb("ln_scr", [128, D], F32)
            xn_tm = lsb("xn_tm", [128, D], BF16)
            ps_xt = kb.psum(pctx, "ps_xt", [128, D], BF16)
            b = kb.snap()
            EVT = 3

            @blk.sync
            def _(e):
                kb.bar(e, b)
                for t in range(NT):
                    if t >= 2:
                        s_ev.wait(e, b[s_ev] + (t - 1) * EVT)
                    stld(e, 0, t, utile[t % 2][:], u_pad[t * 128:(t + 1) * 128, :])

            @blk.vector
            def _(e):
                kb.bar(e, b)
                for t in range(NT):
                    stw(e, 0, t, b)
                    ub = utile[t % 2]
                    vE(e, e.reduce_sum(ln_sc[:, 0:1], ub[:], AX.X))
                    s_ac.wait(e, b[s_ac] + 2 * t + 1)
                    vE(e, e.tensor_scalar_mul(ln_sc[:, 1:2], ln_sc[:, 0:1], 1.0 / D))
                    vE(e, e.tensor_scalar_mul(ln_sc[:, 5:6], ln_sc[:, 4:5], 1.0 / D))
                    vE(e, e.tensor_tensor(ln_sc[:, 2:3], ln_sc[:, 1:2], ln_sc[:, 1:2], ALU.mult))
                    s_ev.inc(e.tensor_tensor(ln_sc[:, 3:4], ln_sc[:, 5:6], ln_sc[:, 2:3], ALU.subtract))
                    s_ev.wait(e)
                    s_ac.wait(e, b[s_ac] + 2 * t + 2)
                    s_ev.inc(e.tensor_scalar(xn_tm[:], ub[:], ln_sc[:, 1:2], ln_sc[:, 7:8],
                                             ALU.subtract, ALU.mult))
                    s_mm.wait(e, b[s_mm] + 8 * (t + 1))
                    v = xn_sb[:].rearrange("p (dc tt) -> p dc tt", dc=8)
                    pv = ps_xt[:].rearrange("p (dc tt) -> p dc tt", dc=8)
                    s_ev.inc(e.tensor_copy(v[:, :, t * 128:(t + 1) * 128], pv))

            @blk.scalar
            def _(e):
                kb.bar(e, b)
                for t in range(NT):
                    stw(e, 0, t, b)
                    ub = utile[t % 2]
                    s_ac.inc(e.activation(ln_scr[:], ub[:], AF.Square, accum_out=ln_sc[:, 4:5]))
                    s_ev.wait(e, b[s_ev] + t * EVT + 1)
                    sE(e, e.activation(ln_sc[:, 6:7], ln_sc[:, 3:4], AF.Ln, bias=EPS))
                    s_ac.inc(e.activation(ln_sc[:, 7:8], ln_sc[:, 6:7], AF.Exp, scale=-0.5))
                    if "d_xnfm" in kb.taps:
                        s_ev.wait(e, b[s_ev] + t * EVT + 3)
                        dst = d_xnfm.rearrange("(dc p) tt -> p dc tt", p=128)[:, :, t * 128:(t + 1) * 128]
                        src = xn_sb[:].rearrange("p (dc tt) -> p dc tt", dc=8)[:, :, t * 128:(t + 1) * 128]
                        s_st.dinc(e.dma_start(out=dst, in_=src))

            @blk.tensor
            def _(e):
                kb.bar(e, b)
                for t in range(NT):
                    s_ev.wait(e, b[s_ev] + t * EVT + 2)
                    if t >= 1:
                        s_ev.wait(e, b[s_ev] + t * EVT)
                    for dc in range(8):
                        s_mm.inc(e.transpose(ps_xt[:, dc * 128:(dc + 1) * 128],
                                             xn_tm[:, dc * 128:(dc + 1) * 128], eyebf[:]))

        if _STOP == "p1":
            ctx12.close()
            ctx_mix.close()
            finalize_dummy()
            for s in kb.sems:
                s.check()
            return nc

        NSPL = [(0, 512), (512, 512), (1024, 512), (1536, 512), (2048, 128)]
        NSP = len(NSPL)
        with ExitStack() as pctx:
            lsb = lambda n, s, d: kb.lsb(pctx, n, s, d)
            wnat = [lsb("wnat%d" % i, [128, D], BF16) for i in range(2)]
            wTt = [lsb("wTt%d" % i, [128, 8 * 128], BF16) for i in range(2)]
            evb = [lsb("evb%d" % i, [128, TTOK], BF16) for i in range(2)]
            ps_wt = [kb.psum(pctx, "ps_wt%d" % i, [128, 8 * 128], BF16) for i in range(2)]
            ps_ip = [kb.psum(pctx, "ps_ip%d" % i, [128, 512]) for i in range(2)]
            b = kb.snap()
            MMF = 8 + 8 * NSP

            @blk.gpsimd
            def _(e):
                kb.bar(e, b)
                for fc in range(34):
                    mrows = 32 if fc == 33 else 128
                    if fc >= 2:
                        s_mm.wait(e, b[s_mm] + (fc - 2) * MMF + 8)
                    stgld(e, 0, fc, wnat[fc % 2][0:mrows, :],
                         w_in[fc * 128:fc * 128 + mrows, :])

            @blk.tensor
            def _(e):
                kb.bar(e, b)
                for fc in range(34):
                    mrows = 32 if fc == 33 else 128
                    stgw(e, 0, fc, b)
                    if fc >= 2:
                        s_x1.wait(e, b[s_x1] + (fc - 1))
                    for dc in range(8):
                        s_mm.inc(e.transpose(ps_wt[fc % 2][:, dc * 128:dc * 128 + mrows],
                                             wnat[fc % 2][0:mrows, dc * 128:(dc + 1) * 128],
                                             eyebf[0:mrows, 0:mrows]))
                    s_x1.wait(e, b[s_x1] + (fc + 1))
                    for j, (t0, tw) in enumerate(NSPL):
                        g = fc * NSP + j
                        if g >= 2:
                            s_x2.wait(e, b[s_x2] + (g - 1))
                        for dc in range(8):
                            s_mm.inc(e.matmul(ps_ip[g % 2][0:mrows, 0:tw],
                                              wTt[fc % 2][:, dc * 128:dc * 128 + mrows],
                                              xn_sb[:, dc * TTOK + t0: dc * TTOK + t0 + tw],
                                              start=(dc == 0), stop=(dc == 7)))

            @blk.vector
            def _(e):
                kb.bar(e, b)
                for fc in range(34):
                    mrows = 32 if fc == 33 else 128
                    s_mm.wait(e, b[s_mm] + fc * MMF + 8)
                    dv = wTt[fc % 2][:].rearrange("p (dc m) -> p dc m", dc=8)[:, :, 0:mrows]
                    pv = ps_wt[fc % 2][:].rearrange("p (dc m) -> p dc m", dc=8)[:, :, 0:mrows]
                    s_x1.inc(e.tensor_copy(dv, pv))
                    if fc >= 2 and fc < 33:
                        stw(e, 1, fc - 2, b)
                    for j, (t0, tw) in enumerate(NSPL):
                        g = fc * NSP + j
                        s_mm.wait(e, b[s_mm] + fc * MMF + 8 + 8 * (j + 1))
                        if fc == 33:
                            s_x2.inc(e.tensor_copy(dtraw_fm[:, t0:t0 + tw], ps_ip[g % 2][0:32, 0:tw]))
                        else:
                            s_x2.inc(e.tensor_copy(evb[fc % 2][:, t0:t0 + tw], ps_ip[g % 2][:, 0:tw]))

            @blk.scalar
            def _(e):
                kb.bar(e, b)
                for fc in range(33):
                    s_x2.wait(e, b[s_x2] + (fc + 1) * NSP)
                    if fc < 16:
                        dst = d_z[fc * 128:(fc + 1) * 128, :]
                    else:
                        dst = d_xbc[(fc - 16) * 128:(fc - 15) * 128, :]
                    stld(e, 1, fc, dst, evb[fc % 2][:])

        if _STOP == "p2":
            ctx12.close()
            ctx_mix.close()
            finalize_dummy()
            for s in kb.sems:
                s.check()
            return nc

        ctx12.close()

        with ExitStack() as pctx:
            lsb = lambda n, s, d: kb.lsb(pctx, n, s, d)
            dtb_col = lsb("dtb_col", [H, 1], F32)
            alog_col = lsb("alog_col", [H, 1], F32)
            a_col = lsb("a_col", [H, 1], F32)
            sp0 = lsb("sp0", [H, T], F32)
            sp1 = lsb("sp1", [H, T], F32)
            sp2 = lsb("sp2", [H, T], F32)
            dt_fm = lsb("dt_fm", [H, T], F32)
            cuma = lsb("cuma", [H, T], F32)
            lnadj = lsb("lnadj", [H, T], F32)
            cahi_bf = lsb("cahi_bf", [H, T], BF16)
            calo = lsb("calo", [H, T], F32)
            calo_r = lsb("calo_r", [H, T], F32R)
            lahi_bf = lsb("lahi_bf", [H, T], BF16)
            lalo_r = lsb("lalo_r", [H, T], F32R)
            a2_col = lsb("a2_col", [H, 1], F32)
            adt = sp2
            lndt = sp1
            lalo = sp2
            ddarg = sp1
            ddexp = sp0
            ddt_fm = sp2
            decfm = lsb("decfm", [H, NCH], F32)
            dectm = lsb("dectm", [NCH, H], F32)
            tot_col = lsb("tot_col", [H, 1], F32)
            dr32 = lsb("dr32", [2, NPAIR * H], F32)
            ps_sm = kb.psum(pctx, "ps_sm", [128, NPAIR * H])
            ps_sm2 = kb.psum(pctx, "ps_sm2", [NCH, 2 * H])
            b = kb.snap()

            @blk.sync
            def _(e):
                kb.bar(e, b)
                s_ld.dinc(e.dma_start(out=dtb_col[:], in_=dt_bias[:]))
                s_ld.dinc(e.dma_start(out=alog_col[:], in_=a_log[:]))
                s_st.wait(e, b[s_st] + 16 * 3)
                src2 = d_dec.rearrange("(p eo) h -> eo p h", eo=2)
                s_ld.dinc(e.dma_start(out=dr32[:].rearrange("eo (p h) -> eo p h", p=NPAIR), in_=src2))

            @blk.scalar
            def _(e):
                kb.bar(e, b)
                s_ld.wait(e, b[s_ld] + 32)
                e.activation(sp0[:], dtraw_fm[:, HALO:], AF.Identity, bias=dtb_col[:])
                e.activation(sp1[:], sp0[:], AF.Abs)
                e.activation(sp2[:], sp1[:], AF.Exp, scale=-1.0)
                sE(e, e.activation(sp1[:], sp2[:], AF.Ln, bias=1.0))
                e.activation(sp2[:], sp0[:], AF.Relu)
                s_ac.inc(e.activation(a_col[:], alog_col[:], AF.Exp))
                s_ev.wait(e, b[s_ev] + 1)
                s_ac.inc(e.activation(lndt[:], dt_fm[:], AF.Ln))
                s_ev.wait(e, b[s_ev] + 2)
                e.activation(ddexp[:], ddarg[:], AF.Exp)
                lastv2 = cuma[:].rearrange("h (c s) -> h c s", c=NCH)[:, :, CH - 1:CH].rearrange("h c s -> h (c s)")
                s_ac.inc(e.activation(decfm[:], lastv2, AF.Exp))

            @blk.vector
            def _(e):
                kb.bar(e, b)
                s_ac.wait(e, b[s_ac] + 1)
                e.tensor_tensor(dt_fm[:], sp1[:], sp2[:], ALU.add)
                vE(e, e.tensor_scalar_mul(a2_col[:], a_col[:], -1.0))
                e.tensor_scalar_mul(adt[:], dt_fm[:], a2_col[:])
                s_ev.inc(e.tensor_tensor_scan(cuma[:], rstmask[:], adt[:], 0.0, ALU.mult, ALU.add))
                s_ac.wait(e, b[s_ac] + 2)
                e.tensor_tensor(lnadj[:], lndt[:], cuma[:], ALU.subtract)
                e.tensor_copy(cahi_bf[:], cuma[:])
                e.tensor_copy(sp0[:], cahi_bf[:])
                e.tensor_tensor(calo[:], cuma[:], sp0[:], ALU.subtract)
                e.tensor_copy(calo_r[:], calo[:])
                e.tensor_copy(lahi_bf[:], lnadj[:])
                e.tensor_copy(sp0[:], lahi_bf[:])
                e.tensor_tensor(lalo[:], lnadj[:], sp0[:], ALU.subtract)
                e.tensor_copy(lalo_r[:], lalo[:])
                e.memset(lhsc_hi[:], 0.0)
                e.memset(lhsc_lo[:].bitcast(F32), 0.0)
                lh = lhsc_hi[:].rearrange("k (p m) -> k p m", p=NPAIR)
                ll = lhsc_lo[:].rearrange("k (p m) -> k p m", p=NPAIR)
                la_hi = lahi_bf[:].rearrange("h (p eo s) -> h p eo s", p=NPAIR, eo=2)
                e.tensor_copy(lh[0:H, :, 0:CH], la_hi[:, :, 0, :])
                e.tensor_copy(lh[32:64, :, CH:128], la_hi[:, :, 1, :])
                la_lo = lalo_r[:].rearrange("h (p eo s) -> h p eo s", p=NPAIR, eo=2)
                e.tensor_copy(ll[0:H, :, 0:CH], la_lo[:, :, 0, :])
                e.tensor_copy(ll[32:64, :, CH:128], la_lo[:, :, 1, :])
                ho_v = hones_bf[:].unsqueeze(1).broadcast_to([2, NPAIR, 128])
                e.tensor_copy(lh[64:66, :, :], ho_v)
                ho_vr = hones_r[:].unsqueeze(1).broadcast_to([2, NPAIR, 128])
                e.tensor_copy(ll[64:66, :, :], ho_vr)
                e.memset(lhscv_hi[:], 0.0)
                e.memset(lhscv_lo[:].bitcast(F32), 0.0)
                e.tensor_copy(lhscv_hi[64:66, :], hones_bf[:])
                e.tensor_copy(lhscv_lo[64:66, :], hones_r[:])
                last_v = cuma[:].rearrange("h (c s) -> h c s", c=NCH)[:, :, CH - 1:CH]
                last_b = last_v.broadcast_to([H, NCH, CH])
                e.tensor_tensor(ddarg[:].rearrange("h (c s) -> h c s", c=NCH), last_b,
                                cuma[:].rearrange("h (c s) -> h c s", c=NCH), ALU.subtract)
                s_ev.inc(e.reduce_sum(tot_col[:], last_v.squeeze(2), AX.X))
                s_ac.wait(e, b[s_ac] + 3)
                s_ev.inc(e.tensor_tensor(ddt_fm[:], ddexp[:], dt_fm[:], ALU.mult))
                s_mm.wait(e, b[s_mm] + NPAIR + 2)
                e.tensor_copy(ddt_tm[:], ps_sm[:])
                e.tensor_copy(dectm[:], ps_sm2[0:NCH, 0:H])
                s_ev.inc(e.tensor_copy(totrow[:], ps_sm2[0:1, H:2 * H]))
                s_ld.wait(e, b[s_ld] + 48)
                s_ev.inc(e.tensor_copy(decrows[:], dr32[:]))

            @blk.tensor
            def _(e):
                kb.bar(e, b)
                s_ev.wait(e, b[s_ev] + 3)
                for p in range(NPAIR):
                    s_mm.inc(e.transpose(ps_sm[:, p * H:(p + 1) * H],
                                         ddt_fm[:, p * 128:(p + 1) * 128], eye32[0:H, 0:H]))
                s_mm.inc(e.transpose(ps_sm2[0:NCH, 0:H], decfm[:], eye32[0:H, 0:H]))
                s_mm.inc(e.transpose(ps_sm2[0:1, H:2 * H], tot_col[:], eye32[0:H, 0:H]))

            @blk.scalar
            def _(e):
                s_ev.wait(e, b[s_ev] + 4)
                s_st.dinc(e.dma_start(out=d_cumahi[:], in_=cahi_bf[:]))
                s_st.dinc(e.dma_start(out=d_cumalo[:], in_=calo_r[:]))
                s_st.dinc(e.dma_start(out=d_dec[:], in_=dectm[:]))
                if "d_dt" in kb.taps:
                    s_st.dinc(e.dma_start(out=d_dt[:], in_=dt_fm[:]))
                if "d_cuma" in kb.taps:
                    s_st.dinc(e.dma_start(out=d_cuma[:], in_=cuma[:]))

        if _STOP == "p2b":
            ctx_mix.close()
            finalize_dummy()
            for s in kb.sems:
                s.check()
            return nc

        with ExitStack() as pctx:
            lsb = lambda n, s, d: kb.lsb(pctx, n, s, d)
            cw_cols = lsb("cw_cols", [128, 17 * 4], F32)
            cb32 = lsb("cb32", [1, 128], F32)
            diag = lsb("diag", [128, 17 * 4 * 128], BF16)
            bcb_col = lsb("bcb_col", [128, 1], F32)
            strip = lsb("strip", [128, TTOK], BF16)
            acc0 = lsb("acc0", [128, TTOK], BF16)
            acc1 = lsb("acc1", [128, TTOK], BF16)
            bcs = lsb("bcs", [128, TTOK], BF16)
            xbct = [lsb("xbct%d" % i, [128, 17 * 131], BF16) for i in range(2)]
            zt = [lsb("zt%d" % i, [128, 16 * 128], BF16) for i in range(2)]
            xcv = [lsb("xcv%d" % i, [128, CONVD], BF16) for i in range(2)]
            szt = [lsb("szt%d" % i, [128, DIN], BF16) for i in range(2)]
            ps_cv = [kb.psum(pctx, "ps_cv%d" % i, [128, 512]) for i in range(2)]
            ps_z = kb.psum(pctx, "ps_z", [128, DIN], BF16)
            ps_bc = kb.psum(pctx, "ps_bc", [128, 512])
            b = kb.snap()
            NG = 5
            CUM = [20, 40, 60, 80, 85]
            MMT = 85 + 16
            ACT_T = 5 + 2
            PRE_MM = 1

            def evac_done(gg):
                t2, g2 = divmod(gg, NG)
                return b[s_ac] + 1 + t2 * ACT_T + g2 + 1

            @blk.sync
            def _(e):
                kb.bar(e, b)
                src = conv_w.rearrange("(fc p) k -> p fc k", p=128)
                s_ld.dinc(e.dma_start(out=cw_cols[:].rearrange("p (fc k) -> p fc k", fc=17), in_=src))
                s_ld.dinc(e.dma_start(out=strip[:], in_=d_xbc[DIN:CONVD, :]))
                for t in range(NPAIR):
                    if t >= 2:
                        s_mm.wait(e, b[s_mm] + PRE_MM + (t - 1) * MMT)
                    src = d_xbc.rearrange("(fc p) tt -> p fc tt", p=128)[:, :, t * 128 + 125:t * 128 + 256]
                    stld(e, 0, t, xbct[t % 2][:].rearrange("p (fc c) -> p fc c", fc=17), src)
                    srcz = d_z.rearrange("(zc p) tt -> p zc tt", p=128)[:, :, t * 128 + HALO:t * 128 + HALO + 128]
                    stld(e, 2, t, zt[t % 2][:].rearrange("p (zc c) -> p zc c", zc=16), srcz)

            @blk.tensor
            def _(e):
                kb.bar(e, b)
                s_ev.wait(e, b[s_ev] + 1)
                s_mm.inc(e.transpose(ps_bc[:, 0:1], cb32[0:1, :], eye32[0:1, 0:1]))
                for t in range(NPAIR):
                    stw(e, 0, t, b)
                    stw(e, 2, t, b)
                    s_ev.wait(e, b[s_ev] + 2)
                    xb = xbct[t % 2][:].rearrange("p (fc c) -> p fc c", fc=17)
                    for fcg in range(NG):
                        gg = t * NG + fcg
                        if gg >= 2:
                            s_ac.wait(e, evac_done(gg - 2))
                        fcs = range(fcg * 4, min(fcg * 4 + 4, 17))
                        for fc in fcs:
                            col = (fc % 4) * 128
                            for k in range(4):
                                s_mm.inc(e.matmul(ps_cv[fcg % 2][:, col:col + 128],
                                                  xb[:, fc, k:k + 128],
                                                  diag[:, (fc * 4 + k) * 128:(fc * 4 + k + 1) * 128],
                                                  start=(k == 0), stop=False))
                            s_mm.inc(e.matmul(ps_cv[fcg % 2][:, col:col + 128],
                                              ones_bf[:],
                                              convb_row[0:1, fc * 128:(fc + 1) * 128],
                                              start=False, stop=True))
                    ztv = zt[t % 2][:].rearrange("p (zc c) -> p zc c", zc=16)
                    if t >= 1:
                        s_ac.wait(e, b[s_ac] + 1 + (t - 1) * ACT_T + 7)
                    for zc in range(16):
                        s_mm.inc(e.transpose(ps_z[:, zc * 128:(zc + 1) * 128], ztv[:, zc, :], eyebf[:]))

            @blk.scalar
            def _(e):
                kb.bar(e, b)
                s_ev.wait(e, b[s_ev] + 3)
                s_ac.inc(e.activation(bcs[:], acc0[:], AF.Silu, bias=bcb_col[:]))
                for t in range(NPAIR):
                    if t >= 2:
                        stgw(e, 1, t - 2, b)
                        stgw(e, 3, t - 2, b)
                    for fcg in range(NG):
                        s_mm.wait(e, b[s_mm] + PRE_MM + t * MMT + CUM[fcg])
                        w = 512 if fcg < 4 else 128
                        s_ac.inc(e.activation(xcv[t % 2][:, fcg * 512:fcg * 512 + w],
                                              ps_cv[fcg % 2][:, 0:w], AF.Silu))
                    s_mm.wait(e, b[s_mm] + PRE_MM + t * MMT + 85 + 16)
                    s_ac.inc(e.activation(szt[t % 2][:, 0:1024], ps_z[:, 0:1024], AF.Silu))
                    s_ac.inc(e.activation(szt[t % 2][:, 1024:2048], ps_z[:, 1024:2048], AF.Silu))

            @blk.vector
            def _(e):
                kb.bar(e, b)
                s_ld.wait(e, b[s_ld] + 32)
                cwv = cw_cols[:].rearrange("p (fc k) -> p fc k", fc=17)
                s_ev.inc(e.tensor_copy(cb32[0:1, :], convb_row[0:1, DIN:CONVD]))
                for fc in range(17):
                    for k in range(4):
                        inst = e.tensor_scalar_mul(diag[:, (fc * 4 + k) * 128:(fc * 4 + k + 1) * 128],
                                                   eyebf[:], cwv[:, fc, k:k + 1])
                s_ev.inc(inst)
                s_mm.wait(e, b[s_mm] + 1)
                e.tensor_copy(bcb_col[:], ps_bc[:, 0:1])
                s_ld.wait(e, b[s_ld] + 32)
                W = TTOK - 3
                e.memset(acc0[:, 0:3], 0.0)
                e.memset(acc1[:, 0:3], 0.0)
                e.tensor_scalar_mul(acc1[:, 3:], strip[:, 0:W], cwv[:, 16, 0:1])
                e.scalar_tensor_tensor(acc0[:, 3:], strip[:, 1:W + 1], cwv[:, 16, 1:2],
                                       acc1[:, 3:], ALU.mult, ALU.add)
                e.scalar_tensor_tensor(acc1[:, 3:], strip[:, 2:W + 2], cwv[:, 16, 2:3],
                                       acc0[:, 3:], ALU.mult, ALU.add)
                s_ev.inc(e.scalar_tensor_tensor(acc0[:, 3:], strip[:, 3:W + 3], cwv[:, 16, 3:4],
                                                acc1[:, 3:], ALU.mult, ALU.add))
                s_ac.wait(e, b[s_ac] + 1)
                e.tensor_copy(B_fm[:], bcs[0:64, :])
                e.tensor_copy(C_fm[0:64, :], bcs[64:128, :])
                s_ev.inc(e.tensor_copy(C_fm[64:128, :], bcs[64:128, :]))

            @blk.gpsimd
            def _(e):
                kb.bar(e, b)
                for t in range(NPAIR):
                    s_ac.wait(e, b[s_ac] + 1 + t * ACT_T + 5)
                    stgld(e, 1, t, d_xconv[t * 128:(t + 1) * 128, :], xcv[t % 2][:])
                    s_ac.wait(e, b[s_ac] + 1 + (t + 1) * ACT_T)
                    stgld(e, 3, t, d_sz[t * 128:(t + 1) * 128, :], szt[t % 2][:])

        if _STOP == "p3":
            ctx_mix.close()
            finalize_dummy()
            for s in kb.sems:
                s.check()
            return nc

        def emit_states_pair(e_pe, p, xct, Bp, ps_cs, ps_dr, b, ev_of_pair, evs_per_pair):
            s_ev.wait(e_pe, ev_of_pair(p) + 1)
            s_mm.inc(e_pe.matmul(ps_dr[:, 0:H], hones_r[:], decrows[:, p * H:(p + 1) * H],
                                 start=True, stop=True))
            for half, base in ((0, 0), (1, 64)):
                for h in range(H):
                    s_mm.inc(e_pe.matmul(ps_cs[base:base + 64, h * 64:(h + 1) * 64],
                                         Bp[base:base + 64, h * 64:(h + 1) * 64],
                                         xct[base:base + 64, h * 64:(h + 1) * 64],
                                         start=True, stop=True))

        def emit_scan_pair(e_v, p, ps_cs, ps_dr, tmp_e, tmp_o):
            drv0 = ps_dr[0:64, 0:H].unsqueeze(2).broadcast_to([64, H, 64])
            drv1 = ps_dr[64:128, 0:H].unsqueeze(2).broadcast_to([64, H, 64])
            s3 = lambda ap: ap.rearrange("n (h pp) -> n h pp", h=H)
            s_ev.inc(e_v.tensor_tensor(s3(tmp_e[:]), s3(S_pp[0:64, :]), drv0, ALU.mult))
            s_ev.inc(e_v.tensor_tensor(S_pp[64:128, :], tmp_e[:], ps_cs[0:64, :], ALU.add))
            s_ev.inc(e_v.tensor_tensor(s3(tmp_o[:]), s3(S_pp[64:128, :]), drv1, ALU.mult))
            s_ev.inc(e_v.tensor_tensor(S_pp[0:64, :], tmp_o[:], ps_cs[64:128, :], ALU.add))

        with ExitStack() as pctx:
            lsb = lambda n, s, d: kb.lsb(pctx, n, s, d)
            xct = [lsb("xctA%d" % i, [128, CONVD], BF16) for i in range(2)]
            Bp = [lsb("BpA%d" % i, [128, H * NS], BF16) for i in range(2)]
            tmp_e = lsb("tmpAe", [64, H * PDIM], BF16)
            tmp_o = lsb("tmpAo", [64, H * PDIM], BF16)
            sk32 = lsb("sk32", [NS, H * PDIM], F32)
            ps_cs = kb.psum(pctx, "ps_csA", [128, H * PDIM])
            ps_dr = kb.psum(pctx, "ps_drA", [128, 512])
            b = kb.snap()
            EVP = 5
            MMP = 65

            def ev_of_pair(p):
                return b[s_ev] + p * EVP

            @blk.sync
            def _(e):
                kb.bar(e, b)
                for p in range(NPAIR):
                    if p >= 2:
                        s_ev.wait(e, ev_of_pair(p - 1))
                    stld(e, 0, p, xct[p % 2][:], d_xconv[p * 128:(p + 1) * 128, :])

            @blk.vector
            def _(e):
                kb.bar(e, b)
                e.memset(S_pp[:], 0.0)
                for p in range(NPAIR):
                    stw(e, 0, p, b)
                    bview = xct[p % 2][:, DIN:DIN + NS].unsqueeze(1).broadcast_to([128, H, NS])
                    dview = ddt_tm[:, p * H:(p + 1) * H].unsqueeze(2).broadcast_to([128, H, NS])
                    s_ev.inc(e.tensor_tensor(Bp[p % 2][:].rearrange("t (h n) -> t h n", h=H),
                                             bview, dview, ALU.mult))
                    s_mm.wait(e, b[s_mm] + (p + 1) * MMP)
                    emit_scan_pair(e, p, ps_cs, ps_dr, tmp_e, tmp_o)
                s_ev.inc(e.tensor_copy(sk32[:], S_pp[0:64, :]))

            @blk.tensor
            def _(e):
                kb.bar(e, b)
                for p in range(NPAIR):
                    if p >= 1:
                        s_ev.wait(e, ev_of_pair(p))
                    stw(e, 0, p, b)
                    emit_states_pair(e, p, xct[p % 2][:], Bp[p % 2][:], ps_cs, ps_dr,
                                     b, ev_of_pair, EVP)

            @blk.gpsimd
            def _(e):
                kb.bar(e, b)
                s_ev.wait(e, b[s_ev] + NPAIR * EVP + 1)
                s_wl.dinc(e.dma_start(out=d_ccin[0:NS, :], in_=sk32[:]))
                s_wl.dinc(e.dma_start(out=d_ccin[NS:NS + 1, 0:H], in_=totrow[:]))
                if "d_sk" in kb.taps:
                    s_wl.dinc(e.dma_start(out=d_sk[0:NS, :], in_=sk32[:]))
                    s_wl.dinc(e.dma_start(out=d_sk[NS:NS + 1, 0:H], in_=totrow[:]))

        if _STOP == "p4":
            ctx_mix.close()
            finalize_dummy()
            for s in kb.sems:
                s.check()
            return nc

        with ExitStack() as pctx:
            lsb = lambda n, s, d: kb.lsb(pctx, n, s, d)
            ttl = lsb("ttl", [1, NCORES * H], F32)
            mt = lsb("mt", [1, NCORES * H], F32)
            srow = lsb("srow", [1, 9 * H], F32)
            wrow = lsb("wrow", [1, 9 * H], F32)
            wm = lsb("wm", [1, 9 * H], F32)
            sgb = [lsb("sgb%d" % i, [NS, H * PDIM], F32) for i in range(2)]
            tb = lsb("tb", [NS, H * PDIM], F32)
            accb = [lsb("accb%d" % i, [NS, H * PDIM], F32) for i in range(2)]
            ps_w = kb.psum(pctx, "ps_w", [NS, 512])
            b = kb.snap()

            @blk.gpsimd
            def _(e):
                kb.bar(e, b)
                e.collective_compute(
                    "AllGather", ALU.bypass,
                    replica_groups=[list(range(NCORES))],
                    ins=[d_ccin.opt()],
                    outs=[d_ccout.opt()],
                ).then_inc(s_cc.h)
                s_cc.n += 1
                s_cc.wait(e)
                srct = d_ccout.rearrange("(j r) w -> j r w", r=65)[:, 64, 0:H].unsqueeze(0)
                s_wl.dinc(e.dma_start(out=ttl[:].rearrange("one (j h) -> one j h", j=NCORES),
                                      in_=srct))

            @blk.sync
            def _(e):
                kb.bar(e, b)
                s_cc.wait(e)
                for j in range(9):
                    if j >= 2:
                        s_ev.wait(e, b[s_ev] + 3 + (j - 1) * 2)
                    if j == 0:
                        stld(e, 0, j, sgb[0][:], init_nhp[:])
                    else:
                        stld(e, 0, j, sgb[j % 2][:],
                             d_ccout[65 * (j - 1):65 * (j - 1) + NS, :])

            @blk.vector
            def _(e):
                kb.bar(e, b)
                s_wl.wait(e, b[s_wl] + 16)
                m9v = m9[0:1, 1:9].unsqueeze(2).broadcast_to([1, 8, H])
                vE(e, e.tensor_tensor(mt[:].rearrange("one (j h) -> one j h", j=8),
                                      ttl[:].rearrange("one (j h) -> one j h", j=8),
                                      m9v, ALU.mult))
                vE(e, e.memset(srow[0:1, 8 * H:9 * H], 0.0))
                for j in range(6, -2, -1):
                    sl = (j + 1) * H
                    vE(e, e.tensor_tensor(srow[0:1, sl:sl + H], srow[0:1, sl + H:sl + 2 * H],
                                          mt[0:1, (j + 1) * H:(j + 2) * H] if j < 7 else mt[0:1, 7 * H:8 * H],
                                          ALU.add))
                s_ev.inc(e.nop())

            @blk.scalar
            def _(e):
                kb.bar(e, b)
                s_ev.wait(e, b[s_ev] + 1)
                s_ac.inc(e.activation(wrow[:], srow[:], AF.Exp))

            @blk.vector
            def _(e):
                s_ac.wait(e, b[s_ac] + 1)
                m9w = m9[0:1, 0:9].unsqueeze(2).broadcast_to([1, 9, H])
                s_ev.inc(e.tensor_tensor(wm[:].rearrange("one (j h) -> one j h", j=9),
                                         wrow[:].rearrange("one (j h) -> one j h", j=9),
                                         m9w, ALU.mult))
                for j in range(9):
                    stw(e, 0, j, b)
                    s_mm.wait(e, b[s_mm] + (j + 1))
                    wv = ps_w[:, 0:H].unsqueeze(2).broadcast_to([NS, H, 64])
                    s_ev.inc(e.tensor_tensor(tb[:].rearrange("n (h pp) -> n h pp", h=H),
                                             sgb[j % 2][:].rearrange("n (h pp) -> n h pp", h=H),
                                             wv, ALU.mult))
                    if j == 0:
                        s_ev.inc(e.tensor_copy(accb[1][:], tb[:]))
                    else:
                        s_ev.inc(e.tensor_tensor(accb[(j + 1) % 2][:], accb[j % 2][:], tb[:], ALU.add))
                s_ev.inc(e.tensor_copy(S_pp[0:64, :], accb[1][:]))

            @blk.tensor
            def _(e):
                kb.bar(e, b)
                s_ev.wait(e, b[s_ev] + 2)
                for j in range(9):
                    if j >= 1:
                        s_ev.wait(e, b[s_ev] + 3 + (j - 1) * 2 + 1)
                    s_mm.inc(e.matmul(ps_w[:, 0:H], ones64f[:], wm[0:1, j * H:(j + 1) * H],
                                      start=True, stop=True))

            @blk.gpsimd
            def _(e):
                if "d_sin" in kb.taps:
                    s_ev.wait(e, b[s_ev] + 3 + 18)
                    s_wl.dinc(e.dma_start(out=d_sin[:], in_=accb[1][:]))

        if _STOP == "p5":
            ctx_mix.close()
            finalize_dummy()
            for s in kb.sems:
                s.check()
            return nc

        with ExitStack() as pctx:
            lsb = lambda n, s, d: kb.lsb(pctx, n, s, d)
            xct = [lsb("xctB%d" % i, [128, CONVD], BF16) for i in range(2)]
            sztl = [lsb("sztB%d" % i, [128, DIN], BF16) for i in range(2)]
            utl = [lsb("utB%d" % i, [128, D], F32) for i in range(2)]
            comp_hi = [lsb("comph%d" % i, [66, H * CH], BF16) for i in range(2)]
            comp_lo = [lsb("compl%d" % i, [66, H * CH], F32R) for i in range(2)]
            wog = [lsb("wog%d" % i, [128, 4 * D], BF16) for i in range(2)]
            Bp = [lsb("BpB%d" % i, [128, H * NS], BF16) for i in range(2)]
            gd = lsb("gd", [128, 96], F32)
            expsR = lsb("expsR", [128, H * CH], BF16)
            expsV = lsb("expsV", [128, H * CH], BF16)
            rt0 = lsb("rt0", [128, H * CH], BF16)
            rtF = lsb("rtF", [128, H * CH], BF16)
            vtF = lsb("vtF", [128, H * CH], BF16)
            tmp_e = lsb("tmpBe", [64, H * PDIM], BF16)
            tmp_o = lsb("tmpBo", [64, H * PDIM], BF16)
            yg = lsb("yg", [128, DIN], BF16)
            sqscr = lsb("sqscr", [128, 1024], BF16)
            msc = lsb("msc", [128, 16], F32)
            yfm = lsb("yfm", [128, 16 * 128], BF16)
            xout = [lsb("xoutB%d" % i, [128, D], F32) for i in range(2)]
            ps_big = kb.psum(pctx, "ps_bigB", [128, 2048])
            ps_y0 = kb.psum(pctx, "ps_y0B", [128, 1024])
            ps_y1 = kb.psum(pctx, "ps_y1B", [128, 1024])
            ps_gt = ps_big[:, 0:64]
            ps_dr = ps_big[:, 64:96]
            ps_o0 = ps_big[:, 0:512]
            ps_o1 = ps_big[:, 512:1024]
            ps_yt0 = ps_big[:, 1024:1536].bitcast(BF16)
            ps_yt1 = ps_big[:, 1536:2048].bitcast(BF16)
            clampR = lsb("clampR", [128, H * CH], F32)
            b = kb.snap()
            MMP = 259
            EVP = 18
            ACP = 5
            WLP = 18 * 16

            def pb(p):
                return b[s_mm] + p * MMP

            def evb(p):
                return b[s_ev] + 1 + p * EVP

            def acb(p):
                return b[s_ac] + p * ACP

            @blk.sync
            def _(e):
                kb.bar(e, b)
                for p in range(NPAIR):
                    if p >= 2:
                        s_ev.wait(e, evb(p - 1))
                    stld(e, 0, p, xct[p % 2][:], d_xconv[p * 128:(p + 1) * 128, :])
                    stld(e, 2, p, sztl[p % 2][:], d_sz[p * 128:(p + 1) * 128, :])
                    stld(e, 4, p, utl[p % 2][:], u_pad[p * 128 + HALO:p * 128 + HALO + 128, :])

            @blk.gpsimd
            def _(e):
                kb.bar(e, b)
                for p in range(NPAIR):
                    if p >= 2:
                        s_mm.wait(e, pb(p - 2) + 19)
                    srch = d_cumahi.rearrange("h (c s) -> c h s", c=NCH)[2 * p:2 * p + 2, :, :]
                    stld(e, 5, p, comp_hi[p % 2][64:66, :].rearrange("k (h s) -> k h s", h=H), srch)
                    srcl = d_cumalo.rearrange("h (c s) -> c h s", c=NCH)[2 * p:2 * p + 2, :, :]
                    stld(e, 6, p, comp_lo[p % 2][64:66, :].rearrange("k (h s) -> k h s", h=H), srcl)
                    for g in range(4):
                        gg = p * 4 + g
                        if gg >= 2:
                            p2, g2 = divmod(gg - 2, 4)
                            s_mm.wait(e, pb(p2) + 237 + 8 * g2)
                        srcw = d_woutT.rearrange("(fc p2) d -> p2 fc d", p2=128)[:, 4 * g:4 * g + 4, :]
                        stld(e, 7, gg, wog[gg % 2][:].rearrange("p2 (fc d) -> p2 fc d", fc=4), srcw)

            @blk.vector
            def _(e):
                kb.bar(e, b)
                e.tensor_copy(comp_hi[0][0:64, :], ddel_bf[:])
                e.tensor_copy(comp_lo[0][0:64, :], ddel_r[:])
                e.tensor_copy(comp_hi[1][0:64, :], ddel_bf[:])
                s_ev.inc(e.tensor_copy(comp_lo[1][0:64, :], ddel_r[:]))
                for p in range(NPAIR):
                    s_mm.wait(e, pb(p) + 3)
                    e.scalar_tensor_tensor(gd[:, 0:64], ps_big[:, 0:64], 1.0, ltm2[:],
                                           ALU.mult, ALU.mult)
                    s_ev.inc(e.tensor_copy(gd[:, 64:96], ps_big[:, 64:96]))
                    stw(e, 0, p, b)
                    bview = xct[p % 2][:, DIN:DIN + NS].unsqueeze(1).broadcast_to([128, H, NS])
                    dview = ddt_tm[:, p * H:(p + 1) * H].unsqueeze(2).broadcast_to([128, H, NS])
                    s_ev.inc(e.tensor_tensor(Bp[p % 2][:].rearrange("t (h n) -> t h n", h=H),
                                             bview, dview, ALU.mult))
                    s_mm.wait(e, pb(p) + 11)
                    s_ev.inc(e.tensor_scalar(clampR[:], ps_big[:], 30.0, None,
                                             ALU.min))
                    s_ac.wait(e, acb(p) + 1)
                    gtv = gd[:, 0:64].unsqueeze(1).broadcast_to([128, H, CH])
                    s_ev.inc(e.tensor_tensor(rt0[:].rearrange("s (h l) -> s h l", h=H),
                                             expsR[:].rearrange("s (h l) -> s h l", h=H),
                                             gtv, ALU.mult))
                    s_ev.inc(e.tensor_tensor(rtF[:].rearrange("s (h l) -> s h l", h=H),
                                             rt0[:].rearrange("s (h l) -> s h l", h=H),
                                             Ddiag[:].rearrange("s (h l) -> s h l", h=H),
                                             ALU.add))
                    s_ac.wait(e, acb(p) + 2)
                    cve = C_fm[0:64, HALO + 128 * p:HALO + 128 * p + CH].unsqueeze(1).broadcast_to([NS, H, CH])
                    cvo = C_fm[64:128, HALO + 128 * p + CH:HALO + 128 * p + 2 * CH].unsqueeze(1).broadcast_to([NS, H, CH])
                    s_ev.inc(e.tensor_tensor(vtF[0:64, :].rearrange("s (h l) -> s h l", h=H),
                                             expsV[0:64, :].rearrange("s (h l) -> s h l", h=H),
                                             cve, ALU.mult))
                    s_ev.inc(e.tensor_tensor(vtF[64:128, :].rearrange("s (h l) -> s h l", h=H),
                                             expsV[64:128, :].rearrange("s (h l) -> s h l", h=H),
                                             cvo, ALU.mult))
                    s_mm.wait(e, pb(p) + 83)
                    drv0 = gd[0:64, 64:96].unsqueeze(2).broadcast_to([64, H, 64])
                    drv1 = gd[64:128, 64:96].unsqueeze(2).broadcast_to([64, H, 64])
                    s3 = lambda ap: ap.rearrange("n (h pp) -> n h pp", h=H)
                    s_ev.inc(e.tensor_tensor(s3(tmp_e[:]), s3(S_pp[0:64, :]), drv0, ALU.mult))
                    s_ev.inc(e.tensor_tensor(S_pp[64:128, :], tmp_e[:], ps_big[0:64, :], ALU.add))
                    s_ev.inc(e.tensor_tensor(s3(tmp_o[:]), s3(S_pp[64:128, :]), drv1, ALU.mult))
                    s_mm.wait(e, pb(p) + 147)
                    s_ev.inc(e.tensor_tensor(S_pp[0:64, :], tmp_o[:], ps_big[64:128, :], ALU.add))
                    s_mm.wait(e, pb(p) + 179)
                    stw(e, 2, p, b)
                    s_ev.inc(e.tensor_tensor(yg[:, 0:1024], ps_y0[:], sztl[p % 2][:, 0:1024], ALU.mult))
                    s_mm.wait(e, pb(p) + 211)
                    s_ev.inc(e.tensor_tensor(yg[:, 1024:2048], ps_y1[:], sztl[p % 2][:, 1024:2048], ALU.mult))
                    s_ac.wait(e, acb(p) + 4)
                    vE(e, e.tensor_tensor(msc[:, 2:3], msc[:, 0:1], msc[:, 1:2], ALU.add))
                    s_ev.inc(e.nop())
                    s_mm.wait(e, pb(p) + 219)
                    s_ev.inc(e.tensor_copy(yfm[:, 0:1024], ps_yt0[:]))
                    s_mm.wait(e, pb(p) + 227)
                    s_ev.inc(e.tensor_copy(yfm[:, 1024:2048], ps_yt1[:]))
                    s_mm.wait(e, pb(p) + 259)
                    s_ac.wait(e, acb(p) + 5)
                    stw(e, 4, p, b)
                    if p >= 2:
                        stw(e, 1, p - 2, b)
                    s_ev.inc(e.scalar_tensor_tensor(xout[p % 2][:, 0:512], ps_o0, msc[:, 4:5],
                                                    utl[p % 2][:, 0:512], ALU.mult, ALU.add))
                    s_ev.inc(e.scalar_tensor_tensor(xout[p % 2][:, 512:1024], ps_o1, msc[:, 4:5],
                                                    utl[p % 2][:, 512:1024], ALU.mult, ALU.add))

            @blk.tensor
            def _(e):
                kb.bar(e, b)
                s_ev.wait(e, b[s_ev] + 1)
                for p in range(NPAIR):
                    s_ev.wait(e, evb(p))
                    stw(e, 0, p, b)
                    stw(e, 5, p, b)
                    stw(e, 6, p, b)
                    ec = HALO + 128 * p
                    s_mm.inc(e.matmul(ps_gt[0:64, :], B_fm[:, ec:ec + CH], C_fm[0:64, ec:ec + CH],
                                      start=True, stop=True))
                    s_mm.inc(e.matmul(ps_gt[64:128, :], B_fm[:, ec + CH:ec + 2 * CH],
                                      C_fm[0:64, ec + CH:ec + 2 * CH], start=True, stop=True))
                    s_mm.inc(e.matmul(ps_dr[:, 0:H], hones_r[:], decrows[:, p * H:(p + 1) * H],
                                      start=True, stop=True))
                    s_ev.wait(e, evb(p) + 1)
                    for j in range(4):
                        s_mm.inc(e.matmul(ps_big[:, j * 512:(j + 1) * 512],
                                          lhsc_hi[:, p * 128:(p + 1) * 128],
                                          comp_hi[p % 2][:, j * 512:(j + 1) * 512],
                                          start=True, stop=False))
                        s_mm.inc(e.matmul(ps_big[:, j * 512:(j + 1) * 512],
                                          lhsc_lo[:, p * 128:(p + 1) * 128],
                                          comp_lo[p % 2][:, j * 512:(j + 1) * 512],
                                          start=False, stop=True))
                    s_ac.wait(e, acb(p) + 1)
                    for j in range(4):
                        s_mm.inc(e.matmul(ps_big[:, j * 512:(j + 1) * 512],
                                          lhscv_hi[:], comp_hi[p % 2][:, j * 512:(j + 1) * 512],
                                          start=True, stop=False))
                        s_mm.inc(e.matmul(ps_big[:, j * 512:(j + 1) * 512],
                                          lhscv_lo[:], comp_lo[p % 2][:, j * 512:(j + 1) * 512],
                                          start=False, stop=True))
                    s_ac.wait(e, acb(p) + 2)
                    s_ev.wait(e, evb(p) + 2)
                    for base in (0, 64):
                        for h in range(H):
                            s_mm.inc(e.matmul(ps_big[base:base + 64, h * 64:(h + 1) * 64],
                                              Bp[p % 2][base:base + 64, h * 64:(h + 1) * 64],
                                              xct[p % 2][base:base + 64, h * 64:(h + 1) * 64],
                                              start=True, stop=True))
                    s_ev.wait(e, evb(p) + 7)
                    for base, psn in ((0, None), (64, None)):
                        if base == 64:
                            s_ev.wait(e, evb(p) + 9)
                        for hg in range(2):
                            psy = ps_y0 if hg == 0 else ps_y1
                            for hh in range(16):
                                h = hg * 16 + hh
                                s_mm.inc(e.matmul(psy[base:base + 64, hh * 64:(hh + 1) * 64],
                                                  rtF[base:base + 64, h * 64:(h + 1) * 64],
                                                  xct[p % 2][base:base + 64, h * 64:(h + 1) * 64],
                                                  start=True, stop=False))
                                s_mm.inc(e.matmul(psy[base:base + 64, hh * 64:(hh + 1) * 64],
                                                  vtF[base:base + 64, h * 64:(h + 1) * 64],
                                                  S_pp[base:base + 64, h * 64:(h + 1) * 64],
                                                  start=False, stop=True))
                    s_ev.wait(e, evb(p) + 13)
                    for fc in range(8):
                        s_mm.inc(e.transpose(ps_yt0[:, fc * 128:(fc + 1) * 128],
                                             yg[:, fc * 128:(fc + 1) * 128], eyebf[:]))
                    for fc in range(8):
                        s_mm.inc(e.transpose(ps_yt1[:, fc * 128:(fc + 1) * 128],
                                             yg[:, (8 + fc) * 128:(9 + fc) * 128], eyebf[:]))
                    s_ev.wait(e, evb(p) + 16)
                    for fc in range(16):
                        gg = p * 4 + fc // 4
                        if fc % 4 == 0:
                            stw(e, 7, gg, b)
                        wslice = wog[gg % 2][:, (fc % 4) * D:(fc % 4) * D + D]
                        s_mm.inc(e.matmul(ps_o0, yfm[:, fc * 128:(fc + 1) * 128],
                                          wslice[:, 0:512],
                                          start=(fc == 0), stop=(fc == 15)))
                        s_mm.inc(e.matmul(ps_o1, yfm[:, fc * 128:(fc + 1) * 128],
                                          wslice[:, 512:1024],
                                          start=(fc == 0), stop=(fc == 15)))

            @blk.scalar
            def _(e):
                kb.bar(e, b)
                for p in range(NPAIR):
                    s_ev.wait(e, evb(p) + 3)
                    s_ac.inc(e.activation(expsR[:], clampR[:], AF.Exp))
                    s_mm.wait(e, pb(p) + 19)
                    s_ac.inc(e.activation(expsV[:], ps_big[:], AF.Exp))
                    s_ev.wait(e, evb(p) + 12)
                    s_ac.inc(e.activation(sqscr[:], yg[:, 0:1024], AF.Square, accum_out=msc[:, 0:1]))
                    s_ev.wait(e, evb(p) + 13)
                    s_ac.inc(e.activation(sqscr[:], yg[:, 1024:2048], AF.Square, accum_out=msc[:, 1:2]))
                    s_ev.wait(e, evb(p) + 14)
                    sE(e, e.activation(msc[:, 3:4], msc[:, 2:3], AF.Ln, bias=EPS, scale=1.0 / DIN))
                    s_ac.inc(e.activation(msc[:, 4:5], msc[:, 3:4], AF.Exp, scale=-0.5))
                    s_ev.wait(e, evb(p) + 18)
                    stld(e, 1, p, d_x[p * 128:(p + 1) * 128, :], xout[p % 2][:])
                    if "d_yg" in kb.taps:
                        s_st.dinc(e.dma_start(out=d_yg[p * 128:(p + 1) * 128, :], in_=yg[:]))
                    if "d_yfm" in kb.taps:
                        s_st.dinc(e.dma_start(out=d_yfm[p * 128:(p + 1) * 128, :], in_=yfm[:]))
                    if "d_ms" in kb.taps:
                        s_st.dinc(e.dma_start(out=d_ms[p * 128:(p + 1) * 128, :], in_=msc[:]))

        if _STOP == "p6":
            ctx_mix.close()
            finalize_dummy()
            for s in kb.sems:
                s.check()
            return nc

        ctx_mix.close()

        with ExitStack() as pctx:
            lsb = lambda n, s, d: kb.lsb(pctx, n, s, d)
            xtl = [lsb("xtl%d" % i, [128, D], F32) for i in range(2)]
            xtf = [lsb("xtf%d" % i, [128, D], F32) for i in range(2)]
            lnb = lsb("lnb", [128, 8], F32)
            lnscr = lsb("lnscr", [128, D], F32)
            xn2_tm = lsb("xn2_tm", [128, D], BF16)
            xn2_fm = lsb("xn2_fm", [128, 8 * 1024], BF16)
            pr0 = lsb("pr0", [128, 8], F32)
            prs = lsb("prs", [128, 64], F32)
            probs_tm = lsb("probs_tm", [128, 2 * 64], F32)
            gsl = [lsb("gsl%d" % i, [128, 8 * 512], BF16) for i in range(2)]
            usl = [lsb("usl%d" % i, [128, 8 * 512], BF16) for i in range(2)]
            dwn = lsb("dwn", [128, 16 * D], BF16)
            sg = [lsb("sg%d" % i, [128, 1024], BF16) for i in range(2)]
            hbuf = lsb("hbuf", [128, 16 * 1024], BF16)
            ybuf = [lsb("ybuf%d" % i, [128, 8 * 1024], BF16) for i in range(2)]
            xo = [lsb("xo%d" % i, [128, D], F32) for i in range(2)]
            donb = lsb("donb", [1, NE * D], BF16)
            psA = kb.psum(pctx, "psA7", [128, 1024])
            psB = kb.psum(pctx, "psB7", [128, 1024])
            psC = kb.psum(pctx, "psC7", [128, 1024])
            ps_sm7 = kb.psum(pctx, "ps_sm7", [128, 512])
            ps_lnf = kb.psum(pctx, "ps_ln7", [128, 512])
            ps_ln = ps_lnf[:].bitcast(BF16)
            slots = [psA, psB, psC]
            b = kb.snap()
            MM_TT = 64 + 64 + 8 * 784
            EV_TT = 24 + 32 + 8 * 24 + 8
            AC_TT = 16 + 16 + 8 * 16
            NC7 = 16

            def m7(tt):
                return b[s_mm] + tt * MM_TT

            def e7(tt):
                return b[s_ev] + tt * EV_TT

            def a7(tt):
                return b[s_ac] + tt * AC_TT

            def ebm(tt, ei):
                return m7(tt) + 128 + 784 * ei

            def ebe(tt, ei):
                return e7(tt) + 56 + 24 * ei

            def eba(tt, ei):
                return a7(tt) + 32 + 16 * ei

            @blk.sync
            def _(e):
                kb.bar(e, b)
                for tt in range(2):
                    for ts in range(8):
                        k = tt * 8 + ts
                        if k >= 2:
                            t2, s2 = divmod(k - 2, 8)
                            s_ev.wait(e, e7(t2) + 3 * s2 + 2)
                            s_ac.wait(e, a7(t2) + 2 * s2 + 1)
                        stld(e, 0, k, xtl[k % 2][:], d_x[k * 128:(k + 1) * 128, :])
                    for ts in range(8):
                        k = tt * 8 + ts
                        if k >= 2:
                            t2, s2 = divmod(k - 2, 8)
                            s_ev.wait(e, e7(t2) + 248 + s2 + 1)
                        stld(e, 6, k, xtf[k % 2][:], d_x[k * 128:(k + 1) * 128, :])

            @blk.gpsimd
            def _(e):
                kb.bar(e, b)
                s_c2.dinc(e.dma_start(out=donb[:], in_=down_b[:]))
                gwv = gate_w.rearrange("ee (dc p) f -> ee p dc f", p=128)
                uwv = up_w.rearrange("ee (dc p) f -> ee p dc f", p=128)
                dwv = down_w.rearrange("ee (fc p) dd -> ee p fc dd", p=128)
                for tt in range(2):
                    for ei in range(NE):
                        di = tt * NE + ei
                        for fs in range(4):
                            gi = di * 4 + fs
                            if gi >= 2:
                                p2, f2 = divmod(gi - 2, 4)
                                t2, e2 = divmod(p2, NE)
                                s_mm.wait(e, ebm(t2, e2) + 128 * f2 + 128)
                            src = gwv[ei:ei + 1, :, :, fs * 512:(fs + 1) * 512].squeeze(0)
                            stgld(e, 2, gi, gsl[gi % 2][:].rearrange("p (dc f) -> p dc f", dc=8), src)
                            src = uwv[ei:ei + 1, :, :, fs * 512:(fs + 1) * 512].squeeze(0)
                            stgld(e, 4, gi, usl[gi % 2][:].rearrange("p (dc f) -> p dc f", dc=8), src)
                        if di >= 1:
                            t2, e2 = divmod(di - 1, NE)
                            s_mm.wait(e, ebm(t2, e2) + 784)
                        src = dwv[ei:ei + 1].squeeze(0)
                        stld(e, 5, di, dwn[:].rearrange("p (fc dd) -> p fc dd", fc=16), src)

            @blk.vector
            def _(e):
                kb.bar(e, b)
                for tt in range(2):
                    for ts in range(8):
                        k = tt * 8 + ts
                        stw(e, 0, k, b)
                        xb = xtl[k % 2]
                        vE(e, e.reduce_sum(lnb[:, 0:1], xb[:], AX.X))
                        s_ac.wait(e, a7(tt) + 2 * ts + 1)
                        vE(e, e.tensor_scalar_mul(lnb[:, 1:2], lnb[:, 0:1], 1.0 / D))
                        vE(e, e.tensor_scalar_mul(lnb[:, 5:6], lnb[:, 4:5], 1.0 / D))
                        vE(e, e.tensor_tensor(lnb[:, 2:3], lnb[:, 1:2], lnb[:, 1:2], ALU.mult))
                        s_ev.inc(e.tensor_tensor(lnb[:, 3:4], lnb[:, 5:6], lnb[:, 2:3], ALU.subtract))
                        s_ev.wait(e)
                        s_ac.wait(e, a7(tt) + 2 * ts + 2)
                        s_ev.inc(e.tensor_scalar(xn2_tm[:], xb[:], lnb[:, 1:2], lnb[:, 7:8],
                                                 ALU.subtract, ALU.mult))
                        s_mm.wait(e, m7(tt) + 8 * (ts + 1))
                        dst = xn2_fm[:].rearrange("p (dc t2) -> p dc t2", dc=8)
                        pv = ps_ln.rearrange("p (dc t2) -> p dc t2", dc=8)
                        s_ev.inc(e.tensor_copy(dst[:, :, ts * 128:(ts + 1) * 128], pv))
                    for ts in range(8):
                        pr = ps_sm7[:, (ts % 2) * 8:(ts % 2) * 8 + 8]
                        s_mm.wait(e, m7(tt) + 64 + 8 * ts + 8)
                        vE(e, e.reduce_max(prs[:, 0:1], pr, AX.X))
                        s_ev.inc(e.tensor_scalar_mul(prs[:, 1:2], prs[:, 0:1], -1.0))
                        s_ac.wait(e, a7(tt) + 16 + 2 * ts + 1)
                        vE(e, e.reduce_max(prs[:, 4:5], pr0[:], AX.X))
                        vE(e, e.tensor_scalar(prs[:, 16:24], pr0[:], prs[:, 4:5], None, ALU.is_lt))
                        vE(e, e.tensor_tensor(prs[:, 24:32], prs[:, 16:24], pr0[:], ALU.mult))
                        vE(e, e.reduce_max(prs[:, 5:6], prs[:, 24:32], AX.X))
                        vE(e, e.tensor_scalar(prs[:, 32:40], pr0[:], prs[:, 5:6], None, ALU.is_ge))
                        vE(e, e.tensor_tensor(prs[:, 40:48], prs[:, 32:40], pr0[:], ALU.mult))
                        vE(e, e.reduce_sum(prs[:, 6:7], prs[:, 40:48], AX.X))
                        vE(e, e.tensor_scalar_mul(prs[:, 56:57], prs[:, 2:3], 1e-9))
                        s_ev.inc(e.tensor_tensor(prs[:, 7:8], prs[:, 6:7], prs[:, 56:57], ALU.add))
                        s_ac.wait(e, a7(tt) + 16 + 2 * ts + 2)
                        s_ev.inc(e.tensor_scalar_mul(prs[:, 48:56], prs[:, 40:48], prs[:, 3:4]))
                        s_ev.wait(e)
                        s_ev.inc(e.tensor_copy(probs_tm[:, tt * 64 + ts * 8:tt * 64 + ts * 8 + 8],
                                               prs[:, 48:56]))
                    for ei in range(NE):
                        for kk in range(16):
                            s_mm.wait(e, ebm(tt, ei) + 32 * kk + 32)
                            s_ac.wait(e, eba(tt, ei) + kk + 1)
                            ups = slots[(2 * kk + 1) % 3]
                            s_ev.inc(e.scalar_tensor_tensor(
                                hbuf[:, kk * 1024:(kk + 1) * 1024], ups[:],
                                ub_cols[:, ei * 16 + kk:ei * 16 + kk + 1],
                                sg[kk % 2][:], ALU.add, ALU.mult))
                        for ts in range(8):
                            s_mm.wait(e, ebm(tt, ei) + 512 + 34 * (ts + 1))
                            if ei == 0 and ts == 0:
                                s_ev.wait(e, e7(tt) + 56)
                            ysl = slice(ts * 1024, (ts + 1) * 1024)
                            pcol = probs_tm[:, tt * 64 + ts * 8 + ei:tt * 64 + ts * 8 + ei + 1]
                            yps = slots[ts % 2]
                            if ei == 0:
                                s_ev.inc(e.tensor_scalar_mul(ybuf[1][:, ysl], yps[:], pcol))
                            else:
                                s_ev.inc(e.scalar_tensor_tensor(ybuf[(ei + 1) % 2][:, ysl], yps[:],
                                                                pcol, ybuf[ei % 2][:, ysl],
                                                                ALU.mult, ALU.add))
                    for ts in range(8):
                        k = tt * 8 + ts
                        stw(e, 6, k, b)
                        if k >= 2:
                            stw(e, 1, k - 2, b)
                        s_ev.inc(e.tensor_tensor(xo[k % 2][:], xtf[k % 2][:],
                                                 ybuf[0][:, ts * 1024:(ts + 1) * 1024], ALU.add))

            @blk.scalar
            def _(e):
                kb.bar(e, b)
                s_c2.wait(e, b[s_c2] + NC7)
                for tt in range(2):
                    for ts in range(8):
                        k = tt * 8 + ts
                        stw(e, 0, k, b)
                        xb = xtl[k % 2]
                        s_ac.inc(e.activation(lnscr[:], xb[:], AF.Square, accum_out=lnb[:, 4:5]))
                        s_ev.wait(e, e7(tt) + 3 * ts + 1)
                        sE(e, e.activation(lnb[:, 6:7], lnb[:, 3:4], AF.Ln, bias=EPS))
                        s_ac.inc(e.activation(lnb[:, 7:8], lnb[:, 6:7], AF.Exp, scale=-0.5))
                    for ts in range(8):
                        pr = ps_sm7[:, (ts % 2) * 8:(ts % 2) * 8 + 8]
                        s_ev.wait(e, e7(tt) + 24 + 4 * ts + 1)
                        s_ac.inc(e.activation(pr0[:], pr, AF.Exp, bias=prs[:, 1:2],
                                              accum_out=prs[:, 2:3]))
                        s_ev.wait(e, e7(tt) + 24 + 4 * ts + 2)
                        sE(e, e.activation(prs[:, 57:58], prs[:, 7:8], AF.Ln))
                        s_ac.inc(e.activation(prs[:, 3:4], prs[:, 57:58], AF.Exp, scale=-1.0))
                    for ei in range(NE):
                        for kk in range(16):
                            s_mm.wait(e, ebm(tt, ei) + 32 * kk + 16)
                            if kk < 2:
                                if ei > 0:
                                    s_ev.wait(e, ebe(tt, ei - 1) + 15 + kk)
                                elif tt > 0:
                                    s_ev.wait(e, ebe(tt - 1, NE - 1) + 15 + kk)
                            else:
                                s_ev.wait(e, ebe(tt, ei) + kk - 1)
                            gps = slots[(2 * kk) % 3]
                            s_ac.inc(e.activation(sg[kk % 2][:], gps[:], AF.Silu,
                                                  bias=gb_cols[:, ei * 16 + kk:ei * 16 + kk + 1]))
                    for ts in range(8):
                        k = tt * 8 + ts
                        s_ev.wait(e, e7(tt) + 248 + ts + 1)
                        stld(e, 1, k, out[k * 128:(k + 1) * 128, :], xo[k % 2][:])

            @blk.tensor
            def _(e):
                kb.bar(e, b)
                s_c2.wait(e, b[s_c2] + NC7)
                for tt in range(2):
                    for ts in range(8):
                        s_ev.wait(e, e7(tt) + 3 * ts + 2)
                        if ts >= 1:
                            s_ev.wait(e, e7(tt) + 3 * ts)
                        for dc in range(8):
                            s_mm.inc(e.transpose(ps_ln[:, dc * 128:(dc + 1) * 128],
                                                 xn2_tm[:, dc * 128:(dc + 1) * 128], eyebf[:]))
                    for ts in range(8):
                        s_ev.wait(e, e7(tt) + 3 * ts + 3)
                        if ts >= 2:
                            s_ac.wait(e, a7(tt) + 16 + 2 * ts - 3)
                        pr = ps_sm7[:, (ts % 2) * 8:(ts % 2) * 8 + 8]
                        for dc in range(8):
                            s_mm.inc(e.matmul(pr,
                                              xn2_fm[:, dc * 1024 + ts * 128:dc * 1024 + (ts + 1) * 128],
                                              wrT[:, dc * NE:(dc + 1) * NE],
                                              start=(dc == 0), stop=(dc == 7)))
                    for ei in range(NE):
                        if ei > 0:
                            s_ev.wait(e, ebe(tt, ei - 1) + 24)
                        elif tt > 0:
                            s_ev.wait(e, ebe(tt - 1, NE - 1) + 24)
                        for fs in range(4):
                            gi = (tt * NE + ei) * 4 + fs
                            stgw(e, 2, gi, b)
                            stgw(e, 4, gi, b)
                            for fb in range(4):
                                kk = fs * 4 + fb
                                if kk >= 2:
                                    s_ev.wait(e, ebe(tt, ei) + kk - 1)
                                gps = slots[(2 * kk) % 3]
                                for dc in range(8):
                                    for nn in range(2):
                                        s_mm.inc(e.matmul(
                                            gps[:, nn * 512:(nn + 1) * 512],
                                            gsl[gi % 2][:, dc * 512 + fb * 128:dc * 512 + (fb + 1) * 128],
                                            xn2_fm[:, dc * 1024 + nn * 512:dc * 1024 + (nn + 1) * 512],
                                            start=(dc == 0), stop=(dc == 7)))
                                if kk >= 1:
                                    s_ac.wait(e, eba(tt, ei) + kk)
                                ups = slots[(2 * kk + 1) % 3]
                                for dc in range(8):
                                    for nn in range(2):
                                        s_mm.inc(e.matmul(
                                            ups[:, nn * 512:(nn + 1) * 512],
                                            usl[gi % 2][:, dc * 512 + fb * 128:dc * 512 + (fb + 1) * 128],
                                            xn2_fm[:, dc * 1024 + nn * 512:dc * 1024 + (nn + 1) * 512],
                                            start=(dc == 0), stop=(dc == 7)))
                        di = tt * NE + ei
                        stw(e, 5, di, b)
                        s_ev.wait(e, ebe(tt, ei) + 16)
                        s_ac.wait(e, eba(tt, ei) + 16)
                        for ts in range(8):
                            if ts >= 2:
                                s_ev.wait(e, ebe(tt, ei) + 16 + ts - 1)
                            yps = slots[ts % 2]
                            s_mm.inc(e.matmul(yps[:, 0:512], ones_bf[0:1, :],
                                              donb[0:1, ei * D:ei * D + 512],
                                              start=True, stop=False))
                            s_mm.inc(e.matmul(yps[:, 512:1024], ones_bf[0:1, :],
                                              donb[0:1, ei * D + 512:(ei + 1) * D],
                                              start=True, stop=False))
                            for fc in range(16):
                                s_mm.inc(e.matmul(yps[:, 0:512],
                                                  hbuf[:, fc * 1024 + ts * 128:fc * 1024 + (ts + 1) * 128],
                                                  dwn[:, fc * D:fc * D + 512],
                                                  start=False, stop=(fc == 15)))
                                s_mm.inc(e.matmul(yps[:, 512:1024],
                                                  hbuf[:, fc * 1024 + ts * 128:fc * 1024 + (ts + 1) * 128],
                                                  dwn[:, fc * D + 512:(fc + 1) * D],
                                                  start=False, stop=(fc == 15)))

        sn = kb.snap()

        @blk.sync
        def _(e):
            kb.bar(e, sn)

        for s in kb.sems:
            s.check()
    return nc


def _consts(core):
    c = {}
    c["eye_f"] = np.eye(128, dtype=np.float32)
    c["eye_bf"] = np.eye(128, dtype=ml_dtypes.bfloat16)
    ltm = np.triu(np.ones((CH, CH), np.float32))
    c["ltm2"] = np.concatenate([ltm, ltm], axis=0)
    dd = np.zeros((64, H * CH), np.float32)
    for h2 in range(H):
        dd[h2, h2 * CH:(h2 + 1) * CH] = 1.0
        dd[32 + h2, h2 * CH:(h2 + 1) * CH] = 1.0
    c["deltadelta"] = dd
    ho = np.zeros((2, 128), np.float32)
    ho[0, 0:64] = 1.0
    ho[1, 64:128] = 1.0
    c["halfones"] = ho
    m9 = np.zeros((1, 9), np.float32)
    m9[0, 0] = 1.0
    m9[0, 1:1 + core] = 1.0
    c["m9"] = m9
    s8 = np.zeros((NE, NE * 128), np.float32)
    for ei in range(NE):
        s8[ei, ei * 128:(ei + 1) * 128] = 1.0
    c["sel8"] = s8
    c["ones_row"] = np.ones((1, 128), np.float32)
    return c


def make_inmaps(inputs):
    u = np.asarray(inputs["L0_attn_u"], np.float32).reshape(16384, 1024)
    shared = {
        "w_in": np.ascontiguousarray(np.asarray(inputs["L0_attn_in_proj_w"], np.float32)),
        "conv_w": np.ascontiguousarray(np.asarray(inputs["L0_attn_conv_w"], np.float32)),
        "conv_b": np.asarray(inputs["L0_attn_conv_b"], np.float32).reshape(1, CONVD),
        "init_nhp": np.ascontiguousarray(
            np.transpose(np.asarray(inputs["L0_attn_init_states"], np.float32), (2, 0, 1)).reshape(NS, H * PDIM)),
        "dt_bias": np.asarray(inputs["L0_attn_dt_bias"], np.float32).reshape(H, 1),
        "a_log": np.asarray(inputs["L0_attn_A_log"], np.float32).reshape(H, 1),
        "d_head": np.asarray(inputs["L0_attn_D"], np.float32).reshape(1, H),
        "norm_w": np.asarray(inputs["L0_attn_norm_w"], np.float32).reshape(1, DIN),
        "w_out": np.ascontiguousarray(np.asarray(inputs["L0_attn_out_proj_w"], np.float32)),
        "router_w": np.ascontiguousarray(np.asarray(inputs["L0_ffn_router_w"], np.float32)),
        "gate_w": np.ascontiguousarray(np.asarray(inputs["L0_ffn_gate_w"], np.float32)),
        "gate_b": np.asarray(inputs["L0_ffn_gate_b"], np.float32).reshape(1, NE * DFF),
        "up_w": np.ascontiguousarray(np.asarray(inputs["L0_ffn_up_w"], np.float32)),
        "up_b": np.asarray(inputs["L0_ffn_up_b"], np.float32).reshape(1, NE * DFF),
        "down_w": np.ascontiguousarray(np.asarray(inputs["L0_ffn_down_w"], np.float32)),
        "down_b": np.asarray(inputs["L0_ffn_down_b"], np.float32).reshape(1, NE * D),
    }
    in_maps = []
    for k in range(NCORES):
        m = dict(shared)
        if k == 0:
            up = np.concatenate([np.zeros((HALO, D), np.float32), u[0:T]], axis=0)
        else:
            up = u[k * T - HALO:k * T + T]
        m["u_pad"] = np.ascontiguousarray(up)
        m.update(_consts(k))
        in_maps.append(m)
    return in_maps


def kernel(**inputs):
    in_maps = make_inmaps(inputs)
    nc = build_program(taps=_TAPS)
    res = run_bass_kernel_spmd(nc, in_maps, core_ids=list(range(NCORES)))
    outs = [r["out"] for r in res.results]
    full = np.concatenate(outs, axis=0).reshape(1, 16384, 1024)
    if _TAPS:
        kernel.tap_results = res.results
    return full


# version 03
# speedup vs baseline: 1.0584x; 1.0036x over previous
import numpy as np
import ml_dtypes
from contextlib import ExitStack
import concourse.bass as bass
import concourse.mybir as mybir
from concourse.bass_utils import run_bass_kernel_spmd

F32 = mybir.dt.float32
BF16 = mybir.dt.bfloat16
F32R = mybir.dt.float32r
AF = mybir.ActivationFunctionType
ALU = mybir.AluOpType
AX = mybir.AxisListType

NCORES = 8
T = 2048
HALO = 128
TTOK = T + HALO
D = 1024
DIN = 2048
CONVD = 2176
DPROJ = 4256
H = 32
PDIM = 64
NS = 64
CH = 64
NCH = T // CH
NPAIR = NCH // 2
NT = TTOK // 128
DFF = 2048
NE = 8
EPS = 1e-5

_TAPS = set()
_STOP = None


class Sem:
    def __init__(self, ctx, nc, name):
        self.h = ctx.enter_context(nc.semaphore(name))
        self.n = 0
        self.name = name
        self.maxwait = 0

    def inc(self, inst, k=1):
        inst.then_inc(self.h, k)
        self.n += k
        return inst

    def dinc(self, inst):
        return self.inc(inst, 16)

    def wait(self, eng, n=None):
        v = self.n if n is None else n
        self.maxwait = max(self.maxwait, v)
        if v > 0:
            eng.wait_ge(self.h, v)

    def check(self):
        assert self.maxwait <= self.n, \
            f"sem {self.name}: wait {self.maxwait} exceeds final count {self.n}"


class KB:
    def __init__(self, taps):
        self.taps = set(taps)
        self.nc = bass.Bass("TRN2", target_bir_lowering=False, debug=False, detect_race_conditions=False)
        self.ctx = ExitStack()

    def sb(self, name, shape, dt):
        return self.ctx.enter_context(self.nc.sbuf_tensor(name, shape, dt))

    def psum(self, pctx, name, shape, dt=F32):
        return pctx.enter_context(self.nc.psum_tensor(name, shape, dt))

    def lsb(self, pctx, name, shape, dt):
        return pctx.enter_context(self.nc.sbuf_tensor(name, shape, dt))

    def snap(self):
        return {s: s.n for s in self.sems}

    def bar(self, eng, snap):
        for s in self.sems:
            s.wait(eng, snap[s])


def build_program(taps=()):
    kb = KB(taps)
    nc = kb.nc

    def dram_in(name, shape, dt=F32):
        return nc.dram_tensor(name, shape, dt, kind="ExternalInput").ap()

    def dram_tmp(name, shape, dt):
        kind = "ExternalOutput" if name in kb.taps else "Internal"
        return nc.dram_tensor(name, shape, dt, kind=kind).ap()

    u_pad = dram_in("u_pad", [TTOK, D])
    w_in = dram_in("w_in", [DPROJ, D])
    conv_w = dram_in("conv_w", [CONVD, 4])
    conv_b = dram_in("conv_b", [1, CONVD])
    init_nhp = dram_in("init_nhp", [NS, H * PDIM])
    dt_bias = dram_in("dt_bias", [H, 1])
    a_log = dram_in("a_log", [H, 1])
    d_head = dram_in("d_head", [1, H])
    norm_w = dram_in("norm_w", [1, DIN])
    w_out = dram_in("w_out", [D, DIN])
    router_w = dram_in("router_w", [NE, D])
    gate_w = dram_in("gate_w", [NE, D, DFF])
    gate_b = dram_in("gate_b", [1, NE * DFF])
    up_w = dram_in("up_w", [NE, D, DFF])
    up_b = dram_in("up_b", [1, NE * DFF])
    down_w = dram_in("down_w", [NE, DFF, D])
    down_b = dram_in("down_b", [1, NE * D])
    eye_f = dram_in("eye_f", [128, 128])
    eye_bfc = dram_in("eye_bf", [128, 128], BF16)
    ltm2_c = dram_in("ltm2", [128, CH])
    dd_c = dram_in("deltadelta", [64, H * CH])
    hones_c = dram_in("halfones", [2, 128])
    m9_c = dram_in("m9", [1, 9])
    sel8_c = dram_in("sel8", [NE, NE * 128])
    ones_c = dram_in("ones_row", [1, 128])

    out = nc.dram_tensor("out", [T, D], F32, kind="ExternalOutput").ap()

    d_xnfm = dram_tmp("d_xnfm", [D, TTOK], BF16)
    d_z = dram_tmp("d_z", [DFF, TTOK], BF16)
    d_xbc = dram_tmp("d_xbc", [CONVD, TTOK], BF16)
    d_xconv = dram_tmp("d_xconv", [T, CONVD], BF16)
    d_sz = dram_tmp("d_sz", [T, DIN], BF16)
    d_x = dram_tmp("d_x", [T, D], F32)
    d_woutT = dram_tmp("d_woutT", [DIN, D], BF16)
    d_cumahi = dram_tmp("d_cumahi", [H, T], BF16)
    d_cumalo = dram_tmp("d_cumalo", [H, T], F32R)
    d_dec = dram_tmp("d_dec", [NCH, H], F32)
    d_ccin = nc.dram_tensor("d_ccin", [65, 2048], F32, kind="Internal").ap()
    d_ccout = nc.dram_tensor("d_ccout", [520, 2048], F32, kind="Internal").ap()
    d_dt = dram_tmp("d_dt", [H, T], F32)
    d_cuma = dram_tmp("d_cuma", [H, T], F32)
    d_sk = dram_tmp("d_sk", [NS + 1, H * PDIM], F32)
    d_sin = dram_tmp("d_sin", [NS, H * PDIM], F32)
    d_yg = dram_tmp("d_yg", [T, DIN], BF16)
    d_yfm = dram_tmp("d_yfm", [T, DIN], BF16)
    d_ms = dram_tmp("d_ms", [T, 16], F32)

    with kb.ctx:
        sb = kb.sb
        eye32 = sb("eye32", [128, 128], F32)
        eyebf = sb("eyebf", [128, 128], BF16)
        ones_bf = sb("ones_bf", [1, 128], BF16)
        gb_cols = sb("gb_cols", [128, NE * 16], F32)
        ub_cols = sb("ub_cols", [128, NE * 16], F32)
        wrT = sb("wrT", [128, 8 * NE], BF16)
        epsc = sb("epsc", [128, 1], F32)
        ones64f = sb("ones64f", [1, 64], F32)
        nc.const_aps.register(F32, EPS, epsc[:])
        ctx_mix = ExitStack()
        msb = lambda n, s, d=F32: kb.lsb(ctx_mix, n, s, d)
        ltm2 = msb("ltm2s", [128, CH])
        ddel_bf = msb("ddel_bf", [64, H * CH], BF16)
        ddel_r = msb("ddel_r", [64, H * CH], F32R)
        hones_bf = msb("hones_bf", [2, 128], BF16)
        hones_r = msb("hones_r", [2, 128], F32R)
        rstmask = msb("rstmask", [H, T])
        m9 = msb("m9s", [1, 9])
        sel8_r = msb("sel8_r", [NE, NE * 128], F32R)
        convb_row = msb("convb_row", [1, CONVD], BF16)
        normw_cols = msb("normw_cols", [128, 16])
        dhead_rep = msb("dhead_rep", [64, H])
        Ddiag = msb("Ddiag", [128, H * CH], BF16)
        ddt_tm = msb("ddt_tm", [128, NPAIR * H], BF16)
        lhsc_hi = msb("lhsc_hi", [66, NPAIR * 128], BF16)
        lhsc_lo = msb("lhsc_lo", [66, NPAIR * 128], F32R)
        decrows = msb("decrows", [2, NPAIR * H], F32R)
        lhscv_hi = msb("lhscv_hi", [66, 128], BF16)
        lhscv_lo = msb("lhscv_lo", [66, 128], F32R)
        totrow = msb("totrow", [1, H])
        B_fm = msb("B_fm", [NS, TTOK], BF16)
        C_fm = msb("C_fm", [128, TTOK], BF16)
        S_pp = msb("S_pp", [128, H * PDIM], BF16)
        dtraw_fm = msb("dtraw_fm", [H, TTOK])

        s_ld = Sem(kb.ctx, nc, "s_ld")
        s_wl = Sem(kb.ctx, nc, "s_wl")
        s_mm = Sem(kb.ctx, nc, "s_mm")
        s_ev = Sem(kb.ctx, nc, "s_ev")
        s_ac = Sem(kb.ctx, nc, "s_ac")
        s_po = Sem(kb.ctx, nc, "s_po")
        s_st = Sem(kb.ctx, nc, "s_st")
        s_cc = Sem(kb.ctx, nc, "s_cc")
        s_x1 = Sem(kb.ctx, nc, "s_x1")
        s_x2 = Sem(kb.ctx, nc, "s_x2")
        s_vc = Sem(kb.ctx, nc, "s_vc")
        s_sc = Sem(kb.ctx, nc, "s_sc")
        s_pc = Sem(kb.ctx, nc, "s_pc")
        s_c1 = Sem(kb.ctx, nc, "s_c1")
        s_c2 = Sem(kb.ctx, nc, "s_c2")
        ST = [[Sem(kb.ctx, nc, "st%d%s" % (i, ab)) for ab in "ab"] for i in range(8)]
        STG = [[Sem(kb.ctx, nc, "sg%d%s" % (i, ab)) for ab in "ab"] for i in range(5)]
        kb.sems = [s_ld, s_wl, s_mm, s_ev, s_ac, s_po, s_st, s_cc, s_x1, s_x2,
                   s_vc, s_sc, s_pc, s_c1, s_c2] + \
                  [s for pr in ST for s in pr] + [s for pr in STG for s in pr]

        def stld(e, slot, idx, out, in_):
            ST[slot][idx % 2].dinc(e.dma_start(out=out, in_=in_))

        def stw(e, slot, idx, base):
            s = ST[slot][idx % 2]
            s.wait(e, base[s] + 16 * (idx // 2 + 1))

        def stgld(e, slot, idx, out, in_):
            STG[slot][idx % 2].dinc(e.dma_start(out=out, in_=in_))

        def stgw(e, slot, idx, base):
            s = STG[slot][idx % 2]
            s.wait(e, base[s] + 16 * (idx // 2 + 1))

        def vE(e, inst):
            s_vc.inc(inst)
            s_vc.wait(e)
            return inst

        def sE(e, inst):
            s_sc.inc(inst)
            s_sc.wait(e)
            return inst

        def pE(e, inst):
            s_pc.inc(inst)
            s_pc.wait(e)
            return inst

        blk = kb.ctx.enter_context(nc.Block())

        def finalize_dummy():
            with nc.sbuf_tensor("fin_zero", [128, D], F32) as fz:
                sn = kb.snap()

                @blk.vector
                def _(e):
                    kb.bar(e, sn)
                    s_ev.inc(e.memset(fz[:], 0.0))

                @blk.scalar
                def _(e):
                    kb.bar(e, sn)
                    s_ev.wait(e)
                    for t in range(T // 128):
                        s_st.dinc(e.dma_start(out=out[t * 128:(t + 1) * 128, :], in_=fz[:]))
                    s_st.wait(e)

        with ExitStack() as pctx:
            lsb = lambda n, s, d: kb.lsb(pctx, n, s, d)
            ps = lambda n, s, d=F32: kb.psum(pctx, n, s, d)
            rw_bf = lsb("rw_bf", [NE, D], BF16)
            c32a = lsb("c32a", [64, H * CH], F32)
            c32b = lsb("c32b", [NE, NE * 128], F32)
            rowbuf = lsb("rowbuf", [1, DIN], F32)
            dh_row = lsb("dh_row", [1, H], F32)
            ps_cols = ps("ps_cols", [128, 17 * 16])
            ps_dh = ps("ps_dh", [64, H])
            ps_rt = ps("ps_rt", [128, 8 * NE], BF16)
            wo_nat = [lsb("wo_nat%d" % i, [128, DIN], BF16) for i in range(2)]
            wo_col = [lsb("wo_col%d" % i, [128, 16 * 128], BF16) for i in range(2)]
            ps_wo = [ps("ps_wo%d" % i, [128, 2048], BF16) for i in range(2)]

            rows = [(norm_w[:], DIN, normw_cols, 0)]
            for ei in range(NE):
                rows.append((gate_b[0:1, ei * DFF:(ei + 1) * DFF], DFF, gb_cols, ei * 16))
                rows.append((up_b[0:1, ei * DFF:(ei + 1) * DFF], DFF, ub_cols, ei * 16))

            b = kb.snap()

            @blk.sync
            def _(e):
                s_c1.dinc(e.dma_start(out=eye32[:], in_=eye_f[:]))
                s_c1.dinc(e.dma_start(out=eyebf[:], in_=eye_bfc[:]))
                s_c1.dinc(e.dma_start(out=ltm2[:], in_=ltm2_c[:]))
                s_c1.dinc(e.dma_start(out=m9[:], in_=m9_c[:]))
                s_c1.dinc(e.dma_start(out=sel8_r[:].bitcast(F32), in_=sel8_c[:]))
                s_c1.dinc(e.dma_start(out=dh_row[:], in_=d_head[:]))
                for i, (ap, w, _, _o) in enumerate(rows):
                    if i >= 1:
                        s_mm.wait(e, b[s_mm] + 16 * i)
                    s_ld.dinc(e.dma_start(out=rowbuf[0:1, 0:w], in_=ap))

            NC1 = 6 * 16
            NC2 = 5 * 16

            @blk.gpsimd
            def _(e):
                s_c2.dinc(e.dma_start(out=ddel_bf[:], in_=dd_c[:]))
                s_c2.dinc(e.dma_start(out=hones_bf[:], in_=hones_c[:]))
                s_c2.dinc(e.dma_start(out=ones_bf[:], in_=ones_c[:]))
                s_c2.dinc(e.dma_start(out=convb_row[:], in_=conv_b[:]))
                s_c2.dinc(e.dma_start(out=rw_bf[:], in_=router_w[:]))

            @blk.tensor
            def _(e):
                s_c1.wait(e, b[s_c1] + NC1)
                s_c2.wait(e, b[s_c2] + NC2)
                for i, (ap, w, _, _o) in enumerate(rows):
                    s_ld.wait(e, b[s_ld] + 16 * (i + 1))
                    for j in range(w // 128):
                        s_mm.inc(e.transpose(ps_cols[:, i * 16 + j:i * 16 + j + 1],
                                             rowbuf[0:1, j * 128:(j + 1) * 128], eye32[0:1, 0:1]))
                s_ev.wait(e, b[s_ev] + 1)
                s_mm.inc(e.matmul(ps_dh[:], ones64f[:], dh_row[:], start=True, stop=True))
                s_c1.wait(e, b[s_c1] + NC1)
                s_c2.wait(e, b[s_c2] + NC2)
                for dc in range(8):
                    s_mm.inc(e.transpose(ps_rt[:, dc * NE:(dc + 1) * NE],
                                         rw_bf[:, dc * 128:(dc + 1) * 128], eyebf[0:NE, 0:NE]))

            @blk.vector
            def _(e):
                e.memset(epsc[:], EPS)
                s_ev.inc(e.memset(ones64f[:], 1.0))
                vE(e, e.memset(rstmask[:], 0.0))
                for cc in range(NCH):
                    e.memset(rstmask[:, cc * CH + 1:(cc + 1) * CH], 1.0)
                s_c1.wait(e, b[s_c1] + NC1)
                s_c2.wait(e, b[s_c2] + NC2)
                e.tensor_copy(c32a[:], ddel_bf[:])
                e.tensor_copy(ddel_r[:], c32a[:])
                vE(e, e.tensor_copy(c32b[0:2, 0:128], hones_bf[:]))
                e.tensor_copy(hones_r[:], c32b[0:2, 0:128])
                e.tensor_copy(c32b[:], sel8_r[:].bitcast(F32))
                s_ev.inc(e.tensor_copy(sel8_r[:], c32b[:]))
                s_mm.wait(e, b[s_mm] + 17 * 16 + 1 + 8)
                for i, (ap, w, ctile, off) in enumerate(rows):
                    e.tensor_copy(ctile[:, off:off + w // 128], ps_cols[:, i * 16:i * 16 + w // 128])
                vE(e, e.tensor_copy(dhead_rep[:], ps_dh[:]))
                e.tensor_copy(wrT[:], ps_rt[:])
                ey_v = eyebf[0:64, 0:CH].unsqueeze(1).broadcast_to([64, H, CH])
                dh_v = dhead_rep[:].unsqueeze(2).broadcast_to([64, H, CH])
                dg_v = Ddiag[0:64, :].rearrange("s (h l) -> s h l", h=H)
                e.tensor_tensor(dg_v, ey_v, dh_v, ALU.mult)
                s_ev.inc(e.tensor_copy(Ddiag[64:128, :], Ddiag[0:64, :]))

            if _STOP == "p0a":
                finalize_dummy()
                for s in kb.sems:
                    s.check()
                return nc
            b2 = kb.snap()

            @blk.gpsimd
            def _(e):
                for dc in range(8):
                    if dc >= 2:
                        s_mm.wait(e, b2[s_mm] + 16 * (dc - 1))
                    stgld(e, 0, dc, wo_nat[dc % 2][:], w_out[dc * 128:(dc + 1) * 128, :])

            @blk.tensor
            def _(e):
                for dc in range(8):
                    stgw(e, 0, dc, b2)
                    if dc >= 2:
                        s_x1.wait(e, b2[s_x1] + (dc - 1))
                    for fc in range(16):
                        s_mm.inc(e.transpose(ps_wo[dc % 2][:, fc * 128:(fc + 1) * 128],
                                             wo_nat[dc % 2][:, fc * 128:(fc + 1) * 128], eyebf[:]))

            @blk.vector
            def _(e):
                for dc in range(8):
                    s_mm.wait(e, b2[s_mm] + 16 * (dc + 1))
                    if dc >= 2:
                        stw(e, 1, dc - 2, b2)
                    for fc in range(16):
                        inst = e.tensor_scalar_mul(wo_col[dc % 2][:, fc * 128:(fc + 1) * 128],
                                                   ps_wo[dc % 2][:, fc * 128:(fc + 1) * 128],
                                                   normw_cols[:, fc:fc + 1])
                        if fc == 15:
                            s_x1.inc(inst)

            @blk.scalar
            def _(e):
                for dc in range(8):
                    s_x1.wait(e, b2[s_x1] + (dc + 1))
                    dst = d_woutT.rearrange("(fc p) d -> p fc d", p=128)[:, :, dc * 128:(dc + 1) * 128]
                    src = wo_col[dc % 2][:].rearrange("p (fc d) -> p fc d", fc=16)
                    stld(e, 1, dc, dst, src)

        if _STOP == "p0":
            ctx_mix.close()
            finalize_dummy()
            for s in kb.sems:
                s.check()
            return nc

        ctx12 = ExitStack()
        xn_sb = kb.lsb(ctx12, "xn_sb", [128, 8 * TTOK], BF16)
        with ExitStack() as pctx:
            lsb = lambda n, s, d: kb.lsb(pctx, n, s, d)
            utile = [lsb("u_tile%d" % i, [128, D], F32) for i in range(2)]
            ln_sc = lsb("ln_sc", [128, 8], F32)
            ln_scr = lsb("ln_scr", [128, D], F32)
            xn_tm = lsb("xn_tm", [128, D], BF16)
            ps_xt = kb.psum(pctx, "ps_xt", [128, D], BF16)
            b = kb.snap()
            EVT = 3

            @blk.sync
            def _(e):
                kb.bar(e, b)
                for t in range(NT):
                    if t >= 2:
                        s_ev.wait(e, b[s_ev] + (t - 1) * EVT)
                    stld(e, 0, t, utile[t % 2][:], u_pad[t * 128:(t + 1) * 128, :])

            @blk.vector
            def _(e):
                kb.bar(e, b)
                for t in range(NT):
                    stw(e, 0, t, b)
                    ub = utile[t % 2]
                    vE(e, e.reduce_sum(ln_sc[:, 0:1], ub[:], AX.X))
                    s_ac.wait(e, b[s_ac] + 2 * t + 1)
                    vE(e, e.tensor_scalar_mul(ln_sc[:, 1:2], ln_sc[:, 0:1], 1.0 / D))
                    vE(e, e.tensor_scalar_mul(ln_sc[:, 5:6], ln_sc[:, 4:5], 1.0 / D))
                    vE(e, e.tensor_tensor(ln_sc[:, 2:3], ln_sc[:, 1:2], ln_sc[:, 1:2], ALU.mult))
                    s_ev.inc(e.tensor_tensor(ln_sc[:, 3:4], ln_sc[:, 5:6], ln_sc[:, 2:3], ALU.subtract))
                    s_ev.wait(e)
                    s_ac.wait(e, b[s_ac] + 2 * t + 2)
                    s_ev.inc(e.tensor_scalar(xn_tm[:], ub[:], ln_sc[:, 1:2], ln_sc[:, 7:8],
                                             ALU.subtract, ALU.mult))
                    s_mm.wait(e, b[s_mm] + 8 * (t + 1))
                    v = xn_sb[:].rearrange("p (dc tt) -> p dc tt", dc=8)
                    pv = ps_xt[:].rearrange("p (dc tt) -> p dc tt", dc=8)
                    s_ev.inc(e.tensor_copy(v[:, :, t * 128:(t + 1) * 128], pv))

            @blk.scalar
            def _(e):
                kb.bar(e, b)
                for t in range(NT):
                    stw(e, 0, t, b)
                    ub = utile[t % 2]
                    s_ac.inc(e.activation(ln_scr[:], ub[:], AF.Square, accum_out=ln_sc[:, 4:5]))
                    s_ev.wait(e, b[s_ev] + t * EVT + 1)
                    sE(e, e.activation(ln_sc[:, 6:7], ln_sc[:, 3:4], AF.Ln, bias=EPS))
                    s_ac.inc(e.activation(ln_sc[:, 7:8], ln_sc[:, 6:7], AF.Exp, scale=-0.5))
                    if "d_xnfm" in kb.taps:
                        s_ev.wait(e, b[s_ev] + t * EVT + 3)
                        dst = d_xnfm.rearrange("(dc p) tt -> p dc tt", p=128)[:, :, t * 128:(t + 1) * 128]
                        src = xn_sb[:].rearrange("p (dc tt) -> p dc tt", dc=8)[:, :, t * 128:(t + 1) * 128]
                        s_st.dinc(e.dma_start(out=dst, in_=src))

            @blk.tensor
            def _(e):
                kb.bar(e, b)
                for t in range(NT):
                    s_ev.wait(e, b[s_ev] + t * EVT + 2)
                    if t >= 1:
                        s_ev.wait(e, b[s_ev] + t * EVT)
                    for dc in range(8):
                        s_mm.inc(e.transpose(ps_xt[:, dc * 128:(dc + 1) * 128],
                                             xn_tm[:, dc * 128:(dc + 1) * 128], eyebf[:]))

        if _STOP == "p1":
            ctx12.close()
            ctx_mix.close()
            finalize_dummy()
            for s in kb.sems:
                s.check()
            return nc

        NSPL = [(0, 512), (512, 512), (1024, 512), (1536, 512), (2048, 128)]
        NSP = len(NSPL)
        with ExitStack() as pctx:
            lsb = lambda n, s, d: kb.lsb(pctx, n, s, d)
            wnat = [lsb("wnat%d" % i, [128, D], BF16) for i in range(2)]
            wTt = [lsb("wTt%d" % i, [128, 8 * 128], BF16) for i in range(2)]
            evb = [lsb("evb%d" % i, [128, TTOK], BF16) for i in range(2)]
            ps_wt = [kb.psum(pctx, "ps_wt%d" % i, [128, 8 * 128], BF16) for i in range(2)]
            ps_ip = [kb.psum(pctx, "ps_ip%d" % i, [128, 512]) for i in range(2)]
            b = kb.snap()
            MMF = 8 + 8 * NSP

            @blk.gpsimd
            def _(e):
                kb.bar(e, b)
                for fc in range(34):
                    mrows = 32 if fc == 33 else 128
                    if fc >= 2:
                        s_mm.wait(e, b[s_mm] + (fc - 2) * MMF + 8)
                    stgld(e, 0, fc, wnat[fc % 2][0:mrows, :],
                         w_in[fc * 128:fc * 128 + mrows, :])

            @blk.tensor
            def _(e):
                kb.bar(e, b)
                for fc in range(34):
                    mrows = 32 if fc == 33 else 128
                    stgw(e, 0, fc, b)
                    if fc >= 2:
                        s_x1.wait(e, b[s_x1] + (fc - 1))
                    for dc in range(8):
                        s_mm.inc(e.transpose(ps_wt[fc % 2][:, dc * 128:dc * 128 + mrows],
                                             wnat[fc % 2][0:mrows, dc * 128:(dc + 1) * 128],
                                             eyebf[0:mrows, 0:mrows]))
                    s_x1.wait(e, b[s_x1] + (fc + 1))
                    for j, (t0, tw) in enumerate(NSPL):
                        g = fc * NSP + j
                        if g >= 2:
                            s_x2.wait(e, b[s_x2] + (g - 1))
                        for dc in range(8):
                            s_mm.inc(e.matmul(ps_ip[g % 2][0:mrows, 0:tw],
                                              wTt[fc % 2][:, dc * 128:dc * 128 + mrows],
                                              xn_sb[:, dc * TTOK + t0: dc * TTOK + t0 + tw],
                                              start=(dc == 0), stop=(dc == 7)))

            @blk.vector
            def _(e):
                kb.bar(e, b)
                for fc in range(34):
                    mrows = 32 if fc == 33 else 128
                    s_mm.wait(e, b[s_mm] + fc * MMF + 8)
                    dv = wTt[fc % 2][:].rearrange("p (dc m) -> p dc m", dc=8)[:, :, 0:mrows]
                    pv = ps_wt[fc % 2][:].rearrange("p (dc m) -> p dc m", dc=8)[:, :, 0:mrows]
                    s_x1.inc(e.tensor_copy(dv, pv))
                    if fc >= 2 and fc < 33:
                        stw(e, 1, fc - 2, b)
                    for j, (t0, tw) in enumerate(NSPL):
                        g = fc * NSP + j
                        s_mm.wait(e, b[s_mm] + fc * MMF + 8 + 8 * (j + 1))
                        if fc == 33:
                            s_x2.inc(e.tensor_copy(dtraw_fm[:, t0:t0 + tw], ps_ip[g % 2][0:32, 0:tw]))
                        else:
                            s_x2.inc(e.tensor_copy(evb[fc % 2][:, t0:t0 + tw], ps_ip[g % 2][:, 0:tw]))

            @blk.scalar
            def _(e):
                kb.bar(e, b)
                for fc in range(33):
                    s_x2.wait(e, b[s_x2] + (fc + 1) * NSP)
                    if fc < 16:
                        dst = d_z[fc * 128:(fc + 1) * 128, :]
                    else:
                        dst = d_xbc[(fc - 16) * 128:(fc - 15) * 128, :]
                    stld(e, 1, fc, dst, evb[fc % 2][:])

        if _STOP == "p2":
            ctx12.close()
            ctx_mix.close()
            finalize_dummy()
            for s in kb.sems:
                s.check()
            return nc

        ctx12.close()

        with ExitStack() as pctx:
            lsb = lambda n, s, d: kb.lsb(pctx, n, s, d)
            dtb_col = lsb("dtb_col", [H, 1], F32)
            alog_col = lsb("alog_col", [H, 1], F32)
            a_col = lsb("a_col", [H, 1], F32)
            sp0 = lsb("sp0", [H, T], F32)
            sp1 = lsb("sp1", [H, T], F32)
            sp2 = lsb("sp2", [H, T], F32)
            dt_fm = lsb("dt_fm", [H, T], F32)
            cuma = lsb("cuma", [H, T], F32)
            lnadj = lsb("lnadj", [H, T], F32)
            cahi_bf = lsb("cahi_bf", [H, T], BF16)
            calo = lsb("calo", [H, T], F32)
            calo_r = lsb("calo_r", [H, T], F32R)
            lahi_bf = lsb("lahi_bf", [H, T], BF16)
            lalo_r = lsb("lalo_r", [H, T], F32R)
            a2_col = lsb("a2_col", [H, 1], F32)
            adt = sp2
            lndt = sp1
            lalo = sp2
            ddarg = sp1
            ddexp = sp0
            ddt_fm = sp2
            decfm = lsb("decfm", [H, NCH], F32)
            dectm = lsb("dectm", [NCH, H], F32)
            tot_col = lsb("tot_col", [H, 1], F32)
            dr32 = lsb("dr32", [2, NPAIR * H], F32)
            ps_sm = kb.psum(pctx, "ps_sm", [128, NPAIR * H])
            ps_sm2 = kb.psum(pctx, "ps_sm2", [NCH, 2 * H])
            b = kb.snap()

            @blk.sync
            def _(e):
                kb.bar(e, b)
                s_ld.dinc(e.dma_start(out=dtb_col[:], in_=dt_bias[:]))
                s_ld.dinc(e.dma_start(out=alog_col[:], in_=a_log[:]))
                s_st.wait(e, b[s_st] + 16 * 3)
                src2 = d_dec.rearrange("(p eo) h -> eo p h", eo=2)
                s_ld.dinc(e.dma_start(out=dr32[:].rearrange("eo (p h) -> eo p h", p=NPAIR), in_=src2))

            @blk.scalar
            def _(e):
                kb.bar(e, b)
                s_ld.wait(e, b[s_ld] + 32)
                e.activation(sp0[:], dtraw_fm[:, HALO:], AF.Identity, bias=dtb_col[:])
                e.activation(sp1[:], sp0[:], AF.Abs)
                e.activation(sp2[:], sp1[:], AF.Exp, scale=-1.0)
                sE(e, e.activation(sp1[:], sp2[:], AF.Ln, bias=1.0))
                e.activation(sp2[:], sp0[:], AF.Relu)
                s_ac.inc(e.activation(a_col[:], alog_col[:], AF.Exp))
                s_ev.wait(e, b[s_ev] + 1)
                s_ac.inc(e.activation(lndt[:], dt_fm[:], AF.Ln))
                s_ev.wait(e, b[s_ev] + 2)
                e.activation(ddexp[:], ddarg[:], AF.Exp)
                lastv2 = cuma[:].rearrange("h (c s) -> h c s", c=NCH)[:, :, CH - 1:CH].rearrange("h c s -> h (c s)")
                s_ac.inc(e.activation(decfm[:], lastv2, AF.Exp))

            @blk.vector
            def _(e):
                kb.bar(e, b)
                s_ac.wait(e, b[s_ac] + 1)
                e.tensor_tensor(dt_fm[:], sp1[:], sp2[:], ALU.add)
                vE(e, e.tensor_scalar_mul(a2_col[:], a_col[:], -1.0))
                e.tensor_scalar_mul(adt[:], dt_fm[:], a2_col[:])
                s_ev.inc(e.tensor_tensor_scan(cuma[:], rstmask[:], adt[:], 0.0, ALU.mult, ALU.add))
                s_ac.wait(e, b[s_ac] + 2)
                e.tensor_tensor(lnadj[:], lndt[:], cuma[:], ALU.subtract)
                e.tensor_copy(cahi_bf[:], cuma[:])
                e.tensor_copy(sp0[:], cahi_bf[:])
                e.tensor_tensor(calo[:], cuma[:], sp0[:], ALU.subtract)
                e.tensor_copy(calo_r[:], calo[:])
                e.tensor_copy(lahi_bf[:], lnadj[:])
                e.tensor_copy(sp0[:], lahi_bf[:])
                e.tensor_tensor(lalo[:], lnadj[:], sp0[:], ALU.subtract)
                e.tensor_copy(lalo_r[:], lalo[:])
                e.memset(lhsc_hi[:], 0.0)
                e.memset(lhsc_lo[:].bitcast(F32), 0.0)
                lh = lhsc_hi[:].rearrange("k (p m) -> k p m", p=NPAIR)
                ll = lhsc_lo[:].rearrange("k (p m) -> k p m", p=NPAIR)
                la_hi = lahi_bf[:].rearrange("h (p eo s) -> h p eo s", p=NPAIR, eo=2)
                e.tensor_copy(lh[0:H, :, 0:CH], la_hi[:, :, 0, :])
                e.tensor_copy(lh[32:64, :, CH:128], la_hi[:, :, 1, :])
                la_lo = lalo_r[:].rearrange("h (p eo s) -> h p eo s", p=NPAIR, eo=2)
                e.tensor_copy(ll[0:H, :, 0:CH], la_lo[:, :, 0, :])
                e.tensor_copy(ll[32:64, :, CH:128], la_lo[:, :, 1, :])
                ho_v = hones_bf[:].unsqueeze(1).broadcast_to([2, NPAIR, 128])
                e.tensor_copy(lh[64:66, :, :], ho_v)
                ho_vr = hones_r[:].unsqueeze(1).broadcast_to([2, NPAIR, 128])
                e.tensor_copy(ll[64:66, :, :], ho_vr)
                e.memset(lhscv_hi[:], 0.0)
                e.memset(lhscv_lo[:].bitcast(F32), 0.0)
                e.tensor_copy(lhscv_hi[64:66, :], hones_bf[:])
                e.tensor_copy(lhscv_lo[64:66, :], hones_r[:])
                last_v = cuma[:].rearrange("h (c s) -> h c s", c=NCH)[:, :, CH - 1:CH]
                last_b = last_v.broadcast_to([H, NCH, CH])
                e.tensor_tensor(ddarg[:].rearrange("h (c s) -> h c s", c=NCH), last_b,
                                cuma[:].rearrange("h (c s) -> h c s", c=NCH), ALU.subtract)
                s_ev.inc(e.reduce_sum(tot_col[:], last_v.squeeze(2), AX.X))
                s_ac.wait(e, b[s_ac] + 3)
                s_ev.inc(e.tensor_tensor(ddt_fm[:], ddexp[:], dt_fm[:], ALU.mult))
                s_mm.wait(e, b[s_mm] + NPAIR + 2)
                e.tensor_copy(ddt_tm[:], ps_sm[:])
                e.tensor_copy(dectm[:], ps_sm2[0:NCH, 0:H])
                s_ev.inc(e.tensor_copy(totrow[:], ps_sm2[0:1, H:2 * H]))
                s_ld.wait(e, b[s_ld] + 48)
                s_ev.inc(e.tensor_copy(decrows[:], dr32[:]))

            @blk.tensor
            def _(e):
                kb.bar(e, b)
                s_ev.wait(e, b[s_ev] + 3)
                for p in range(NPAIR):
                    s_mm.inc(e.transpose(ps_sm[:, p * H:(p + 1) * H],
                                         ddt_fm[:, p * 128:(p + 1) * 128], eye32[0:H, 0:H]))
                s_mm.inc(e.transpose(ps_sm2[0:NCH, 0:H], decfm[:], eye32[0:H, 0:H]))
                s_mm.inc(e.transpose(ps_sm2[0:1, H:2 * H], tot_col[:], eye32[0:H, 0:H]))

            @blk.scalar
            def _(e):
                s_ev.wait(e, b[s_ev] + 4)
                s_st.dinc(e.dma_start(out=d_cumahi[:], in_=cahi_bf[:]))
                s_st.dinc(e.dma_start(out=d_cumalo[:], in_=calo_r[:]))
                s_st.dinc(e.dma_start(out=d_dec[:], in_=dectm[:]))
                if "d_dt" in kb.taps:
                    s_st.dinc(e.dma_start(out=d_dt[:], in_=dt_fm[:]))
                if "d_cuma" in kb.taps:
                    s_st.dinc(e.dma_start(out=d_cuma[:], in_=cuma[:]))

        if _STOP == "p2b":
            ctx_mix.close()
            finalize_dummy()
            for s in kb.sems:
                s.check()
            return nc

        with ExitStack() as pctx:
            lsb = lambda n, s, d: kb.lsb(pctx, n, s, d)
            cw_cols = lsb("cw_cols", [128, 17 * 4], F32)
            cb32 = lsb("cb32", [1, 128], F32)
            diag = lsb("diag", [128, 17 * 4 * 128], BF16)
            bcb_col = lsb("bcb_col", [128, 1], F32)
            strip = lsb("strip", [128, TTOK], BF16)
            acc0 = lsb("acc0", [128, TTOK], BF16)
            acc1 = lsb("acc1", [128, TTOK], BF16)
            bcs = lsb("bcs", [128, TTOK], BF16)
            xbct = [lsb("xbct%d" % i, [128, 17 * 131], BF16) for i in range(2)]
            zt = [lsb("zt%d" % i, [128, 16 * 128], BF16) for i in range(2)]
            xcv = [lsb("xcv%d" % i, [128, CONVD], BF16) for i in range(2)]
            szt = [lsb("szt%d" % i, [128, DIN], BF16) for i in range(2)]
            ps_cv = [kb.psum(pctx, "ps_cv%d" % i, [128, 512]) for i in range(2)]
            ps_z = kb.psum(pctx, "ps_z", [128, DIN], BF16)
            ps_bc = kb.psum(pctx, "ps_bc", [128, 512])
            b = kb.snap()
            NG = 5
            CUM = [20, 40, 60, 80, 85]
            MMT = 85 + 16
            ACT_T = 5 + 2
            PRE_MM = 1

            def evac_done(gg):
                t2, g2 = divmod(gg, NG)
                return b[s_ac] + 1 + t2 * ACT_T + g2 + 1

            @blk.sync
            def _(e):
                kb.bar(e, b)
                src = conv_w.rearrange("(fc p) k -> p fc k", p=128)
                s_ld.dinc(e.dma_start(out=cw_cols[:].rearrange("p (fc k) -> p fc k", fc=17), in_=src))
                s_ld.dinc(e.dma_start(out=strip[:], in_=d_xbc[DIN:CONVD, :]))
                for t in range(NPAIR):
                    if t >= 2:
                        s_mm.wait(e, b[s_mm] + PRE_MM + (t - 1) * MMT)
                    src = d_xbc.rearrange("(fc p) tt -> p fc tt", p=128)[:, :, t * 128 + 125:t * 128 + 256]
                    stld(e, 0, t, xbct[t % 2][:].rearrange("p (fc c) -> p fc c", fc=17), src)
                    srcz = d_z.rearrange("(zc p) tt -> p zc tt", p=128)[:, :, t * 128 + HALO:t * 128 + HALO + 128]
                    stld(e, 2, t, zt[t % 2][:].rearrange("p (zc c) -> p zc c", zc=16), srcz)

            @blk.tensor
            def _(e):
                kb.bar(e, b)
                s_ev.wait(e, b[s_ev] + 1)
                s_mm.inc(e.transpose(ps_bc[:, 0:1], cb32[0:1, :], eye32[0:1, 0:1]))
                for t in range(NPAIR):
                    stw(e, 0, t, b)
                    stw(e, 2, t, b)
                    s_ev.wait(e, b[s_ev] + 2)
                    xb = xbct[t % 2][:].rearrange("p (fc c) -> p fc c", fc=17)
                    for fcg in range(NG):
                        gg = t * NG + fcg
                        if gg >= 2:
                            s_ac.wait(e, evac_done(gg - 2))
                        fcs = range(fcg * 4, min(fcg * 4 + 4, 17))
                        for fc in fcs:
                            col = (fc % 4) * 128
                            for k in range(4):
                                s_mm.inc(e.matmul(ps_cv[fcg % 2][:, col:col + 128],
                                                  xb[:, fc, k:k + 128],
                                                  diag[:, (fc * 4 + k) * 128:(fc * 4 + k + 1) * 128],
                                                  start=(k == 0), stop=False))
                            s_mm.inc(e.matmul(ps_cv[fcg % 2][:, col:col + 128],
                                              ones_bf[:],
                                              convb_row[0:1, fc * 128:(fc + 1) * 128],
                                              start=False, stop=True))
                    ztv = zt[t % 2][:].rearrange("p (zc c) -> p zc c", zc=16)
                    if t >= 1:
                        s_ac.wait(e, b[s_ac] + 1 + (t - 1) * ACT_T + 7)
                    for zc in range(16):
                        s_mm.inc(e.transpose(ps_z[:, zc * 128:(zc + 1) * 128], ztv[:, zc, :], eyebf[:]))

            @blk.scalar
            def _(e):
                kb.bar(e, b)
                s_ev.wait(e, b[s_ev] + 3)
                s_ac.inc(e.activation(bcs[:], acc0[:], AF.Silu, bias=bcb_col[:]))
                for t in range(NPAIR):
                    if t >= 2:
                        stgw(e, 1, t - 2, b)
                        stgw(e, 3, t - 2, b)
                    for fcg in range(NG):
                        s_mm.wait(e, b[s_mm] + PRE_MM + t * MMT + CUM[fcg])
                        w = 512 if fcg < 4 else 128
                        s_ac.inc(e.activation(xcv[t % 2][:, fcg * 512:fcg * 512 + w],
                                              ps_cv[fcg % 2][:, 0:w], AF.Silu))
                    s_mm.wait(e, b[s_mm] + PRE_MM + t * MMT + 85 + 16)
                    s_ac.inc(e.activation(szt[t % 2][:, 0:1024], ps_z[:, 0:1024], AF.Silu))
                    s_ac.inc(e.activation(szt[t % 2][:, 1024:2048], ps_z[:, 1024:2048], AF.Silu))

            @blk.vector
            def _(e):
                kb.bar(e, b)
                s_ld.wait(e, b[s_ld] + 32)
                cwv = cw_cols[:].rearrange("p (fc k) -> p fc k", fc=17)
                s_ev.inc(e.tensor_copy(cb32[0:1, :], convb_row[0:1, DIN:CONVD]))
                for fc in range(17):
                    for k in range(4):
                        inst = e.tensor_scalar_mul(diag[:, (fc * 4 + k) * 128:(fc * 4 + k + 1) * 128],
                                                   eyebf[:], cwv[:, fc, k:k + 1])
                s_ev.inc(inst)
                s_mm.wait(e, b[s_mm] + 1)
                e.tensor_copy(bcb_col[:], ps_bc[:, 0:1])
                s_ld.wait(e, b[s_ld] + 32)
                W = TTOK - 3
                e.memset(acc0[:, 0:3], 0.0)
                e.memset(acc1[:, 0:3], 0.0)
                e.tensor_scalar_mul(acc1[:, 3:], strip[:, 0:W], cwv[:, 16, 0:1])
                e.scalar_tensor_tensor(acc0[:, 3:], strip[:, 1:W + 1], cwv[:, 16, 1:2],
                                       acc1[:, 3:], ALU.mult, ALU.add)
                e.scalar_tensor_tensor(acc1[:, 3:], strip[:, 2:W + 2], cwv[:, 16, 2:3],
                                       acc0[:, 3:], ALU.mult, ALU.add)
                s_ev.inc(e.scalar_tensor_tensor(acc0[:, 3:], strip[:, 3:W + 3], cwv[:, 16, 3:4],
                                                acc1[:, 3:], ALU.mult, ALU.add))
                s_ac.wait(e, b[s_ac] + 1)
                e.tensor_copy(B_fm[:], bcs[0:64, :])
                e.tensor_copy(C_fm[0:64, :], bcs[64:128, :])
                s_ev.inc(e.tensor_copy(C_fm[64:128, :], bcs[64:128, :]))

            @blk.gpsimd
            def _(e):
                kb.bar(e, b)
                for t in range(NPAIR):
                    s_ac.wait(e, b[s_ac] + 1 + t * ACT_T + 5)
                    stgld(e, 1, t, d_xconv[t * 128:(t + 1) * 128, :], xcv[t % 2][:])
                    s_ac.wait(e, b[s_ac] + 1 + (t + 1) * ACT_T)
                    stgld(e, 3, t, d_sz[t * 128:(t + 1) * 128, :], szt[t % 2][:])

        if _STOP == "p3":
            ctx_mix.close()
            finalize_dummy()
            for s in kb.sems:
                s.check()
            return nc

        def emit_states_pair(e_pe, p, xct, Bp, ps_cs, ps_dr, b, ev_of_pair, evs_per_pair):
            s_ev.wait(e_pe, ev_of_pair(p) + 1)
            s_mm.inc(e_pe.matmul(ps_dr[:, 0:H], hones_r[:], decrows[:, p * H:(p + 1) * H],
                                 start=True, stop=True))
            for half, base in ((0, 0), (1, 64)):
                for h in range(H):
                    s_mm.inc(e_pe.matmul(ps_cs[base:base + 64, h * 64:(h + 1) * 64],
                                         Bp[base:base + 64, h * 64:(h + 1) * 64],
                                         xct[base:base + 64, h * 64:(h + 1) * 64],
                                         start=True, stop=True))

        def emit_scan_pair(e_v, p, ps_cs, ps_dr, tmp_e, tmp_o):
            drv0 = ps_dr[0:64, 0:H].unsqueeze(2).broadcast_to([64, H, 64])
            drv1 = ps_dr[64:128, 0:H].unsqueeze(2).broadcast_to([64, H, 64])
            s3 = lambda ap: ap.rearrange("n (h pp) -> n h pp", h=H)
            s_ev.inc(e_v.tensor_tensor(s3(tmp_e[:]), s3(S_pp[0:64, :]), drv0, ALU.mult))
            s_ev.inc(e_v.tensor_tensor(S_pp[64:128, :], tmp_e[:], ps_cs[0:64, :], ALU.add))
            s_ev.inc(e_v.tensor_tensor(s3(tmp_o[:]), s3(S_pp[64:128, :]), drv1, ALU.mult))
            s_ev.inc(e_v.tensor_tensor(S_pp[0:64, :], tmp_o[:], ps_cs[64:128, :], ALU.add))

        with ExitStack() as pctx:
            lsb = lambda n, s, d: kb.lsb(pctx, n, s, d)
            xct = [lsb("xctA%d" % i, [128, CONVD], BF16) for i in range(2)]
            Bp = [lsb("BpA%d" % i, [128, H * NS], BF16) for i in range(2)]
            tmp_e = lsb("tmpAe", [64, H * PDIM], BF16)
            tmp_o = lsb("tmpAo", [64, H * PDIM], BF16)
            sk32 = lsb("sk32", [NS, H * PDIM], F32)
            trow = lsb("trowA", [1, H * PDIM], F32)
            ps_cs = kb.psum(pctx, "ps_csA", [128, H * PDIM])
            ps_dr = kb.psum(pctx, "ps_drA", [128, 512])
            b = kb.snap()
            EVP = 5
            MMP = 65

            def ev_of_pair(p):
                return b[s_ev] + p * EVP

            @blk.sync
            def _(e):
                kb.bar(e, b)
                for p in range(NPAIR):
                    if p >= 2:
                        s_ev.wait(e, ev_of_pair(p - 1))
                    stld(e, 0, p, xct[p % 2][:], d_xconv[p * 128:(p + 1) * 128, :])

            @blk.vector
            def _(e):
                kb.bar(e, b)
                e.memset(S_pp[:], 0.0)
                for p in range(NPAIR):
                    stw(e, 0, p, b)
                    bview = xct[p % 2][:, DIN:DIN + NS].unsqueeze(1).broadcast_to([128, H, NS])
                    dview = ddt_tm[:, p * H:(p + 1) * H].unsqueeze(2).broadcast_to([128, H, NS])
                    s_ev.inc(e.tensor_tensor(Bp[p % 2][:].rearrange("t (h n) -> t h n", h=H),
                                             bview, dview, ALU.mult))
                    s_mm.wait(e, b[s_mm] + (p + 1) * MMP)
                    emit_scan_pair(e, p, ps_cs, ps_dr, tmp_e, tmp_o)
                s_ev.inc(e.tensor_copy(sk32[:], S_pp[0:64, :]))
                vE(e, e.memset(trow[:], 0.0))
                s_ev.inc(e.tensor_copy(trow[0:1, 0:H], totrow[:]))

            @blk.tensor
            def _(e):
                kb.bar(e, b)
                for p in range(NPAIR):
                    if p >= 1:
                        s_ev.wait(e, ev_of_pair(p))
                    stw(e, 0, p, b)
                    emit_states_pair(e, p, xct[p % 2][:], Bp[p % 2][:], ps_cs, ps_dr,
                                     b, ev_of_pair, EVP)

            @blk.gpsimd
            def _(e):
                kb.bar(e, b)
                s_ev.wait(e, b[s_ev] + NPAIR * EVP + 2)
                s_wl.dinc(e.dma_start(out=d_ccin[0:NS, :], in_=sk32[:]))
                s_wl.dinc(e.dma_start(out=d_ccin[NS:NS + 1, :], in_=trow[:]))
                if "d_sk" in kb.taps:
                    s_wl.dinc(e.dma_start(out=d_sk[0:NS, :], in_=sk32[:]))
                    s_wl.dinc(e.dma_start(out=d_sk[NS:NS + 1, 0:H], in_=totrow[:]))

        if _STOP == "p4":
            ctx_mix.close()
            finalize_dummy()
            for s in kb.sems:
                s.check()
            return nc

        with ExitStack() as pctx:
            lsb = lambda n, s, d: kb.lsb(pctx, n, s, d)
            ttl = lsb("ttl", [1, NCORES * H], F32)
            mt = lsb("mt", [1, NCORES * H], F32)
            srow = lsb("srow", [1, 9 * H], F32)
            wrow = lsb("wrow", [1, 9 * H], F32)
            wm = lsb("wm", [1, 9 * H], F32)
            sgb = [lsb("sgb%d" % i, [NS, H * PDIM], F32) for i in range(2)]
            tb = lsb("tb", [NS, H * PDIM], F32)
            accb = [lsb("accb%d" % i, [NS, H * PDIM], F32) for i in range(2)]
            ps_w = kb.psum(pctx, "ps_w", [NS, 512])
            b = kb.snap()

            @blk.gpsimd
            def _(e):
                kb.bar(e, b)
                e.collective_compute(
                    "AllGather", ALU.bypass,
                    replica_groups=[list(range(NCORES))],
                    ins=[d_ccin.opt()],
                    outs=[d_ccout.opt()],
                ).then_inc(s_cc.h)
                s_cc.n += 1
                s_cc.wait(e)
                srct = d_ccout.rearrange("(j r) w -> j r w", r=65)[:, 64, 0:H].unsqueeze(0)
                s_wl.dinc(e.dma_start(out=ttl[:].rearrange("one (j h) -> one j h", j=NCORES),
                                      in_=srct))

            @blk.sync
            def _(e):
                kb.bar(e, b)
                s_cc.wait(e)
                for j in range(9):
                    if j >= 2:
                        s_ev.wait(e, b[s_ev] + 3 + (j - 1) * 2)
                    if j == 0:
                        stld(e, 0, j, sgb[0][:], init_nhp[:])
                    else:
                        stld(e, 0, j, sgb[j % 2][:],
                             d_ccout[65 * (j - 1):65 * (j - 1) + NS, :])

            @blk.vector
            def _(e):
                kb.bar(e, b)
                s_wl.wait(e, b[s_wl] + 16)
                m9v = m9[0:1, 1:9].unsqueeze(2).broadcast_to([1, 8, H])
                vE(e, e.tensor_tensor(mt[:].rearrange("one (j h) -> one j h", j=8),
                                      ttl[:].rearrange("one (j h) -> one j h", j=8),
                                      m9v, ALU.mult))
                vE(e, e.memset(srow[0:1, 8 * H:9 * H], 0.0))
                for j in range(6, -2, -1):
                    sl = (j + 1) * H
                    vE(e, e.tensor_tensor(srow[0:1, sl:sl + H], srow[0:1, sl + H:sl + 2 * H],
                                          mt[0:1, (j + 1) * H:(j + 2) * H] if j < 7 else mt[0:1, 7 * H:8 * H],
                                          ALU.add))
                s_ev.inc(e.nop())

            @blk.scalar
            def _(e):
                kb.bar(e, b)
                s_ev.wait(e, b[s_ev] + 1)
                s_ac.inc(e.activation(wrow[:], srow[:], AF.Exp))

            @blk.vector
            def _(e):
                s_ac.wait(e, b[s_ac] + 1)
                m9w = m9[0:1, 0:9].unsqueeze(2).broadcast_to([1, 9, H])
                s_ev.inc(e.tensor_tensor(wm[:].rearrange("one (j h) -> one j h", j=9),
                                         wrow[:].rearrange("one (j h) -> one j h", j=9),
                                         m9w, ALU.mult))
                for j in range(9):
                    stw(e, 0, j, b)
                    s_mm.wait(e, b[s_mm] + (j + 1))
                    wv = ps_w[:, 0:H].unsqueeze(2).broadcast_to([NS, H, 64])
                    s_ev.inc(e.tensor_tensor(tb[:].rearrange("n (h pp) -> n h pp", h=H),
                                             sgb[j % 2][:].rearrange("n (h pp) -> n h pp", h=H),
                                             wv, ALU.mult))
                    if j == 0:
                        s_ev.inc(e.tensor_copy(accb[1][:], tb[:]))
                    else:
                        s_ev.inc(e.tensor_tensor(accb[(j + 1) % 2][:], accb[j % 2][:], tb[:], ALU.add))
                s_ev.inc(e.tensor_copy(S_pp[0:64, :], accb[1][:]))

            @blk.tensor
            def _(e):
                kb.bar(e, b)
                s_ev.wait(e, b[s_ev] + 2)
                for j in range(9):
                    if j >= 1:
                        s_ev.wait(e, b[s_ev] + 3 + (j - 1) * 2 + 1)
                    s_mm.inc(e.matmul(ps_w[:, 0:H], ones64f[:], wm[0:1, j * H:(j + 1) * H],
                                      start=True, stop=True))

            @blk.gpsimd
            def _(e):
                if "d_sin" in kb.taps:
                    s_ev.wait(e, b[s_ev] + 3 + 18)
                    s_wl.dinc(e.dma_start(out=d_sin[:], in_=accb[1][:]))

        if _STOP == "p5":
            ctx_mix.close()
            finalize_dummy()
            for s in kb.sems:
                s.check()
            return nc

        with ExitStack() as pctx:
            lsb = lambda n, s, d: kb.lsb(pctx, n, s, d)
            xct = [lsb("xctB%d" % i, [128, CONVD], BF16) for i in range(2)]
            sztl = [lsb("sztB%d" % i, [128, DIN], BF16) for i in range(2)]
            utl = [lsb("utB%d" % i, [128, D], F32) for i in range(2)]
            comp_hi = [lsb("comph%d" % i, [66, H * CH], BF16) for i in range(2)]
            comp_lo = [lsb("compl%d" % i, [66, H * CH], F32R) for i in range(2)]
            wog = [lsb("wog%d" % i, [128, 4 * D], BF16) for i in range(2)]
            Bp = [lsb("BpB%d" % i, [128, H * NS], BF16) for i in range(2)]
            gd = lsb("gd", [128, 96], F32)
            expsR = lsb("expsR", [128, H * CH], BF16)
            expsV = lsb("expsV", [128, H * CH], BF16)
            rt0 = lsb("rt0", [128, H * CH], BF16)
            rtF = lsb("rtF", [128, H * CH], BF16)
            vtF = lsb("vtF", [128, H * CH], BF16)
            tmp_e = lsb("tmpBe", [64, H * PDIM], BF16)
            tmp_o = lsb("tmpBo", [64, H * PDIM], BF16)
            yg = lsb("yg", [128, DIN], BF16)
            sqscr = lsb("sqscr", [128, 1024], BF16)
            msc = lsb("msc", [128, 16], F32)
            yfm = lsb("yfm", [128, 16 * 128], BF16)
            xout = [lsb("xoutB%d" % i, [128, D], F32) for i in range(2)]
            ps_big = kb.psum(pctx, "ps_bigB", [128, 2048])
            ps_y0 = kb.psum(pctx, "ps_y0B", [128, 1024])
            ps_y1 = kb.psum(pctx, "ps_y1B", [128, 1024])
            ps_gt = ps_big[:, 0:64]
            ps_dr = ps_big[:, 64:96]
            ps_o0 = ps_big[:, 0:512]
            ps_o1 = ps_big[:, 512:1024]
            ps_yt0 = ps_big[:, 1024:1536].bitcast(BF16)
            ps_yt1 = ps_big[:, 1536:2048].bitcast(BF16)
            clampR = lsb("clampR", [128, H * CH], F32)
            b = kb.snap()
            MMP = 259
            EVP = 18
            ACP = 5
            WLP = 18 * 16

            def pb(p):
                return b[s_mm] + p * MMP

            def evb(p):
                return b[s_ev] + 1 + p * EVP

            def acb(p):
                return b[s_ac] + p * ACP

            @blk.sync
            def _(e):
                kb.bar(e, b)
                for p in range(NPAIR):
                    if p >= 2:
                        s_ev.wait(e, evb(p - 1))
                    stld(e, 0, p, xct[p % 2][:], d_xconv[p * 128:(p + 1) * 128, :])
                    stld(e, 2, p, sztl[p % 2][:], d_sz[p * 128:(p + 1) * 128, :])
                    stld(e, 4, p, utl[p % 2][:], u_pad[p * 128 + HALO:p * 128 + HALO + 128, :])

            @blk.gpsimd
            def _(e):
                kb.bar(e, b)
                for p in range(NPAIR):
                    if p >= 2:
                        s_mm.wait(e, pb(p - 2) + 19)
                    srch = d_cumahi.rearrange("h (c s) -> c h s", c=NCH)[2 * p:2 * p + 2, :, :]
                    stld(e, 5, p, comp_hi[p % 2][64:66, :].rearrange("k (h s) -> k h s", h=H), srch)
                    srcl = d_cumalo.rearrange("h (c s) -> c h s", c=NCH)[2 * p:2 * p + 2, :, :]
                    stld(e, 6, p, comp_lo[p % 2][64:66, :].rearrange("k (h s) -> k h s", h=H), srcl)
                    for g in range(4):
                        gg = p * 4 + g
                        if gg >= 2:
                            p2, g2 = divmod(gg - 2, 4)
                            s_mm.wait(e, pb(p2) + 237 + 8 * g2)
                        srcw = d_woutT.rearrange("(fc p2) d -> p2 fc d", p2=128)[:, 4 * g:4 * g + 4, :]
                        stld(e, 7, gg, wog[gg % 2][:].rearrange("p2 (fc d) -> p2 fc d", fc=4), srcw)

            @blk.vector
            def _(e):
                kb.bar(e, b)
                e.tensor_copy(comp_hi[0][0:64, :], ddel_bf[:])
                e.tensor_copy(comp_lo[0][0:64, :], ddel_r[:])
                e.tensor_copy(comp_hi[1][0:64, :], ddel_bf[:])
                s_ev.inc(e.tensor_copy(comp_lo[1][0:64, :], ddel_r[:]))
                for p in range(NPAIR):
                    s_mm.wait(e, pb(p) + 3)
                    e.scalar_tensor_tensor(gd[:, 0:64], ps_big[:, 0:64], 1.0, ltm2[:],
                                           ALU.mult, ALU.mult)
                    s_ev.inc(e.tensor_copy(gd[:, 64:96], ps_big[:, 64:96]))
                    stw(e, 0, p, b)
                    bview = xct[p % 2][:, DIN:DIN + NS].unsqueeze(1).broadcast_to([128, H, NS])
                    dview = ddt_tm[:, p * H:(p + 1) * H].unsqueeze(2).broadcast_to([128, H, NS])
                    s_ev.inc(e.tensor_tensor(Bp[p % 2][:].rearrange("t (h n) -> t h n", h=H),
                                             bview, dview, ALU.mult))
                    s_mm.wait(e, pb(p) + 11)
                    s_ev.inc(e.tensor_scalar(clampR[:], ps_big[:], 30.0, None,
                                             ALU.min))
                    s_ac.wait(e, acb(p) + 1)
                    gtv = gd[:, 0:64].unsqueeze(1).broadcast_to([128, H, CH])
                    s_ev.inc(e.tensor_tensor(rt0[:].rearrange("s (h l) -> s h l", h=H),
                                             expsR[:].rearrange("s (h l) -> s h l", h=H),
                                             gtv, ALU.mult))
                    s_ev.inc(e.tensor_tensor(rtF[:].rearrange("s (h l) -> s h l", h=H),
                                             rt0[:].rearrange("s (h l) -> s h l", h=H),
                                             Ddiag[:].rearrange("s (h l) -> s h l", h=H),
                                             ALU.add))
                    s_ac.wait(e, acb(p) + 2)
                    cve = C_fm[0:64, HALO + 128 * p:HALO + 128 * p + CH].unsqueeze(1).broadcast_to([NS, H, CH])
                    cvo = C_fm[64:128, HALO + 128 * p + CH:HALO + 128 * p + 2 * CH].unsqueeze(1).broadcast_to([NS, H, CH])
                    s_ev.inc(e.tensor_tensor(vtF[0:64, :].rearrange("s (h l) -> s h l", h=H),
                                             expsV[0:64, :].rearrange("s (h l) -> s h l", h=H),
                                             cve, ALU.mult))
                    s_ev.inc(e.tensor_tensor(vtF[64:128, :].rearrange("s (h l) -> s h l", h=H),
                                             expsV[64:128, :].rearrange("s (h l) -> s h l", h=H),
                                             cvo, ALU.mult))
                    s_mm.wait(e, pb(p) + 83)
                    drv0 = gd[0:64, 64:96].unsqueeze(2).broadcast_to([64, H, 64])
                    drv1 = gd[64:128, 64:96].unsqueeze(2).broadcast_to([64, H, 64])
                    s3 = lambda ap: ap.rearrange("n (h pp) -> n h pp", h=H)
                    s_ev.inc(e.tensor_tensor(s3(tmp_e[:]), s3(S_pp[0:64, :]), drv0, ALU.mult))
                    s_ev.inc(e.tensor_tensor(S_pp[64:128, :], tmp_e[:], ps_big[0:64, :], ALU.add))
                    s_ev.inc(e.tensor_tensor(s3(tmp_o[:]), s3(S_pp[64:128, :]), drv1, ALU.mult))
                    s_mm.wait(e, pb(p) + 147)
                    s_ev.inc(e.tensor_tensor(S_pp[0:64, :], tmp_o[:], ps_big[64:128, :], ALU.add))
                    s_mm.wait(e, pb(p) + 179)
                    stw(e, 2, p, b)
                    s_ev.inc(e.tensor_tensor(yg[:, 0:1024], ps_y0[:], sztl[p % 2][:, 0:1024], ALU.mult))
                    s_mm.wait(e, pb(p) + 211)
                    s_ev.inc(e.tensor_tensor(yg[:, 1024:2048], ps_y1[:], sztl[p % 2][:, 1024:2048], ALU.mult))
                    s_ac.wait(e, acb(p) + 4)
                    vE(e, e.tensor_tensor(msc[:, 2:3], msc[:, 0:1], msc[:, 1:2], ALU.add))
                    s_ev.inc(e.nop())
                    s_mm.wait(e, pb(p) + 219)
                    s_ev.inc(e.tensor_copy(yfm[:, 0:1024], ps_yt0[:]))
                    s_mm.wait(e, pb(p) + 227)
                    s_ev.inc(e.tensor_copy(yfm[:, 1024:2048], ps_yt1[:]))
                    s_mm.wait(e, pb(p) + 259)
                    s_ac.wait(e, acb(p) + 5)
                    stw(e, 4, p, b)
                    if p >= 2:
                        stw(e, 1, p - 2, b)
                    s_ev.inc(e.scalar_tensor_tensor(xout[p % 2][:, 0:512], ps_o0, msc[:, 4:5],
                                                    utl[p % 2][:, 0:512], ALU.mult, ALU.add))
                    s_ev.inc(e.scalar_tensor_tensor(xout[p % 2][:, 512:1024], ps_o1, msc[:, 4:5],
                                                    utl[p % 2][:, 512:1024], ALU.mult, ALU.add))

            @blk.tensor
            def _(e):
                kb.bar(e, b)
                s_ev.wait(e, b[s_ev] + 1)
                for p in range(NPAIR):
                    s_ev.wait(e, evb(p))
                    stw(e, 0, p, b)
                    stw(e, 5, p, b)
                    stw(e, 6, p, b)
                    ec = HALO + 128 * p
                    s_mm.inc(e.matmul(ps_gt[0:64, :], B_fm[:, ec:ec + CH], C_fm[0:64, ec:ec + CH],
                                      start=True, stop=True))
                    s_mm.inc(e.matmul(ps_gt[64:128, :], B_fm[:, ec + CH:ec + 2 * CH],
                                      C_fm[0:64, ec + CH:ec + 2 * CH], start=True, stop=True))
                    s_mm.inc(e.matmul(ps_dr[:, 0:H], hones_r[:], decrows[:, p * H:(p + 1) * H],
                                      start=True, stop=True))
                    s_ev.wait(e, evb(p) + 1)
                    for j in range(4):
                        s_mm.inc(e.matmul(ps_big[:, j * 512:(j + 1) * 512],
                                          lhsc_hi[:, p * 128:(p + 1) * 128],
                                          comp_hi[p % 2][:, j * 512:(j + 1) * 512],
                                          start=True, stop=False))
                        s_mm.inc(e.matmul(ps_big[:, j * 512:(j + 1) * 512],
                                          lhsc_lo[:, p * 128:(p + 1) * 128],
                                          comp_lo[p % 2][:, j * 512:(j + 1) * 512],
                                          start=False, stop=True))
                    s_ac.wait(e, acb(p) + 1)
                    for j in range(4):
                        s_mm.inc(e.matmul(ps_big[:, j * 512:(j + 1) * 512],
                                          lhscv_hi[:], comp_hi[p % 2][:, j * 512:(j + 1) * 512],
                                          start=True, stop=False))
                        s_mm.inc(e.matmul(ps_big[:, j * 512:(j + 1) * 512],
                                          lhscv_lo[:], comp_lo[p % 2][:, j * 512:(j + 1) * 512],
                                          start=False, stop=True))
                    s_ac.wait(e, acb(p) + 2)
                    s_ev.wait(e, evb(p) + 2)
                    for base in (0, 64):
                        for h in range(H):
                            s_mm.inc(e.matmul(ps_big[base:base + 64, h * 64:(h + 1) * 64],
                                              Bp[p % 2][base:base + 64, h * 64:(h + 1) * 64],
                                              xct[p % 2][base:base + 64, h * 64:(h + 1) * 64],
                                              start=True, stop=True))
                    s_ev.wait(e, evb(p) + 7)
                    for base, psn in ((0, None), (64, None)):
                        if base == 64:
                            s_ev.wait(e, evb(p) + 9)
                        for hg in range(2):
                            psy = ps_y0 if hg == 0 else ps_y1
                            for hh in range(16):
                                h = hg * 16 + hh
                                s_mm.inc(e.matmul(psy[base:base + 64, hh * 64:(hh + 1) * 64],
                                                  rtF[base:base + 64, h * 64:(h + 1) * 64],
                                                  xct[p % 2][base:base + 64, h * 64:(h + 1) * 64],
                                                  start=True, stop=False))
                                s_mm.inc(e.matmul(psy[base:base + 64, hh * 64:(hh + 1) * 64],
                                                  vtF[base:base + 64, h * 64:(h + 1) * 64],
                                                  S_pp[base:base + 64, h * 64:(h + 1) * 64],
                                                  start=False, stop=True))
                    s_ev.wait(e, evb(p) + 13)
                    for fc in range(8):
                        s_mm.inc(e.transpose(ps_yt0[:, fc * 128:(fc + 1) * 128],
                                             yg[:, fc * 128:(fc + 1) * 128], eyebf[:]))
                    for fc in range(8):
                        s_mm.inc(e.transpose(ps_yt1[:, fc * 128:(fc + 1) * 128],
                                             yg[:, (8 + fc) * 128:(9 + fc) * 128], eyebf[:]))
                    s_ev.wait(e, evb(p) + 16)
                    for fc in range(16):
                        gg = p * 4 + fc // 4
                        if fc % 4 == 0:
                            stw(e, 7, gg, b)
                        wslice = wog[gg % 2][:, (fc % 4) * D:(fc % 4) * D + D]
                        s_mm.inc(e.matmul(ps_o0, yfm[:, fc * 128:(fc + 1) * 128],
                                          wslice[:, 0:512],
                                          start=(fc == 0), stop=(fc == 15)))
                        s_mm.inc(e.matmul(ps_o1, yfm[:, fc * 128:(fc + 1) * 128],
                                          wslice[:, 512:1024],
                                          start=(fc == 0), stop=(fc == 15)))

            @blk.scalar
            def _(e):
                kb.bar(e, b)
                for p in range(NPAIR):
                    s_ev.wait(e, evb(p) + 3)
                    s_ac.inc(e.activation(expsR[:], clampR[:], AF.Exp))
                    s_mm.wait(e, pb(p) + 19)
                    s_ac.inc(e.activation(expsV[:], ps_big[:], AF.Exp))
                    s_ev.wait(e, evb(p) + 12)
                    s_ac.inc(e.activation(sqscr[:], yg[:, 0:1024], AF.Square, accum_out=msc[:, 0:1]))
                    s_ev.wait(e, evb(p) + 13)
                    s_ac.inc(e.activation(sqscr[:], yg[:, 1024:2048], AF.Square, accum_out=msc[:, 1:2]))
                    s_ev.wait(e, evb(p) + 14)
                    sE(e, e.activation(msc[:, 3:4], msc[:, 2:3], AF.Ln, bias=EPS, scale=1.0 / DIN))
                    s_ac.inc(e.activation(msc[:, 4:5], msc[:, 3:4], AF.Exp, scale=-0.5))
                    s_ev.wait(e, evb(p) + 18)
                    stld(e, 1, p, d_x[p * 128:(p + 1) * 128, :], xout[p % 2][:])
                    if "d_yg" in kb.taps:
                        s_st.dinc(e.dma_start(out=d_yg[p * 128:(p + 1) * 128, :], in_=yg[:]))
                    if "d_yfm" in kb.taps:
                        s_st.dinc(e.dma_start(out=d_yfm[p * 128:(p + 1) * 128, :], in_=yfm[:]))
                    if "d_ms" in kb.taps:
                        s_st.dinc(e.dma_start(out=d_ms[p * 128:(p + 1) * 128, :], in_=msc[:]))

        if _STOP == "p6":
            ctx_mix.close()
            finalize_dummy()
            for s in kb.sems:
                s.check()
            return nc

        ctx_mix.close()

        with ExitStack() as pctx:
            lsb = lambda n, s, d: kb.lsb(pctx, n, s, d)
            xtl = [lsb("xtl%d" % i, [128, D], F32) for i in range(2)]
            xtf = [lsb("xtf%d" % i, [128, D], F32) for i in range(2)]
            lnb = lsb("lnb", [128, 8], F32)
            lnscr = lsb("lnscr", [128, D], F32)
            xn2_tm = lsb("xn2_tm", [128, D], BF16)
            xn2_fm = lsb("xn2_fm", [128, 8 * 1024], BF16)
            pr0 = lsb("pr0", [128, 8], F32)
            prs = lsb("prs", [128, 64], F32)
            probs_tm = lsb("probs_tm", [128, 2 * 64], F32)
            gsl = [lsb("gsl%d" % i, [128, 8 * 512], BF16) for i in range(2)]
            usl = [lsb("usl%d" % i, [128, 8 * 512], BF16) for i in range(2)]
            dwn = lsb("dwn", [128, 16 * D], BF16)
            sg = [lsb("sg%d" % i, [128, 1024], BF16) for i in range(2)]
            hbuf = lsb("hbuf", [128, 16 * 1024], BF16)
            ybuf = [lsb("ybuf%d" % i, [128, 8 * 1024], BF16) for i in range(2)]
            xo = [lsb("xo%d" % i, [128, D], F32) for i in range(2)]
            donb = lsb("donb", [1, NE * D], BF16)
            psA = kb.psum(pctx, "psA7", [128, 1024])
            psB = kb.psum(pctx, "psB7", [128, 1024])
            psC = kb.psum(pctx, "psC7", [128, 1024])
            ps_sm7 = kb.psum(pctx, "ps_sm7", [128, 512])
            ps_lnf = kb.psum(pctx, "ps_ln7", [128, 512])
            ps_ln = ps_lnf[:].bitcast(BF16)
            slots = [psA, psB, psC]
            b = kb.snap()
            MM_TT = 64 + 64 + 8 * 784
            EV_TT = 24 + 32 + 8 * 24 + 8
            AC_TT = 16 + 16 + 8 * 16
            NC7 = 16

            def m7(tt):
                return b[s_mm] + tt * MM_TT

            def e7(tt):
                return b[s_ev] + tt * EV_TT

            def a7(tt):
                return b[s_ac] + tt * AC_TT

            def ebm(tt, ei):
                return m7(tt) + 128 + 784 * ei

            def ebe(tt, ei):
                return e7(tt) + 56 + 24 * ei

            def eba(tt, ei):
                return a7(tt) + 32 + 16 * ei

            @blk.sync
            def _(e):
                kb.bar(e, b)
                for tt in range(2):
                    for ts in range(8):
                        k = tt * 8 + ts
                        if k >= 2:
                            t2, s2 = divmod(k - 2, 8)
                            s_ev.wait(e, e7(t2) + 3 * s2 + 2)
                            s_ac.wait(e, a7(t2) + 2 * s2 + 1)
                        stld(e, 0, k, xtl[k % 2][:], d_x[k * 128:(k + 1) * 128, :])
                    for ts in range(8):
                        k = tt * 8 + ts
                        if k >= 2:
                            t2, s2 = divmod(k - 2, 8)
                            s_ev.wait(e, e7(t2) + 248 + s2 + 1)
                        stld(e, 6, k, xtf[k % 2][:], d_x[k * 128:(k + 1) * 128, :])

            @blk.gpsimd
            def _(e):
                kb.bar(e, b)
                s_c2.dinc(e.dma_start(out=donb[:], in_=down_b[:]))
                gwv = gate_w.rearrange("ee (dc p) f -> ee p dc f", p=128)
                uwv = up_w.rearrange("ee (dc p) f -> ee p dc f", p=128)
                dwv = down_w.rearrange("ee (fc p) dd -> ee p fc dd", p=128)
                for tt in range(2):
                    for ei in range(NE):
                        di = tt * NE + ei
                        for fs in range(4):
                            gi = di * 4 + fs
                            if gi >= 2:
                                p2, f2 = divmod(gi - 2, 4)
                                t2, e2 = divmod(p2, NE)
                                s_mm.wait(e, ebm(t2, e2) + 128 * f2 + 128)
                            src = gwv[ei:ei + 1, :, :, fs * 512:(fs + 1) * 512].squeeze(0)
                            stgld(e, 2, gi, gsl[gi % 2][:].rearrange("p (dc f) -> p dc f", dc=8), src)
                            src = uwv[ei:ei + 1, :, :, fs * 512:(fs + 1) * 512].squeeze(0)
                            stgld(e, 4, gi, usl[gi % 2][:].rearrange("p (dc f) -> p dc f", dc=8), src)
                        if di >= 1:
                            t2, e2 = divmod(di - 1, NE)
                            s_mm.wait(e, ebm(t2, e2) + 784)
                        src = dwv[ei:ei + 1].squeeze(0)
                        stld(e, 5, di, dwn[:].rearrange("p (fc dd) -> p fc dd", fc=16), src)

            @blk.vector
            def _(e):
                kb.bar(e, b)
                for tt in range(2):
                    for ts in range(8):
                        k = tt * 8 + ts
                        stw(e, 0, k, b)
                        xb = xtl[k % 2]
                        vE(e, e.reduce_sum(lnb[:, 0:1], xb[:], AX.X))
                        s_ac.wait(e, a7(tt) + 2 * ts + 1)
                        vE(e, e.tensor_scalar_mul(lnb[:, 1:2], lnb[:, 0:1], 1.0 / D))
                        vE(e, e.tensor_scalar_mul(lnb[:, 5:6], lnb[:, 4:5], 1.0 / D))
                        vE(e, e.tensor_tensor(lnb[:, 2:3], lnb[:, 1:2], lnb[:, 1:2], ALU.mult))
                        s_ev.inc(e.tensor_tensor(lnb[:, 3:4], lnb[:, 5:6], lnb[:, 2:3], ALU.subtract))
                        s_ev.wait(e)
                        s_ac.wait(e, a7(tt) + 2 * ts + 2)
                        s_ev.inc(e.tensor_scalar(xn2_tm[:], xb[:], lnb[:, 1:2], lnb[:, 7:8],
                                                 ALU.subtract, ALU.mult))
                        s_mm.wait(e, m7(tt) + 8 * (ts + 1))
                        dst = xn2_fm[:].rearrange("p (dc t2) -> p dc t2", dc=8)
                        pv = ps_ln.rearrange("p (dc t2) -> p dc t2", dc=8)
                        s_ev.inc(e.tensor_copy(dst[:, :, ts * 128:(ts + 1) * 128], pv))
                    for ts in range(8):
                        pr = ps_sm7[:, (ts % 2) * 8:(ts % 2) * 8 + 8]
                        s_mm.wait(e, m7(tt) + 64 + 8 * ts + 8)
                        vE(e, e.reduce_max(prs[:, 0:1], pr, AX.X))
                        s_ev.inc(e.tensor_scalar_mul(prs[:, 1:2], prs[:, 0:1], -1.0))
                        s_ac.wait(e, a7(tt) + 16 + 2 * ts + 1)
                        vE(e, e.reduce_max(prs[:, 4:5], pr0[:], AX.X))
                        vE(e, e.tensor_scalar(prs[:, 16:24], pr0[:], prs[:, 4:5], None, ALU.is_lt))
                        vE(e, e.tensor_tensor(prs[:, 24:32], prs[:, 16:24], pr0[:], ALU.mult))
                        vE(e, e.reduce_max(prs[:, 5:6], prs[:, 24:32], AX.X))
                        vE(e, e.tensor_scalar(prs[:, 32:40], pr0[:], prs[:, 5:6], None, ALU.is_ge))
                        vE(e, e.tensor_tensor(prs[:, 40:48], prs[:, 32:40], pr0[:], ALU.mult))
                        vE(e, e.reduce_sum(prs[:, 6:7], prs[:, 40:48], AX.X))
                        vE(e, e.tensor_scalar_mul(prs[:, 56:57], prs[:, 2:3], 1e-9))
                        s_ev.inc(e.tensor_tensor(prs[:, 7:8], prs[:, 6:7], prs[:, 56:57], ALU.add))
                        s_ac.wait(e, a7(tt) + 16 + 2 * ts + 2)
                        s_ev.inc(e.tensor_scalar_mul(prs[:, 48:56], prs[:, 40:48], prs[:, 3:4]))
                        s_ev.wait(e)
                        s_ev.inc(e.tensor_copy(probs_tm[:, tt * 64 + ts * 8:tt * 64 + ts * 8 + 8],
                                               prs[:, 48:56]))
                    for ei in range(NE):
                        for kk in range(16):
                            s_mm.wait(e, ebm(tt, ei) + 32 * kk + 32)
                            s_ac.wait(e, eba(tt, ei) + kk + 1)
                            ups = slots[(2 * kk + 1) % 3]
                            s_ev.inc(e.scalar_tensor_tensor(
                                hbuf[:, kk * 1024:(kk + 1) * 1024], ups[:],
                                ub_cols[:, ei * 16 + kk:ei * 16 + kk + 1],
                                sg[kk % 2][:], ALU.add, ALU.mult))
                        for ts in range(8):
                            s_mm.wait(e, ebm(tt, ei) + 512 + 34 * (ts + 1))
                            if ei == 0 and ts == 0:
                                s_ev.wait(e, e7(tt) + 56)
                            ysl = slice(ts * 1024, (ts + 1) * 1024)
                            pcol = probs_tm[:, tt * 64 + ts * 8 + ei:tt * 64 + ts * 8 + ei + 1]
                            yps = slots[ts % 2]
                            if ei == 0:
                                s_ev.inc(e.tensor_scalar_mul(ybuf[1][:, ysl], yps[:], pcol))
                            else:
                                s_ev.inc(e.scalar_tensor_tensor(ybuf[(ei + 1) % 2][:, ysl], yps[:],
                                                                pcol, ybuf[ei % 2][:, ysl],
                                                                ALU.mult, ALU.add))
                    for ts in range(8):
                        k = tt * 8 + ts
                        stw(e, 6, k, b)
                        if k >= 2:
                            stw(e, 1, k - 2, b)
                        s_ev.inc(e.tensor_tensor(xo[k % 2][:], xtf[k % 2][:],
                                                 ybuf[0][:, ts * 1024:(ts + 1) * 1024], ALU.add))

            @blk.scalar
            def _(e):
                kb.bar(e, b)
                s_c2.wait(e, b[s_c2] + NC7)
                for tt in range(2):
                    for ts in range(8):
                        k = tt * 8 + ts
                        stw(e, 0, k, b)
                        xb = xtl[k % 2]
                        s_ac.inc(e.activation(lnscr[:], xb[:], AF.Square, accum_out=lnb[:, 4:5]))
                        s_ev.wait(e, e7(tt) + 3 * ts + 1)
                        sE(e, e.activation(lnb[:, 6:7], lnb[:, 3:4], AF.Ln, bias=EPS))
                        s_ac.inc(e.activation(lnb[:, 7:8], lnb[:, 6:7], AF.Exp, scale=-0.5))
                    for ts in range(8):
                        pr = ps_sm7[:, (ts % 2) * 8:(ts % 2) * 8 + 8]
                        s_ev.wait(e, e7(tt) + 24 + 4 * ts + 1)
                        s_ac.inc(e.activation(pr0[:], pr, AF.Exp, bias=prs[:, 1:2],
                                              accum_out=prs[:, 2:3]))
                        s_ev.wait(e, e7(tt) + 24 + 4 * ts + 2)
                        sE(e, e.activation(prs[:, 57:58], prs[:, 7:8], AF.Ln))
                        s_ac.inc(e.activation(prs[:, 3:4], prs[:, 57:58], AF.Exp, scale=-1.0))
                    for ei in range(NE):
                        for kk in range(16):
                            s_mm.wait(e, ebm(tt, ei) + 32 * kk + 16)
                            if kk < 2:
                                if ei > 0:
                                    s_ev.wait(e, ebe(tt, ei - 1) + 15 + kk)
                                elif tt > 0:
                                    s_ev.wait(e, ebe(tt - 1, NE - 1) + 15 + kk)
                            else:
                                s_ev.wait(e, ebe(tt, ei) + kk - 1)
                            gps = slots[(2 * kk) % 3]
                            s_ac.inc(e.activation(sg[kk % 2][:], gps[:], AF.Silu,
                                                  bias=gb_cols[:, ei * 16 + kk:ei * 16 + kk + 1]))
                    for ts in range(8):
                        k = tt * 8 + ts
                        s_ev.wait(e, e7(tt) + 248 + ts + 1)
                        stld(e, 1, k, out[k * 128:(k + 1) * 128, :], xo[k % 2][:])

            @blk.tensor
            def _(e):
                kb.bar(e, b)
                s_c2.wait(e, b[s_c2] + NC7)
                for tt in range(2):
                    for ts in range(8):
                        s_ev.wait(e, e7(tt) + 3 * ts + 2)
                        if ts >= 1:
                            s_ev.wait(e, e7(tt) + 3 * ts)
                        for dc in range(8):
                            s_mm.inc(e.transpose(ps_ln[:, dc * 128:(dc + 1) * 128],
                                                 xn2_tm[:, dc * 128:(dc + 1) * 128], eyebf[:]))
                    for ts in range(8):
                        s_ev.wait(e, e7(tt) + 3 * ts + 3)
                        if ts >= 2:
                            s_ac.wait(e, a7(tt) + 16 + 2 * ts - 3)
                        pr = ps_sm7[:, (ts % 2) * 8:(ts % 2) * 8 + 8]
                        for dc in range(8):
                            s_mm.inc(e.matmul(pr,
                                              xn2_fm[:, dc * 1024 + ts * 128:dc * 1024 + (ts + 1) * 128],
                                              wrT[:, dc * NE:(dc + 1) * NE],
                                              start=(dc == 0), stop=(dc == 7)))
                    for ei in range(NE):
                        if ei > 0:
                            s_ev.wait(e, ebe(tt, ei - 1) + 24)
                        elif tt > 0:
                            s_ev.wait(e, ebe(tt - 1, NE - 1) + 24)
                        for fs in range(4):
                            gi = (tt * NE + ei) * 4 + fs
                            stgw(e, 2, gi, b)
                            stgw(e, 4, gi, b)
                            for fb in range(4):
                                kk = fs * 4 + fb
                                if kk >= 2:
                                    s_ev.wait(e, ebe(tt, ei) + kk - 1)
                                gps = slots[(2 * kk) % 3]
                                for dc in range(8):
                                    for nn in range(2):
                                        s_mm.inc(e.matmul(
                                            gps[:, nn * 512:(nn + 1) * 512],
                                            gsl[gi % 2][:, dc * 512 + fb * 128:dc * 512 + (fb + 1) * 128],
                                            xn2_fm[:, dc * 1024 + nn * 512:dc * 1024 + (nn + 1) * 512],
                                            start=(dc == 0), stop=(dc == 7)))
                                if kk >= 1:
                                    s_ac.wait(e, eba(tt, ei) + kk)
                                ups = slots[(2 * kk + 1) % 3]
                                for dc in range(8):
                                    for nn in range(2):
                                        s_mm.inc(e.matmul(
                                            ups[:, nn * 512:(nn + 1) * 512],
                                            usl[gi % 2][:, dc * 512 + fb * 128:dc * 512 + (fb + 1) * 128],
                                            xn2_fm[:, dc * 1024 + nn * 512:dc * 1024 + (nn + 1) * 512],
                                            start=(dc == 0), stop=(dc == 7)))
                        di = tt * NE + ei
                        stw(e, 5, di, b)
                        s_ev.wait(e, ebe(tt, ei) + 16)
                        s_ac.wait(e, eba(tt, ei) + 16)
                        for ts in range(8):
                            if ts >= 2:
                                s_ev.wait(e, ebe(tt, ei) + 16 + ts - 1)
                            yps = slots[ts % 2]
                            s_mm.inc(e.matmul(yps[:, 0:512], ones_bf[0:1, :],
                                              donb[0:1, ei * D:ei * D + 512],
                                              start=True, stop=False))
                            s_mm.inc(e.matmul(yps[:, 512:1024], ones_bf[0:1, :],
                                              donb[0:1, ei * D + 512:(ei + 1) * D],
                                              start=True, stop=False))
                            for fc in range(16):
                                s_mm.inc(e.matmul(yps[:, 0:512],
                                                  hbuf[:, fc * 1024 + ts * 128:fc * 1024 + (ts + 1) * 128],
                                                  dwn[:, fc * D:fc * D + 512],
                                                  start=False, stop=(fc == 15)))
                                s_mm.inc(e.matmul(yps[:, 512:1024],
                                                  hbuf[:, fc * 1024 + ts * 128:fc * 1024 + (ts + 1) * 128],
                                                  dwn[:, fc * D + 512:(fc + 1) * D],
                                                  start=False, stop=(fc == 15)))

        sn = kb.snap()

        @blk.sync
        def _(e):
            kb.bar(e, sn)

        for s in kb.sems:
            s.check()
    return nc


def _consts(core):
    c = {}
    c["eye_f"] = np.eye(128, dtype=np.float32)
    c["eye_bf"] = np.eye(128, dtype=ml_dtypes.bfloat16)
    ltm = np.triu(np.ones((CH, CH), np.float32))
    c["ltm2"] = np.concatenate([ltm, ltm], axis=0)
    dd = np.zeros((64, H * CH), np.float32)
    for h2 in range(H):
        dd[h2, h2 * CH:(h2 + 1) * CH] = 1.0
        dd[32 + h2, h2 * CH:(h2 + 1) * CH] = 1.0
    c["deltadelta"] = dd
    ho = np.zeros((2, 128), np.float32)
    ho[0, 0:64] = 1.0
    ho[1, 64:128] = 1.0
    c["halfones"] = ho
    m9 = np.zeros((1, 9), np.float32)
    m9[0, 0] = 1.0
    m9[0, 1:1 + core] = 1.0
    c["m9"] = m9
    s8 = np.zeros((NE, NE * 128), np.float32)
    for ei in range(NE):
        s8[ei, ei * 128:(ei + 1) * 128] = 1.0
    c["sel8"] = s8
    c["ones_row"] = np.ones((1, 128), np.float32)
    return c


def make_inmaps(inputs):
    u = np.asarray(inputs["L0_attn_u"], np.float32).reshape(16384, 1024)
    shared = {
        "w_in": np.ascontiguousarray(np.asarray(inputs["L0_attn_in_proj_w"], np.float32)),
        "conv_w": np.ascontiguousarray(np.asarray(inputs["L0_attn_conv_w"], np.float32)),
        "conv_b": np.asarray(inputs["L0_attn_conv_b"], np.float32).reshape(1, CONVD),
        "init_nhp": np.ascontiguousarray(
            np.transpose(np.asarray(inputs["L0_attn_init_states"], np.float32), (2, 0, 1)).reshape(NS, H * PDIM)),
        "dt_bias": np.asarray(inputs["L0_attn_dt_bias"], np.float32).reshape(H, 1),
        "a_log": np.asarray(inputs["L0_attn_A_log"], np.float32).reshape(H, 1),
        "d_head": np.asarray(inputs["L0_attn_D"], np.float32).reshape(1, H),
        "norm_w": np.asarray(inputs["L0_attn_norm_w"], np.float32).reshape(1, DIN),
        "w_out": np.ascontiguousarray(np.asarray(inputs["L0_attn_out_proj_w"], np.float32)),
        "router_w": np.ascontiguousarray(np.asarray(inputs["L0_ffn_router_w"], np.float32)),
        "gate_w": np.ascontiguousarray(np.asarray(inputs["L0_ffn_gate_w"], np.float32)),
        "gate_b": np.asarray(inputs["L0_ffn_gate_b"], np.float32).reshape(1, NE * DFF),
        "up_w": np.ascontiguousarray(np.asarray(inputs["L0_ffn_up_w"], np.float32)),
        "up_b": np.asarray(inputs["L0_ffn_up_b"], np.float32).reshape(1, NE * DFF),
        "down_w": np.ascontiguousarray(np.asarray(inputs["L0_ffn_down_w"], np.float32)),
        "down_b": np.asarray(inputs["L0_ffn_down_b"], np.float32).reshape(1, NE * D),
    }
    in_maps = []
    for k in range(NCORES):
        m = dict(shared)
        if k == 0:
            up = np.concatenate([np.zeros((HALO, D), np.float32), u[0:T]], axis=0)
        else:
            up = u[k * T - HALO:k * T + T]
        m["u_pad"] = np.ascontiguousarray(up)
        m.update(_consts(k))
        in_maps.append(m)
    return in_maps


def kernel(**inputs):
    in_maps = make_inmaps(inputs)
    nc = build_program(taps=_TAPS)
    res = run_bass_kernel_spmd(nc, in_maps, core_ids=list(range(NCORES)))
    outs = [r["out"] for r in res.results]
    full = np.concatenate(outs, axis=0).reshape(1, 16384, 1024)
    if _TAPS:
        kernel.tap_results = res.results
    return full
